# version 08
# baseline (speedup 1.0000x reference)
import numpy as np
from contextlib import ExitStack

import concourse.bass as bass
import concourse.mybir as mybir
from concourse.bass_utils import run_bass_kernel_spmd

F32 = mybir.dt.float32
BF16 = mybir.dt.bfloat16
AF = mybir.ActivationFunctionType
ALU = mybir.AluOpType
AX = mybir.AxisListType

NCORES = 8
TOK = 1024
NT = TOK // 128
D = 4096
DC = D // 128
LN_EPS = 1e-5
RMS_EPS_FFN1 = 1.1920929e-07
QK_EPS = 1e-6
INV_SQRT_HD = float(1.0 / np.sqrt(128.0))
C_MARGIN = 30.0


class Sem:
    def __init__(self, nc, es, name):
        self.h = es.enter_context(nc.semaphore(name))
        self.n = 0

    def tick(self, instr, k=1):
        instr.then_inc(self.h, k)
        self.n += k
        return self.n


def waits(eng, deps):
    for sem, n in deps:
        if n > 0:
            eng.wait_ge(sem.h, n)


class DmaSlots:

    def __init__(self, p, name, nslots):
        self.sems = [p.sem(f"{name}{i}") for i in range(nslots)]
        self.nslots = nslots

    def issue(self, eng, slot, out, in_, pre_deps=()):
        s = self.sems[slot % self.nslots]
        if s.n > 0:
            eng.wait_ge(s.h, s.n)
        waits(eng, pre_deps)
        s.tick(eng.dma_start(out=out, in_=in_), 16)
        return (s, s.n)

    def issue_multi(self, eng, slot, pairs, pre_deps=()):
        s = self.sems[slot % self.nslots]
        if s.n > 0:
            eng.wait_ge(s.h, s.n)
        waits(eng, pre_deps)
        for out, in_ in pairs:
            s.tick(eng.dma_start(out=out, in_=in_), 16)
        return (s, s.n)


class Prog:
    def __init__(self):
        self.nc = bass.Bass("TRN2", target_bir_lowering=False)
        self.es = ExitStack()
        self.block = None
        self._sems = {}
        self._phases = []
        self._uid = 0

    def uid(self):
        self._uid += 1
        return self._uid

    def sem(self, name):
        if name not in self._sems:
            self._sems[name] = Sem(self.nc, self.es, name)
        return self._sems[name]

    def sbuf(self, name, shape, dtype):
        es = self._phases[-1] if self._phases else self.es
        if self._phases:
            name = f"{name}_u{self.uid()}"
        return es.enter_context(self.nc.sbuf_tensor(name, shape, dtype))

    def psum(self, name, shape, dtype=F32):
        es = self._phases[-1] if self._phases else self.es
        if self._phases:
            name = f"{name}_u{self.uid()}"
        return es.enter_context(self.nc.psum_tensor(name, shape, dtype))

    def phase_open(self):
        self._phases.append(ExitStack())

    def phase_close(self):
        self._phases.pop().close()

    def stage_pre(self):
        deps = []
        for s in self._sems.values():
            if s.n > 0:
                deps.append((s, s.n))
        return deps

    def start_block(self):
        self.block = self.es.enter_context(self.nc.Block())
        return self.block

    def finish(self):
        self.es.close()
        return self.nc


def emit_norm_tile(p, xa, out_bf, bufs, v_deps, s_deps, apply_deps,
                   rms_only=False, eps=LN_EPS, nfree=D):
    blk = p.block
    vs, ss = p.sem("vs"), p.sem("ss")
    stats, sq, eps_ap = bufs["stats"], bufs["sq"], bufs["eps_" + ("rms" if rms_only else "ln")]
    inv_n = 1.0 / nfree

    if not rms_only:
        @blk.vector
        def _(v):
            waits(v, v_deps)
            vs.tick(v.tensor_reduce(stats[:, 0:1], xa, axis=AX.X, op=ALU.add))
        red_t = vs.n

    @blk.scalar
    def _(s):
        waits(s, s_deps)
        ss.tick(s.activation(out=sq[:, 0:nfree], in_=xa, func=AF.Square,
                             accum_out=stats[:, 1:2]))
    sq_t = ss.n

    if rms_only:
        @blk.scalar
        def _(s):
            s.wait_ge(ss.h, sq_t)
            waits(s, v_deps)
            ss.tick(s.activation(out=stats[:, 3:4], in_=stats[:, 1:2],
                                 func=AF.Ln, bias=eps_ap[:], scale=inv_n))
            s.wait_ge(ss.h, ss.n)
            ss.tick(s.activation(out=stats[:, 3:4], in_=stats[:, 3:4],
                                 func=AF.Exp, scale=-0.5))
        r_t = ss.n

        @blk.vector
        def _(v):
            v.wait_ge(ss.h, r_t)
            waits(v, apply_deps)
            vs.tick(v.tensor_scalar(out=out_bf, in0=xa, scalar1=stats[:, 3:4],
                                    scalar2=None, op0=ALU.mult))
        return vs.n

    @blk.vector
    def _(v):
        v.wait_ge(vs.h, red_t)
        v.wait_ge(ss.h, sq_t)
        vs.tick(v.tensor_scalar(out=stats[:, 0:2], in0=stats[:, 0:2],
                                scalar1=inv_n, scalar2=None, op0=ALU.mult))
        v.wait_ge(vs.h, vs.n)
        vs.tick(v.scalar_tensor_tensor(out=stats[:, 2:3], in0=stats[:, 0:1],
                                       scalar=-1.0, in1=stats[:, 0:1],
                                       op0=ALU.mult, op1=ALU.mult))
        v.wait_ge(vs.h, vs.n)
        vs.tick(v.tensor_tensor(stats[:, 2:3], stats[:, 1:2], stats[:, 2:3],
                                op=ALU.add))
    var_t = vs.n

    @blk.scalar
    def _(s):
        s.wait_ge(vs.h, var_t)
        ss.tick(s.activation(out=stats[:, 3:4], in_=stats[:, 2:3],
                             func=AF.Ln, bias=eps_ap[:], scale=1.0))
        s.wait_ge(ss.h, ss.n)
        ss.tick(s.activation(out=stats[:, 3:4], in_=stats[:, 3:4],
                             func=AF.Exp, scale=-0.5))
    r_t = ss.n

    @blk.vector
    def _(v):
        v.wait_ge(ss.h, r_t)
        waits(v, apply_deps)
        vs.tick(v.tensor_scalar(out=out_bf, in0=xa, scalar1=stats[:, 0:1],
                                scalar2=stats[:, 3:4],
                                op0=ALU.subtract, op1=ALU.mult))
    return vs.n


def emit_xbar(p, out_ap, in_ap, pre_deps):
    blk = p.block
    tr = p.sem("tr")
    prev = tr.n

    @blk.sync
    def _(sync):
        waits(sync, pre_deps)
        if prev > 0:
            sync.wait_ge(tr.h, prev)
        tr.tick(sync.dma_start_transpose(out_ap, in_ap), 16)
    return tr.n



def build_ffn0(ctx=None, ntile=NT):
    if ctx is None:
        p = Prog()
        nc = p.nc
        x_h = nc.dram_tensor("x", [TOK, D], F32, kind="ExternalInput").ap()
        atwr_h = nc.dram_tensor("atwr", [D, 136], F32, kind="ExternalInput").ap()
        brow_h = nc.dram_tensor("brow", [1, 136], F32, kind="ExternalInput").ap()
        bflat_h = nc.dram_tensor("bflat", [128, D], F32, kind="ExternalInput").ap()
        y_h = nc.dram_tensor("y", [TOK, D], F32, kind="ExternalOutput").ap()
        p.start_block()
        emit_act_table_pin(p)
        p.phase_open()
    else:
        p = ctx["p"]
        x_h, y_h = ctx["x"], ctx["y"]
        atwr_h, brow_h, bflat_h = ctx["atwr"], ctx["brow"], ctx["bflat"]
        ntile = ctx.get("ntile", ntile)

    xr = [p.sbuf(f"xr{i}", [128, D], F32) for i in range(3)]
    lnx = [p.sbuf(f"lnx{i}", [128, D], BF16) for i in range(2)]
    lnxT = [p.sbuf(f"lnxT{i}", [128, DC, 128], BF16) for i in range(2)]
    sq = p.sbuf("sq", [128, D], BF16)
    stats = p.sbuf("stats", [128, 4], F32)
    eps_ln = p.sbuf("eps_ln", [128, 1], F32)
    atwr_sb = p.sbuf("atwr_sb", [128, DC * 136], BF16)
    brow_sb = p.sbuf("brow_sb", [1, 136], BF16)
    bflat_sb = p.sbuf("bflat_sb", [128, D], BF16)
    ones_bf = p.sbuf("ones_bf", [1, 128], BF16)
    er = p.sbuf("er", [128, 8], F32)
    shat = p.sbuf("shat", [128, 8], F32)
    smax = p.sbuf("smax", [128, 1], F32)
    ssum = p.sbuf("ssum", [128, 1], F32)
    rr = p.sbuf("rr", [128, 1], F32)
    slow = [p.sbuf(f"slow{i}", [128, 128], BF16) for i in range(2)]
    slowT = [p.sbuf(f"slowT{i}", [128, 128], BF16) for i in range(2)]

    ps_low = [p.psum(f"ps_low{i}", [128, 512]) for i in range(2)]
    ps_up = [p.psum(f"ps_up{i}", [128, 512]) for i in range(2)]

    blk = p.block
    ldw = p.sem("ldw")
    tr = p.sem("tr")
    ts = p.sem("ts")
    vs, ss = p.sem("vs"), p.sem("ss")
    xsl = DmaSlots(p, "xld", 3)
    ysl = DmaSlots(p, "yst", 3)

    @blk.gpsimd
    def _(g):
        ldw.tick(g.dma_start(out=atwr_sb[:].rearrange("p (c n) -> p c n", c=DC),
                             in_=atwr_h.rearrange("(c p) n -> p c n", p=128)), 16)
        ldw.tick(g.dma_start(out=brow_sb[:], in_=brow_h), 16)
        ldw.tick(g.dma_start(out=bflat_sb[:], in_=bflat_h), 16)
    w_ld = ldw.n

    @blk.vector
    def _(v):
        v.memset(ones_bf[:], 1.0)
        vs.tick(v.memset(eps_ln[:], LN_EPS))
    init_t = vs.n

    bufs = dict(stats=stats, sq=sq, eps_ln=eps_ln)

    x_ld = [None] * ntile
    apply_t = [0] * ntile
    low_t = [0] * ntile
    exp_t = [0] * ntile
    slow_t = [0] * ntile
    trL = [0] * ntile
    trS = [0] * ntile
    up_t = [[0] * 8 for _ in range(ntile)]
    evac_t = [[0] * 8 for _ in range(ntile)]
    st_t = [None] * ntile

    @blk.gpsimd
    def _(g):
        for j in range(min(3, ntile)):
            x_ld[j] = xsl.issue(g, j, xr[j % 3][:],
                                x_h[j * 128:(j + 1) * 128, :])

    for j in range(ntile):
        xa = xr[j % 3][:]
        v_deps = [x_ld[j], (vs, init_t)]
        s_deps = [x_ld[j]]
        if j >= 1:
            v_deps.append((vs, apply_t[j - 1]))
            s_deps.append((vs, apply_t[j - 1]))
        apply_deps = []
        if j >= 2:
            apply_deps.append((tr, trL[j - 2]))
        apply_t[j] = emit_norm_tile(p, xa, lnx[j % 2][:], bufs,
                                    v_deps, s_deps, apply_deps)

        pre = [(vs, apply_t[j])]
        if j >= 2:
            pre.append((ts, low_t[j - 2]))
        trL[j] = emit_xbar(p, lnxT[j % 2][:], lnx[j % 2][:], pre)

        @blk.tensor
        def _(t, j=j):
            t.wait_ge(tr.h, trL[j])
            t.wait_ge(ldw.h, w_ld)
            if j >= 2:
                t.wait_ge(vs.h, slow_t[j - 2])
                t.wait_ge(ss.h, exp_t[j - 2])
            pl = ps_low[j % 2][:, 0:136]
            for c in range(DC):
                t.matmul(pl, lnxT[j % 2][:, c, :],
                         atwr_sb[:, c * 136:(c + 1) * 136],
                         start=(c == 0), stop=False)
            ts.tick(t.matmul(pl, ones_bf[:, 0:128], brow_sb[:],
                             start=False, stop=True))
        low_t[j] = ts.n

        @blk.vector
        def _(v, j=j):
            v.wait_ge(ts.h, low_t[j])
            if j >= 1:
                v.wait_ge(ss.h, exp_t[j - 1])
            vs.tick(v.tensor_reduce(smax[:], ps_low[j % 2][:, 128:136],
                                    axis=AX.X, op=ALU.max, negate=True))
        smax_t = vs.n

        @blk.scalar
        def _(s, j=j, smax_t=smax_t):
            s.wait_ge(vs.h, smax_t)
            if j >= 1:
                s.wait_ge(vs.h, slow_t[j - 1])
            ss.tick(s.activation(out=er[:], in_=ps_low[j % 2][:, 128:136],
                                 func=AF.Exp, bias=smax[:], accum_out=ssum[:]))
        exp_t[j] = ss.n

        @blk.vector
        def _(v, j=j):
            v.wait_ge(ss.h, exp_t[j])
            vs.tick(v.reciprocal(rr[:], ssum[:]))
            v.wait_ge(vs.h, vs.n)
            vs.tick(v.tensor_scalar(out=shat[:], in0=er[:], scalar1=rr[:],
                                    scalar2=8.0, op0=ALU.mult, op1=ALU.mult))
            v.wait_ge(vs.h, vs.n)
            if j >= 2:
                v.wait_ge(tr.h, trS[j - 2])
            vs.tick(v.tensor_tensor(
                slow[j % 2][:].rearrange("p (h r) -> p h r", h=8),
                ps_low[j % 2][:, 0:128].rearrange("p (h r) -> p h r", h=8),
                shat[:].unsqueeze(-1).broadcast_to([128, 8, 16]),
                op=ALU.mult))
        slow_t[j] = vs.n

        pre = [(vs, slow_t[j])]
        if j >= 2:
            pre.append((ts, up_t[j - 2][7]))
        trS[j] = emit_xbar(p, slowT[j % 2][:], slow[j % 2][:], pre)

        for oc in range(8):
            @blk.tensor
            def _(t, j=j, oc=oc):
                t.wait_ge(tr.h, trS[j])
                if oc >= 2:
                    t.wait_ge(vs.h, evac_t[j][oc - 2])
                elif j >= 1:
                    t.wait_ge(vs.h, evac_t[j - 1][6 + oc])
                ts.tick(t.matmul(ps_up[oc % 2][:], slowT[j % 2][:],
                                 bflat_sb[:, oc * 512:(oc + 1) * 512],
                                 start=True, stop=True))
            up_t[j][oc] = ts.n

            @blk.vector
            def _(v, j=j, oc=oc):
                v.wait_ge(ts.h, up_t[j][oc])
                xa_s = xr[j % 3][:, oc * 512:(oc + 1) * 512]
                vs.tick(v.tensor_tensor(xa_s, xa_s, ps_up[oc % 2][:], op=ALU.add))
            evac_t[j][oc] = vs.n

        @blk.gpsimd
        def _(g, j=j):
            st_t[j] = ysl.issue(g, j, y_h[j * 128:(j + 1) * 128, :],
                                xr[j % 3][:], pre_deps=[(vs, evac_t[j][7])])

        if j + 3 < ntile:
            @blk.gpsimd
            def _(g, j=j):
                x_ld[j + 3] = xsl.issue(g, j + 3, xr[j % 3][:],
                                        x_h[(j + 3) * 128:(j + 4) * 128, :],
                                        pre_deps=[st_t[j]])

    if ctx is None:
        @blk.gpsimd
        def _(g):
            for s_ in ysl.sems:
                if s_.n:
                    g.wait_ge(s_.h, s_.n)
        p.phase_close()
        return p.finish()
    ctx["stores"] = [(s_, s_.n) for s_ in ysl.sems if s_.n]
    return



def build_ffn1(ctx=None):
    if ctx is None:
        p = Prog()
        nc = p.nc
        x_h = nc.dram_tensor("x", [TOK, D], F32, kind="ExternalInput").ap()
        wallT_h = nc.dram_tensor("wallT", [D, 24], F32, kind="ExternalInput").ap()
        bases_h = nc.dram_tensor("bases", [1, 24], F32, kind="ExternalInput").ap()
        wb_h = nc.dram_tensor("wb", [1024, 1024], F32, kind="ExternalInput").ap()
        gate_h = nc.dram_tensor("gate", [1, 1], F32, kind="ExternalInput").ap()
        y_h = nc.dram_tensor("y", [TOK, D], F32, kind="ExternalOutput").ap()
        p.start_block()
        emit_act_table_pin(p)
        p.phase_open()
    else:
        p = ctx["p"]
        x_h, y_h = ctx["x"], ctx["y"]
        wallT_h, bases_h = ctx["wallT"], ctx["bases"]
        wb_h, gate_h = ctx["wb"], ctx["gate"]

    xr = [p.sbuf(f"xr{i}", [128, D], F32) for i in range(2)]
    lnf = [p.sbuf(f"lnf{i}", [128, D], F32) for i in range(2)]
    lnx = [p.sbuf(f"lnx{i}", [128, D], BF16) for i in range(2)]
    lnxT = [p.sbuf(f"lnxT{i}", [128, DC, 128], BF16) for i in range(2)]
    sq = p.sbuf("sq", [128, D], BF16)
    stats = p.sbuf("stats", [128, 4], F32)
    eps_rms = p.sbuf("eps_rms", [128, 1], F32)
    eps_ln = p.sbuf("eps_ln", [128, 1], F32)
    wallT_sb = p.sbuf("wallT_sb", [128, DC * 24], BF16)
    bases_sb = p.sbuf("bases_sb", [1, 24], BF16)
    ones_bf = p.sbuf("ones_bf", [1, 128], BF16)
    ones_f32 = p.sbuf("ones_f32", [1, 128], F32)
    Z = p.sbuf("Z", [128, NT * 16], F32)
    Hpre = p.sbuf("Hpre", [128, NT * 4], F32)
    Hpost = p.sbuf("Hpost", [128, NT * 4], F32)
    uu = p.sbuf("uu", [128, NT * 4], F32)
    vv = p.sbuf("vv", [128, NT * 4], F32)
    M1 = p.sbuf("M1", [128, NT * 16], F32)
    M2 = p.sbuf("M2", [128, NT * 16], F32)
    mx = p.sbuf("mx", [128, NT * 4], F32)
    sgum = p.sbuf("sgum", [128, NT * 4], F32)
    smax1 = p.sbuf("smax1", [128, 1], F32)
    er4a = p.sbuf("er4a", [128, 4], F32)
    ssum1 = p.sbuf("ssum1", [128, 1], F32)
    rr1 = p.sbuf("rr1", [128, 1], F32)
    smax2 = p.sbuf("smax2", [128, 1], F32)
    er4b = p.sbuf("er4b", [128, 4], F32)
    ssum2 = p.sbuf("ssum2", [128, 1], F32)
    rr2 = p.sbuf("rr2", [128, 1], F32)
    xpre = p.sbuf("xpre", [128, 1024], F32)
    xpre_bf = p.sbuf("xpre_bf", [128, 1024], BF16)
    xpT = p.sbuf("xpT", [128, 8, NT, 128], BF16)
    wb_sb = p.sbuf("wb_sb", [128, 8 * 1024], BF16)
    y_all = p.sbuf("y_all", [128, NT * 1024], F32)
    gsc = p.sbuf("gsc", [1, 4], F32)
    grep = p.sbuf("grep", [128, 2], F32)
    acc = p.sbuf("acc", [128, 1024], F32)

    ps_lg = [p.psum(f"ps_lg{i}", [128, 512]) for i in range(2)]
    ps_y = [p.psum(f"ps_y{i}", [128, 512]) for i in range(2)]
    ps_g = p.psum("ps_g", [128, 512])

    blk = p.block
    ldw = p.sem("ldw")
    stg = p.sem("stg")
    tr = p.sem("tr")
    ts = p.sem("ts")
    vs, ss = p.sem("vs"), p.sem("ss")
    xsl = DmaSlots(p, "xld", 2)
    xsl3 = DmaSlots(p, "xld3", 2)
    ysl = DmaSlots(p, "yst", 2)

    @blk.gpsimd
    def _(g):
        ldw.tick(g.dma_start(out=wallT_sb[:].rearrange("p (c n) -> p c n", c=DC),
                             in_=wallT_h.rearrange("(c p) n -> p c n", p=128)), 16)
        ldw.tick(g.dma_start(out=bases_sb[:], in_=bases_h), 16)
        ldw.tick(g.dma_start(out=wb_sb[:].rearrange("p (c n) -> p c n", c=8),
                             in_=wb_h.rearrange("(c p) n -> p c n", p=128)), 16)
        ldw.tick(g.dma_start(out=gsc[:, 0:1], in_=gate_h), 16)
    w_ld = ldw.n

    @blk.vector
    def _(v):
        v.memset(ones_bf[:], 1.0)
        v.memset(ones_f32[:], 1.0)
        v.memset(eps_rms[:], RMS_EPS_FFN1)
        v.memset(eps_ln[:], LN_EPS)
        v.memset(uu[:], 0.0)
        vs.tick(v.memset(vv[:], 0.0))
    init_t = vs.n

    bufs = dict(stats=stats, sq=sq, eps_rms=eps_rms, eps_ln=eps_ln)

    @blk.scalar
    def _(s):
        s.wait_ge(ldw.h, w_ld)
        ss.tick(s.activation(out=gsc[:, 1:2], in_=gsc[:, 0:1], func=AF.Exp,
                             scale=-1.0))
    g_e = ss.n

    @blk.vector
    def _(v):
        v.wait_ge(ss.h, g_e)
        vs.tick(v.tensor_scalar(out=gsc[:, 1:2], in0=gsc[:, 1:2], scalar1=1.0,
                                scalar2=None, op0=ALU.add))
        v.wait_ge(vs.h, vs.n)
        vs.tick(v.reciprocal(gsc[:, 1:2], gsc[:, 1:2]))
        v.wait_ge(vs.h, vs.n)
        vs.tick(v.tensor_scalar(out=gsc[:, 2:3], in0=gsc[:, 1:2], scalar1=0.5,
                                scalar2=None, op0=ALU.mult))
        v.wait_ge(vs.h, vs.n)
        vs.tick(v.tensor_scalar(out=gsc[:, 3:4], in0=gsc[:, 1:2], scalar1=-0.5,
                                scalar2=0.5, op0=ALU.mult, op1=ALU.add))
    g_pair = vs.n

    @blk.tensor
    def _(t):
        t.wait_ge(vs.h, g_pair)
        ts.tick(t.matmul(ps_g[:, 0:2], ones_f32[:], gsc[:, 2:4],
                         start=True, stop=True))
    g_mm = ts.n

    @blk.vector
    def _(v):
        v.wait_ge(ts.h, g_mm)
        vs.tick(v.tensor_copy(grep[:], ps_g[:, 0:2]))
    g_rep = vs.n

    apply_t = [0] * NT
    lg_t = [0] * NT
    z_t = [0] * NT
    pre_t = [0] * NT
    pe_t = [0] * NT
    qe_t = [0] * NT
    post_t = [0] * NT
    xpre_t = [0] * NT
    trL = [0] * NT
    trX = [0] * NT
    y_t = [[0, 0] for _ in range(NT)]
    yev_t = [[0, 0] for _ in range(NT)]
    x_ld = [None] * NT
    xlast_t = [0] * NT

    @blk.gpsimd
    def _(g):
        x_ld[0] = xsl.issue(g, 0, xr[0][:], x_h[0:128, :])
        x_ld[1] = xsl.issue(g, 1, xr[1][:], x_h[128:256, :])

    lnf_t = [0] * NT
    for j in range(NT):
        v_deps = [x_ld[j], (vs, init_t)]
        s_deps = [x_ld[j], (ldw, w_ld)]
        if j >= 1:
            v_deps.append((vs, apply_t[j - 1]))
            s_deps.append((vs, apply_t[j - 1]))
        lnf_deps = []
        if j >= 2:
            lnf_deps.append((vs, xpre_t[j - 2]))
        lnf_t[j] = emit_norm_tile(p, xr[j % 2][:], lnf[j % 2][:], bufs,
                                  v_deps, s_deps, lnf_deps)
        v_deps2 = [(vs, lnf_t[j])]
        s_deps2 = [(vs, lnf_t[j])]
        apply_deps = []
        if j >= 2:
            apply_deps.append((tr, trL[j - 2]))
        apply_t[j] = emit_norm_tile(p, lnf[j % 2][:], lnx[j % 2][:], bufs,
                                    v_deps2, s_deps2, apply_deps, rms_only=True)

        pre = [(vs, apply_t[j])]
        if j >= 2:
            pre.append((ts, lg_t[j - 2]))
        trL[j] = emit_xbar(p, lnxT[j % 2][:], lnx[j % 2][:], pre)

        @blk.tensor
        def _(t, j=j):
            t.wait_ge(tr.h, trL[j])
            t.wait_ge(ldw.h, w_ld)
            if j >= 2:
                t.wait_ge(vs.h, post_t[j - 2])
                t.wait_ge(ss.h, qe_t[j - 2])
            pl = ps_lg[j % 2][:, 0:24]
            for c in range(DC):
                t.matmul(pl, lnxT[j % 2][:, c, :],
                         wallT_sb[:, c * 24:(c + 1) * 24],
                         start=(c == 0), stop=False)
            ts.tick(t.matmul(pl, ones_bf[:, 0:128], bases_sb[:],
                             start=False, stop=True))
        lg_t[j] = ts.n

        @blk.vector
        def _(v, j=j):
            v.wait_ge(ts.h, lg_t[j])
            vs.tick(v.tensor_scalar(out=Z[:, j * 16:(j + 1) * 16],
                                    in0=ps_lg[j % 2][:, 0:16],
                                    scalar1=10.0, scalar2=None, op0=ALU.mult))
        z_t[j] = vs.n

        @blk.vector
        def _(v, j=j):
            v.wait_ge(ts.h, lg_t[j])
            if j >= 1:
                v.wait_ge(ss.h, pe_t[j - 1])
            vs.tick(v.tensor_reduce(smax1[:], ps_lg[j % 2][:, 16:20],
                                    axis=AX.X, op=ALU.max, negate=True))
        pm = vs.n

        @blk.scalar
        def _(s, j=j, pm=pm):
            s.wait_ge(vs.h, pm)
            if j >= 1:
                s.wait_ge(vs.h, pre_t[j - 1])
            ss.tick(s.activation(out=er4a[:], in_=ps_lg[j % 2][:, 16:20],
                                 func=AF.Exp, bias=smax1[:], accum_out=ssum1[:]))
        pe_t[j] = ss.n

        @blk.vector
        def _(v, j=j, pe=pe_t[j]):
            v.wait_ge(ss.h, pe)
            vs.tick(v.reciprocal(rr1[:], ssum1[:]))
            v.wait_ge(vs.h, vs.n)
            vs.tick(v.tensor_scalar(out=Hpre[:, j * 4:(j + 1) * 4], in0=er4a[:],
                                    scalar1=rr1[:], scalar2=None, op0=ALU.mult))
        pre_t[j] = vs.n

        @blk.vector
        def _(v, j=j):
            v.wait_ge(ts.h, lg_t[j])
            if j >= 1:
                v.wait_ge(ss.h, qe_t[j - 1])
            vs.tick(v.tensor_reduce(smax2[:], ps_lg[j % 2][:, 20:24],
                                    axis=AX.X, op=ALU.max, negate=True))
        qm = vs.n

        @blk.scalar
        def _(s, j=j, qm=qm):
            s.wait_ge(vs.h, qm)
            if j >= 1:
                s.wait_ge(vs.h, post_t[j - 1])
            ss.tick(s.activation(out=er4b[:], in_=ps_lg[j % 2][:, 20:24],
                                 func=AF.Exp, bias=smax2[:], accum_out=ssum2[:]))
        qe_t[j] = ss.n

        @blk.vector
        def _(v, j=j):
            v.wait_ge(ss.h, qe_t[j])
            vs.tick(v.reciprocal(rr2[:], ssum2[:]))
            v.wait_ge(vs.h, vs.n)
            vs.tick(v.tensor_scalar(out=Hpost[:, j * 4:(j + 1) * 4], in0=er4b[:],
                                    scalar1=rr2[:], scalar2=0.5,
                                    op0=ALU.mult, op1=ALU.mult))
        post_t[j] = vs.n

        @blk.vector
        def _(v, j=j):
            v.wait_ge(vs.h, pre_t[j])
            if j >= 1:
                v.wait_ge(tr.h, trX[j - 1])
            vs.tick(v.tensor_scalar(out=xpre[:], in0=lnf[j % 2][:, 0:1024],
                                    scalar1=Hpre[:, j * 4:j * 4 + 1],
                                    scalar2=None, op0=ALU.mult))
            for n in range(1, 4):
                v.wait_ge(vs.h, vs.n)
                out_ap = xpre_bf[:] if n == 3 else xpre[:]
                vs.tick(v.scalar_tensor_tensor(
                    out=out_ap, in0=lnf[j % 2][:, n * 1024:(n + 1) * 1024],
                    scalar=Hpre[:, j * 4 + n:j * 4 + n + 1], in1=xpre[:],
                    op0=ALU.mult, op1=ALU.add))
        xpre_t[j] = vs.n
        xlast_t[j] = vs.n

        trX[j] = emit_xbar(p, xpT[:, :, j, :], xpre_bf[:], [(vs, xpre_t[j])])

        if j + 2 < NT:
            @blk.gpsimd
            def _(g, j=j):
                x_ld[j + 2] = xsl.issue(g, j + 2, xr[j % 2][:],
                                        x_h[(j + 2) * 128:(j + 3) * 128, :],
                                        pre_deps=[(vs, lnf_t[j])])

        for oc in range(2):
            @blk.tensor
            def _(t, j=j, oc=oc):
                t.wait_ge(tr.h, trX[j])
                t.wait_ge(ldw.h, w_ld)
                if j >= 1:
                    t.wait_ge(vs.h, yev_t[j - 1][oc])
                for c in range(8):
                    mm = t.matmul(ps_y[oc][:], xpT[:, c, j, :],
                                  wb_sb[:, c * 1024 + oc * 512:c * 1024 + (oc + 1) * 512],
                                  start=(c == 0), stop=(c == 7))
                ts.tick(mm)
            y_t[j][oc] = ts.n

            @blk.vector
            def _(v, j=j, oc=oc):
                v.wait_ge(ts.h, y_t[j][oc])
                vs.tick(v.tensor_copy(
                    y_all[:, j * 1024 + oc * 512:j * 1024 + (oc + 1) * 512],
                    ps_y[oc][:]))
            yev_t[j][oc] = vs.n

    Z4 = Z[:].rearrange("p (t i j) -> p t i j", t=NT, i=4)
    M14 = M1[:].rearrange("p (t i j) -> p t i j", t=NT, i=4)
    M24 = M2[:].rearrange("p (t i j) -> p t i j", t=NT, i=4)
    M2p = M2[:].rearrange("p (t i j) -> p t j i", t=NT, i=4)
    mx3 = mx[:].rearrange("p (t i) -> p t i", t=NT)
    sg3 = sgum[:].rearrange("p (t i) -> p t i", t=NT)
    uu3 = uu[:].rearrange("p (t i) -> p t i", t=NT)
    vv3 = vv[:].rearrange("p (t i) -> p t i", t=NT)
    BCAST = [128, NT, 4, 4]

    sink_start = [(vs, max(z_t)), (vs, init_t)]

    for it in range(50):
        @blk.vector
        def _(v, it=it):
            if it == 0:
                waits(v, sink_start)
            else:
                v.wait_ge(vs.h, vs.n)
            vs.tick(v.tensor_tensor(M14, Z4,
                                    vv3.unsqueeze(2).broadcast_to(BCAST),
                                    op=ALU.add))
            v.wait_ge(vs.h, vs.n)
            vs.tick(v.tensor_reduce(mx3, M14, axis=AX.X, op=ALU.max))
            v.wait_ge(vs.h, vs.n)
            vs.tick(v.tensor_tensor(M14, M14,
                                    mx3.unsqueeze(-1).broadcast_to(BCAST),
                                    op=ALU.subtract))
        m1_t = vs.n

        @blk.scalar
        def _(s, m1_t=m1_t):
            s.wait_ge(vs.h, m1_t)
            ss.tick(s.activation(out=M1[:], in_=M1[:], func=AF.Exp))
        e1_t = ss.n

        @blk.vector
        def _(v, e1_t=e1_t):
            v.wait_ge(ss.h, e1_t)
            vs.tick(v.tensor_reduce(sg3, M14, axis=AX.X, op=ALU.add))
        s1_t = vs.n

        @blk.scalar
        def _(s, s1_t=s1_t):
            s.wait_ge(vs.h, s1_t)
            ss.tick(s.activation(out=sgum[:], in_=sgum[:], func=AF.Ln))
        l1_t = ss.n

        @blk.vector
        def _(v, l1_t=l1_t):
            v.wait_ge(ss.h, l1_t)
            vs.tick(v.scalar_tensor_tensor(out=uu[:], in0=sgum[:], scalar=-1.0,
                                           in1=mx[:], op0=ALU.mult,
                                           op1=ALU.subtract))
            v.wait_ge(vs.h, vs.n)
            vs.tick(v.tensor_tensor(M24, Z4,
                                    uu3.unsqueeze(-1).broadcast_to(BCAST),
                                    op=ALU.add))
            v.wait_ge(vs.h, vs.n)
            vs.tick(v.tensor_reduce(mx3, M2p, axis=AX.X, op=ALU.max))
            v.wait_ge(vs.h, vs.n)
            vs.tick(v.tensor_tensor(M2p, M2p,
                                    mx3.unsqueeze(-1).broadcast_to(BCAST),
                                    op=ALU.subtract))
        m2_t = vs.n

        @blk.scalar
        def _(s, m2_t=m2_t):
            s.wait_ge(vs.h, m2_t)
            ss.tick(s.activation(out=M2[:], in_=M2[:], func=AF.Exp))
        e2_t = ss.n

        @blk.vector
        def _(v, e2_t=e2_t):
            v.wait_ge(ss.h, e2_t)
            vs.tick(v.tensor_reduce(sg3, M2p, axis=AX.X, op=ALU.add))
        s2_t = vs.n

        @blk.scalar
        def _(s, s2_t=s2_t):
            s.wait_ge(vs.h, s2_t)
            ss.tick(s.activation(out=sgum[:], in_=sgum[:], func=AF.Ln))
        l2_t = ss.n

        @blk.vector
        def _(v, l2_t=l2_t):
            v.wait_ge(ss.h, l2_t)
            vs.tick(v.scalar_tensor_tensor(out=vv[:], in0=sgum[:], scalar=-1.0,
                                           in1=mx[:], op0=ALU.mult,
                                           op1=ALU.subtract))
    @blk.vector
    def _(v):
        v.wait_ge(vs.h, vs.n)
        vs.tick(v.tensor_tensor(M14, Z4, uu3.unsqueeze(-1).broadcast_to(BCAST),
                                op=ALU.add))
        v.wait_ge(vs.h, vs.n)
        vs.tick(v.tensor_tensor(M14, M14, vv3.unsqueeze(2).broadcast_to(BCAST),
                                op=ALU.add))
    hsum_t = vs.n

    @blk.scalar
    def _(s):
        s.wait_ge(vs.h, hsum_t)
        ss.tick(s.activation(out=M1[:], in_=M1[:], func=AF.Exp))
    he_t = ss.n

    @blk.vector
    def _(v):
        v.wait_ge(ss.h, he_t)
        v.wait_ge(vs.h, g_rep)
        vs.tick(v.tensor_scalar(out=M1[:], in0=M1[:], scalar1=grep[:, 0:1],
                                scalar2=None, op0=ALU.mult))
        v.wait_ge(vs.h, vs.n)
        M116 = M1[:].rearrange("p (t d) -> p t d", t=NT)
        last = None
        for k in (0, 5, 10, 15):
            last = v.tensor_scalar(out=M116[:, :, k:k + 1],
                                   in0=M116[:, :, k:k + 1],
                                   scalar1=grep[:, 1:2], scalar2=None,
                                   op0=ALU.add)
        vs.tick(last)
    H_t = vs.n

    x3_ld = [None] * NT
    mix_t = [0] * NT

    @blk.gpsimd
    def _(g):
        x3_ld[0] = xsl3.issue(g, 0, xr[0][:], x_h[0:128, :],
                              pre_deps=[(vs, lnf_t[NT - 2])])
        x3_ld[1] = xsl3.issue(g, 1, xr[1][:], x_h[128:256, :],
                              pre_deps=[(vs, lnf_t[NT - 1])])

    st_t = [None] * NT
    ln3_t = [0] * NT
    for j in range(NT):
        v_deps = [x3_ld[j], (vs, H_t)]
        s_deps = [x3_ld[j], (vs, H_t)]
        lnf_deps = []
        if j >= 2:
            lnf_deps.append((vs, mix_t[j - 2]))
        ln3_t[j] = emit_norm_tile(p, xr[j % 2][:], lnf[j % 2][:], bufs,
                                  v_deps, s_deps, lnf_deps)

        @blk.vector
        def _(v, j=j):
            v.wait_ge(vs.h, ln3_t[j])
            v.wait_ge(vs.h, max(yev_t[j]))
            for i in range(4):
                v.wait_ge(vs.h, vs.n)
                vs.tick(v.tensor_scalar(out=acc[:],
                                        in0=lnf[j % 2][:, 0:1024],
                                        scalar1=M1[:, j * 16 + i * 4:j * 16 + i * 4 + 1],
                                        scalar2=None, op0=ALU.mult))
                for n in range(1, 4):
                    v.wait_ge(vs.h, vs.n)
                    vs.tick(v.scalar_tensor_tensor(
                        out=acc[:], in0=lnf[j % 2][:, n * 1024:(n + 1) * 1024],
                        scalar=M1[:, j * 16 + i * 4 + n:j * 16 + i * 4 + n + 1],
                        in1=acc[:], op0=ALU.mult, op1=ALU.add))
                v.wait_ge(vs.h, vs.n)
                vs.tick(v.scalar_tensor_tensor(
                    out=acc[:], in0=y_all[:, j * 1024:(j + 1) * 1024],
                    scalar=Hpost[:, j * 4 + i:j * 4 + i + 1],
                    in1=acc[:], op0=ALU.mult, op1=ALU.add))
                v.wait_ge(vs.h, vs.n)
                xa_i = xr[j % 2][:, i * 1024:(i + 1) * 1024]
                vs.tick(v.tensor_tensor(xa_i, xa_i, acc[:], op=ALU.add))
        mix_t[j] = vs.n

        @blk.gpsimd
        def _(g, j=j):
            st_t[j] = ysl.issue(g, j, y_h[j * 128:(j + 1) * 128, :],
                                xr[j % 2][:], pre_deps=[(vs, mix_t[j])])

        if j + 2 < NT:
            @blk.gpsimd
            def _(g, j=j):
                x3_ld[j + 2] = xsl3.issue(g, j + 2, xr[j % 2][:],
                                          x_h[(j + 2) * 128:(j + 3) * 128, :],
                                          pre_deps=[st_t[j]])

    if ctx is None:
        @blk.gpsimd
        def _(g):
            for s_ in ysl.sems:
                if s_.n:
                    g.wait_ge(s_.h, s_.n)
        p.phase_close()
        return p.finish()
    ctx["stores"] = [(s_, s_.n) for s_ in ysl.sems if s_.n]
    return



QKV_SS = 256
NSTRIP = 12288 // QKV_SS


def build_qkv0(ctx=None):
    if ctx is None:
        p = Prog()
        nc = p.nc
        x_h = nc.dram_tensor("x", [TOK, D], F32, kind="ExternalInput").ap()
        wqkv_h = nc.dram_tensor("wqkv", [128, 96, DC + 1, 128], F32,
                                kind="ExternalInput").ap()
        gq_h = nc.dram_tensor("gq", [128, 1], F32, kind="ExternalInput").ap()
        gk_h = nc.dram_tensor("gk", [128, 1], F32, kind="ExternalInput").ap()
        qT_o = nc.dram_tensor("qT", [32, 128, TOK], BF16,
                              kind="ExternalOutput").ap()
        kT_o = nc.dram_tensor("kT", [32, 128, TOK], BF16,
                              kind="ExternalOutput").ap()
        v_o = nc.dram_tensor("v", [TOK, D], BF16, kind="ExternalOutput").ap()
        ngroups = 1
        p.start_block()
        emit_act_table_pin(p)
        p.phase_open()
    else:
        p = ctx["p"]
        x_h, wqkv_h = ctx["x"], ctx["wqkv"]
        gq_h, gk_h = ctx["gq"], ctx["gk"]
        qT_o, kT_o, v_o = ctx["qT"], ctx["kT"], ctx["v"]
        ngroups = ctx.get("ngroups", 1)

    ntile = 8

    xr = [p.sbuf(f"xr{i}", [128, D], F32) for i in range(2)]
    lnx = [p.sbuf(f"lnx{i}", [128, D], BF16) for i in range(2)]
    lnxT = p.sbuf("lnxT_full", [128, DC, 1024], BF16)
    sq = p.sbuf("sq", [128, D], BF16)
    stats = p.sbuf("stats", [128, 4], F32)
    gktmp = p.sbuf("gktmp", [128, 1], F32)
    eps_ln = p.sbuf("eps_ln", [128, 1], F32)
    eps_qk = p.sbuf("eps_qk", [128, 1], F32)
    ones_bf = p.sbuf("ones_bf", [1, 512], BF16)
    ones_col = p.sbuf("ones_col", [128, 1], BF16)
    ones128 = p.sbuf("ones128", [128, 128], BF16)
    ones_f32 = p.sbuf("ones_f32", [1, 128], F32)
    gqks = p.sbuf("gqks", [128, 1], F32)
    one_col_f = p.sbuf("one_col_f", [128, 1], F32)
    wcol = [p.sbuf(f"wcol{i}", [128, DC + 1, 128], BF16) for i in range(3)]
    q2f = [p.sbuf(f"q2f{i}", [128, 1024], BF16) for i in range(2)]
    rstd = [p.sbuf(f"rstd{i}", [128, 1024], F32) for i in range(2)]
    ostage = [p.sbuf(f"ostage{i}", [128, 1024], BF16) for i in range(2)]
    vstage = [p.sbuf(f"vstage{i}", [128, ntile, 128], BF16) for i in range(2)]

    ps_fm = [p.psum(f"ps_fm{i}", [128, 1024]) for i in range(2)]
    ps_n = [p.psum(f"ps_n{i}", [128, 1024]) for i in range(2)]

    blk = p.block
    ldw = p.sem("ldw")
    tr = p.sem("tr")
    ts = p.sem("ts")
    vs, ss = p.sem("vs"), p.sem("ss")
    xsl = DmaSlots(p, "xld", 2)
    wsl = DmaSlots(p, "wld", 3)
    osl = DmaSlots(p, "osl", 4)

    @blk.gpsimd
    def _(g):
        ldw.tick(g.dma_start(out=gqks[:], in_=gq_h), 16)
        ldw.tick(g.dma_start(out=gktmp[:], in_=gk_h), 16)
    w_ld = ldw.n

    @blk.vector
    def _(v):
        v.memset(ones_bf[:], 1.0)
        v.memset(ones_col[:], 1.0)
        v.memset(ones128[:], 1.0)
        v.memset(ones_f32[:], 1.0)
        v.memset(one_col_f[:], 1.0)
        v.memset(eps_ln[:], LN_EPS)
        vs.tick(v.memset(eps_qk[:], QK_EPS))
        v.wait_ge(ldw.h, w_ld)
        vs.tick(v.tensor_tensor(gqks[:], gqks[:], gktmp[:], op=ALU.mult))
        v.wait_ge(vs.h, vs.n)
        vs.tick(v.tensor_scalar(out=gqks[:], in0=gqks[:], scalar1=INV_SQRT_HD,
                                scalar2=None, op0=ALU.mult))
    init_t = vs.n

    bufs = dict(stats=stats, sq=sq, eps_ln=eps_ln)

    apply_prev = [0]
    w_ld_t = [None, None, None]
    trL_hist = []
    gi = [0]
    mm_hist = {}
    ev_hist = {}
    nmm_hist = {}
    rstd_hist = {}
    rep_hist = {}
    repv_hist = {}
    sq_prev = [0]
    last_ost = [None, None]
    last_vst = [None, None]
    x_ld_prev = [None, None]
    last_group_mm = [0]

    for grp in range(ngroups):
        trL = [0] * ntile
        apply_t = [0] * ntile
        x_ld = [None] * ntile

        @blk.gpsimd
        def _(g, grp=grp):
            for jj in range(2):
                jg = grp * ntile + jj
                pre = []
                if x_ld_prev[jj] is not None:
                    pre = [(vs, apply_prev[0])]
                x_ld[jj] = xsl.issue(g, jj, xr[jj][:],
                                     x_h[jg * 128:(jg + 1) * 128, :],
                                     pre_deps=pre)
                x_ld_prev[jj] = x_ld[jj]

        for j in range(ntile):
            jg = grp * ntile + j
            v_deps = [x_ld[j], (vs, init_t)]
            s_deps = [x_ld[j]]
            if j >= 1 or grp >= 1:
                v_deps.append((vs, apply_prev[0]))
                s_deps.append((vs, apply_prev[0]))
            if grp >= 1 and j == 0:
                s_deps.append((vs, sq_prev[0]))
                s_deps.append((ss, rstd_hist.get(gi[0] - 1, 0)))
            apply_deps = []
            if len(trL_hist) >= 2:
                apply_deps.append((tr, trL_hist[-2]))
            apply_t[j] = emit_norm_tile(p, xr[j % 2][:], lnx[j % 2][:], bufs,
                                        v_deps, s_deps, apply_deps)
            apply_prev[0] = apply_t[j]
            pre = [(vs, apply_t[j])]
            if grp >= 1 and j == 0:
                pre.append((ts, last_group_mm[0]))
            trL[j] = emit_xbar(p, lnxT[:, :, j * 128:(j + 1) * 128],
                               lnx[j % 2][:], pre)
            trL_hist.append(trL[j])
            if j + 2 < ntile:
                @blk.gpsimd
                def _(g, j=j, grp=grp):
                    jg2 = grp * ntile + j + 2
                    x_ld[j + 2] = xsl.issue(g, j + 2, xr[j % 2][:],
                                            x_h[jg2 * 128:(jg2 + 1) * 128, :],
                                            pre_deps=[(vs, apply_t[j])])

        heavies = (list(range(0, 32)) + list(range(32, 64))) if grp == 0 \
            else list(range(32, 64))
        lights = list(range(64, 96))
        oc_list = []
        hi = vi = 0
        while hi < len(heavies) or vi < len(lights):
            if hi < len(heavies):
                oc_list.append(heavies[hi]); hi += 1
            if vi < len(lights):
                oc_list.append(lights[vi]); vi += 1
            if hi < len(heavies):
                oc_list.append(heavies[hi]); hi += 1
        for oc in oc_list:
            kind = "q" if oc < 32 else ("k" if oc < 64 else "v")
            g_i = gi[0]
            gi[0] += 1

            @blk.gpsimd
            def _(g, oc=oc, g_i=g_i):
                w_ld_t[g_i % 3] = wsl.issue(
                    g, g_i, wcol[g_i % 3][:], wqkv_h[:, oc],
                    pre_deps=[(ts, mm_hist.get(g_i - 3, 0))])

            @blk.tensor
            def _(t, oc=oc, g_i=g_i, grp=grp):
                waits(t, [w_ld_t[g_i % 3], (tr, trL[ntile - 1])])
                if g_i >= 2:
                    t.wait_ge(vs.h, ev_hist.get(g_i - 2, 0))
                pb = ps_fm[g_i % 2]
                for c in range(DC):
                    t.matmul(pb[:, 0:512], wcol[g_i % 3][:, c, :],
                             lnxT[:, c, 0:512], start=(c == 0), stop=False)
                    t.matmul(pb[:, 512:1024], wcol[g_i % 3][:, c, :],
                             lnxT[:, c, 512:1024], start=(c == 0), stop=False)
                t.matmul(pb[:, 0:512], wcol[g_i % 3][0:1, DC, :],
                         ones_bf[:, 0:512], start=False, stop=True)
                ts.tick(t.matmul(pb[:, 512:1024], wcol[g_i % 3][0:1, DC, :],
                                 ones_bf[:, 0:512], start=False, stop=True))
            mm_hist[g_i] = ts.n
            last_group_mm[0] = ts.n

            if kind in ("q", "k"):
                @blk.scalar
                def _(s, g_i=g_i):
                    s.wait_ge(ts.h, mm_hist[g_i])
                    if g_i >= 2:
                        s.wait_ge(ts.h, nmm_hist.get(g_i - 2, 0))
                    ss.tick(s.activation(out=q2f[g_i % 2][:],
                                         in_=ps_fm[g_i % 2][:],
                                         func=AF.Square))
                sq_hist = ss.n
                sq_prev[0] = ss.n

                @blk.tensor
                def _(t, g_i=g_i, sq_hist=sq_hist):
                    t.wait_ge(ss.h, sq_hist)
                    if g_i >= 2:
                        t.wait_ge(ss.h, rstd_hist.get(g_i - 2, 0))
                    pn = ps_n[g_i % 2]
                    t.matmul(pn[:, 0:512], ones128[:],
                             q2f[g_i % 2][:, 0:512], start=True, stop=True)
                    ts.tick(t.matmul(pn[:, 512:1024], ones128[:],
                                     q2f[g_i % 2][:, 512:1024],
                                     start=True, stop=True))
                nmm_hist[g_i] = ts.n

                @blk.scalar
                def _(s, g_i=g_i):
                    s.wait_ge(ts.h, nmm_hist[g_i])
                    if g_i >= 2:
                        s.wait_ge(vs.h, ev_hist.get(g_i - 2, 0))
                    ss.tick(s.activation(out=rstd[g_i % 2][:],
                                          in_=ps_n[g_i % 2][:],
                                          func=AF.Ln, bias=eps_qk[:],
                                          scale=1.0 / 128))
                    s.wait_ge(ss.h, ss.n)
                    ss.tick(s.activation(out=rstd[g_i % 2][:],
                                         in_=rstd[g_i % 2][:],
                                         func=AF.Exp, scale=-0.5))
                rstd_hist[g_i] = ss.n

                @blk.vector
                def _(v, g_i=g_i, kind=kind):
                    v.wait_ge(ss.h, rstd_hist[g_i])
                    if last_ost[g_i % 2] is not None:
                        waits(v, [last_ost[g_i % 2]])
                    vs.tick(v.scalar_tensor_tensor(
                        out=ostage[g_i % 2][:], in0=ps_fm[g_i % 2][:],
                        scalar=(gqks[:] if kind == "k" else one_col_f[:]),
                        in1=rstd[g_i % 2][:], op0=ALU.mult, op1=ALU.mult))
                ev_hist[g_i] = vs.n

                @blk.gpsimd
                def _(g, oc=oc, g_i=g_i, kind=kind, grp=grp):
                    dst = qT_o if kind == "q" else kT_o
                    hh = oc if kind == "q" else oc - 32
                    last_ost[g_i % 2] = osl.issue(
                        g, g_i, dst[hh][:, grp * 1024:(grp + 1) * 1024],
                        ostage[g_i % 2][:],
                        pre_deps=[(vs, ev_hist[g_i])])
            else:
                @blk.vector
                def _(v, g_i=g_i):
                    v.wait_ge(ts.h, mm_hist[g_i])
                    if last_ost[g_i % 2] is not None:
                        waits(v, [last_ost[g_i % 2]])
                    vs.tick(v.tensor_copy(ostage[g_i % 2][:],
                                          ps_fm[g_i % 2][:]))
                ev_hist[g_i] = vs.n

                pre = [(vs, ev_hist[g_i])]
                if last_vst[g_i % 2] is not None:
                    pre.append(last_vst[g_i % 2])
                tr_n = emit_xbar(p, vstage[g_i % 2][:], ostage[g_i % 2][:], pre)
                last_ost[g_i % 2] = (tr, tr_n)

                @blk.gpsimd
                def _(g, oc=oc, g_i=g_i, tr_n=tr_n, grp=grp):
                    g.wait_ge(tr.h, tr_n)
                    vd = (oc - 64) * 128
                    r0 = grp * 1024
                    last_vst[g_i % 2] = osl.issue(
                        g, g_i, v_o[r0:r0 + 1024, vd:vd + 128].rearrange(
                            "(c p) d -> p c d", p=128),
                        vstage[g_i % 2][:])

    if ctx is None:
        @blk.gpsimd
        def _(g):
            for s_ in osl.sems:
                if s_.n:
                    g.wait_ge(s_.h, s_.n)
        p.phase_close()
        return p.finish()
    ctx["stores"] = [(s_, s_.n) for s_ in osl.sems if s_.n]
    return



def emit_cbound_kmax(p, kh_ap, k2, cb, ones_col, ps_misc, Tlen, deps, war_deps):
    blk = p.block
    vs, ss, ts = p.sem("vs"), p.sem("ss"), p.sem("ts")
    nch = Tlen // 512

    @blk.scalar
    def _(s):
        waits(s, deps + war_deps)
        ss.tick(s.activation(out=k2[:, 0:Tlen], in_=kh_ap, func=AF.Square))
    sq_t = ss.n

    red_ts = []
    for ch in range(nch):
        @blk.tensor
        def _(t, ch=ch):
            t.wait_ge(ss.h, sq_t)
            if ch == 0:
                waits(t, war_deps)
            else:
                t.wait_ge(vs.h, red_ts[-1])
            ts.tick(t.matmul(ps_misc[0:1, 0:512], ones_col[:],
                             k2[:, ch * 512:(ch + 1) * 512],
                             start=True, stop=True))
        mmn = ts.n

        @blk.vector
        def _(v, ch=ch, mmn=mmn):
            v.wait_ge(ts.h, mmn)
            vs.tick(v.tensor_reduce(cb[0:1, ch:ch + 1], ps_misc[0:1, 0:512],
                                    axis=AX.X, op=ALU.max))
        red_ts.append(vs.n)

    @blk.vector
    def _(v):
        v.wait_ge(vs.h, red_ts[-1])
        vs.tick(v.tensor_reduce(cb[0:1, 13:14], cb[0:1, 0:nch],
                                axis=AX.X, op=ALU.max))
    return vs.n


def emit_cbound_q(p, qh_ap, q2, cb, ones_col, ps_misc, lnsc, ones_f32,
                  negC_rep, deps, war_deps, kmax_t):
    blk = p.block
    vs, ss, ts = p.sem("vs"), p.sem("ss"), p.sem("ts")

    @blk.scalar
    def _(s):
        waits(s, deps + war_deps)
        ss.tick(s.activation(out=q2[:], in_=qh_ap, func=AF.Square))
    sq_t = ss.n

    red_ts = []
    for ch in range(2):
        @blk.tensor
        def _(t, ch=ch):
            t.wait_ge(ss.h, sq_t)
            if ch == 0:
                waits(t, war_deps)
                t.wait_ge(vs.h, kmax_t)
            else:
                t.wait_ge(vs.h, red_ts[-1])
            ts.tick(t.matmul(ps_misc[0:1, 0:512], ones_col[:],
                             q2[:, ch * 512:(ch + 1) * 512],
                             start=True, stop=True))
        mmn = ts.n

        @blk.vector
        def _(v, ch=ch, mmn=mmn):
            v.wait_ge(ts.h, mmn)
            vs.tick(v.tensor_reduce(cb[0:1, 10 + ch:11 + ch],
                                    ps_misc[0:1, 0:512], axis=AX.X, op=ALU.max))
        red_ts.append(vs.n)

    @blk.vector
    def _(v):
        v.wait_ge(vs.h, red_ts[-1])
        vs.tick(v.tensor_reduce(cb[0:1, 12:13], cb[0:1, 10:12],
                                axis=AX.X, op=ALU.max))
        v.wait_ge(vs.h, vs.n)
        v.wait_ge(vs.h, kmax_t)
        vs.tick(v.tensor_tensor(cb[0:1, 14:15], cb[0:1, 12:13], cb[0:1, 13:14],
                                op=ALU.mult))
    m2_t = vs.n

    @blk.scalar
    def _(s):
        s.wait_ge(vs.h, m2_t)
        ss.tick(s.activation(out=cb[0:1, 15:16], in_=cb[0:1, 14:15], func=AF.Ln))
        s.wait_ge(ss.h, ss.n)
        ss.tick(s.activation(out=cb[0:1, 15:16], in_=cb[0:1, 15:16],
                             func=AF.Exp, scale=0.5, bias=lnsc[:]))
    c_t = ss.n

    @blk.vector
    def _(v):
        v.wait_ge(ss.h, c_t)
        vs.tick(v.tensor_scalar(out=cb[0:1, 15:16], in0=cb[0:1, 15:16],
                                scalar1=-C_MARGIN, scalar2=0.0,
                                op0=ALU.add, op1=ALU.max))
        v.wait_ge(vs.h, vs.n)
        vs.tick(v.tensor_scalar(out=cb[0:1, 15:16], in0=cb[0:1, 15:16],
                                scalar1=-1.0, scalar2=None, op0=ALU.mult))
    neg_t = vs.n

    @blk.tensor
    def _(t):
        t.wait_ge(vs.h, neg_t)
        ts.tick(t.matmul(ps_misc[:, 0:1], ones_f32[:], cb[0:1, 15:16],
                         start=True, stop=True))
    rep_mm = ts.n

    @blk.vector
    def _(v):
        v.wait_ge(ts.h, rep_mm)
        waits(v, war_deps)
        vs.tick(v.tensor_copy(negC_rep[:], ps_misc[:, 0:1]))
    return vs.n, rep_mm


def emit_head(p, b, qh_full, kh_sb, vh_sb, nTT, exp_scale, negC_t, ctx_dsts,
              q_deps, k_deps, v_deps, ctx_war=()):
    blk = p.block
    vs, ss, ts = p.sem("vs"), p.sem("ss"), p.sem("ts")
    ps_s, ps_ctx, ps_l = b["ps_s2"], b["ps_ctx2"], b["ps_l2"]
    pT = b["pT2"]
    st = b["state"]

    s_t = [0] * nTT
    exp_t = [0] * nTT
    av_t = [0] * nTT

    def emit_S(tt):
        @blk.tensor
        def _(t, tt=tt):
            if tt == 0:
                waits(t, q_deps + k_deps)
            if tt >= 2:
                t.wait_ge(ss.h, exp_t[tt - 2])
            elif st["last_exp"]:
                t.wait_ge(ss.h, st["last_exp"])
            kT_tile = kh_sb[:, tt * 128:(tt + 1) * 128]
            t.matmul(ps_s[tt % 2][:, 0:512], kT_tile, qh_full[:, 0:512],
                     start=True, stop=True)
            ts.tick(t.matmul(ps_s[tt % 2][:, 512:1024], kT_tile,
                             qh_full[:, 512:1024], start=True, stop=True))
        s_t[tt] = ts.n

        @blk.scalar
        def _(s, tt=tt):
            s.wait_ge(ts.h, s_t[tt])
            s.wait_ge(vs.h, negC_t)
            if tt >= 2:
                s.wait_ge(ts.h, av_t[tt - 2])
            elif st["last_av"]:
                s.wait_ge(ts.h, st["last_av"])
            ss.tick(s.activation(out=pT[tt % 2][:], in_=ps_s[tt % 2][:],
                                 func=AF.Exp, bias=b["negC_rep"][:],
                                 scale=exp_scale))
        exp_t[tt] = ss.n

    def emit_AV(tt):
        @blk.tensor
        def _(t, tt=tt):
            t.wait_ge(ss.h, exp_t[tt])
            if tt == 0:
                waits(t, v_deps)
                if st["last_ctxev"]:
                    t.wait_ge(vs.h, st["last_ctxev"])
                if st["last_lev"]:
                    t.wait_ge(vs.h, st["last_lev"])
            v_tile = vh_sb[:, tt * 128:(tt + 1) * 128]
            first, last = tt == 0, tt == nTT - 1
            t.matmul(ps_ctx[0][:, 0:512], v_tile, pT[tt % 2][:, 0:512],
                     start=first, stop=last)
            t.matmul(ps_ctx[1][:, 0:512], v_tile, pT[tt % 2][:, 512:1024],
                     start=first, stop=last)
            t.matmul(ps_l[0][0:1, 0:512], b["ones_col"][:], pT[tt % 2][:, 0:512],
                     start=first, stop=last)
            ts.tick(t.matmul(ps_l[1][0:1, 0:512], b["ones_col"][:],
                             pT[tt % 2][:, 512:1024], start=first, stop=last))
        av_t[tt] = ts.n

    for tt in range(nTT + 1):
        if tt < nTT:
            emit_S(tt)
        if tt >= 1:
            emit_AV(tt - 1)

    st["last_exp"] = exp_t[-1]
    st["last_av"] = av_t[-1]

    ctxev = []
    for qc in range(2):
        @blk.vector
        def _(v, qc=qc):
            v.wait_ge(ts.h, av_t[nTT - 1])
            vs.tick(v.tensor_scalar(out=b["lv"][0:1, :], in0=ps_l[qc][0:1, 0:512],
                                    scalar1=1e-30, scalar2=None, op0=ALU.max))
            v.wait_ge(vs.h, vs.n)
            vs.tick(v.reciprocal(b["linv"][0:1, :], b["lv"][0:1, :]))
        lev = vs.n
        st["last_lev"] = lev

        @blk.tensor
        def _(t, qc=qc, lev=lev):
            t.wait_ge(vs.h, lev)
            if st["last_rrep"]:
                t.wait_ge(vs.h, st["last_rrep"])
            ts.tick(t.matmul(ps_s[0][:, qc * 512:(qc + 1) * 512],
                             b["ones_f32"][:], b["linv"][0:1, :],
                             start=True, stop=True))
        rep_mm = ts.n

        @blk.vector
        def _(v, qc=qc, rep_mm=rep_mm):
            v.wait_ge(ts.h, rep_mm)
            vs.tick(v.tensor_copy(b["r_rep"][:], ps_s[0][:, qc * 512:(qc + 1) * 512]))
            v.wait_ge(vs.h, vs.n)
            waits(v, ctx_war)
            vs.tick(v.tensor_tensor(ctx_dsts[qc], ps_ctx[qc][:, 0:512],
                                    b["r_rep"][:], op=ALU.mult))
        st["last_rrep"] = vs.n - 1
        st["last_ctxev"] = vs.n
        ctxev.append(vs.n)
    return dict(ctxev=ctxev, last_ts=ts.n)


def emit_outproj(p, b, w_h, bias_sb, ctx_chunk_fn, x_h, y_h, has_bias,
                 phase_deps):
    blk = p.block
    nc = p.nc
    vs, ss, ts = p.sem("vs"), p.sem("ss"), p.sem("ts")
    stg = p.sem("stg")
    x_grp = b["x_grp"]
    wo_sb = b["wo_sb"]
    ps_o = b["ps_o"]
    ones_bf = b["ones_bf"]
    xgsl = DmaSlots(p, "xgsl", 2)
    wosl = DmaSlots(p, "wosl", 2)
    ysl = DmaSlots(p, "ysl", 4)

    strip_mm_done = {}
    ev_done = {}
    st_done = {}
    for g in range(2):
        x_ld = [None] * 4

        @blk.gpsimd
        def _(gq, g=g):
            for tl in range(4):
                j = g * 4 + tl
                pre = list(phase_deps) if (g == 0 and tl < 2) else []
                if g == 1:
                    pre.append(st_done[(0, tl)])
                x_ld[tl] = xgsl.issue(gq, tl % 2, x_grp[:, tl * D:(tl + 1) * D],
                                      x_h[j * 128:(j + 1) * 128, :],
                                      pre_deps=pre)
        get_lhsT, ctx_deps = ctx_chunk_fn(g, phase_deps)

        for oc in range(8):
            for cb4 in range(4):
                sid = (g * 8 + oc) * 4 + cb4

                @blk.gpsimd
                def _(gq, oc=oc, cb4=cb4, sid=sid):
                    pre = list(phase_deps) if sid < 2 else \
                        [(ts, strip_mm_done.get(sid - 2, 0))]
                    b["wo_ld"][sid] = wosl.issue(
                        gq, sid, wo_sb[sid % 2][:], w_h[:, cb4, oc],
                        pre_deps=pre)

                for tl in range(4):
                    @blk.tensor
                    def _(t, g=g, oc=oc, cb4=cb4, tl=tl, sid=sid):
                        waits(t, [b["wo_ld"][sid]] + ctx_deps)
                        if cb4 == 0:
                            prev = ev_done.get((g, tl, oc - 1)) if oc >= 1 else \
                                (ev_done.get((g - 1, tl, 7)) if g >= 1 else None)
                            if prev:
                                t.wait_ge(vs.h, prev)
                        mm = None
                        for c8 in range(8):
                            c = cb4 * 8 + c8
                            last = (cb4 == 3 and c8 == 7 and not has_bias)
                            mm = t.matmul(ps_o[tl][:, 0:512], get_lhsT(c, tl),
                                          wo_sb[sid % 2][:, c8, :],
                                          start=(cb4 == 0 and c8 == 0),
                                          stop=last)
                        if cb4 == 3 and has_bias:
                            mm = t.matmul(ps_o[tl][:, 0:512], ones_bf[:, 0:128],
                                          bias_sb[0:1, oc * 512:(oc + 1) * 512],
                                          start=False, stop=True)
                        ts.tick(mm)
                    strip_mm_done[sid] = ts.n
                    if cb4 == 3:
                        mmn = ts.n

                        @blk.vector
                        def _(v, g=g, oc=oc, tl=tl, mmn=mmn):
                            v.wait_ge(ts.h, mmn)
                            waits(v, [x_ld[tl]])
                            xa = x_grp[:, tl * D + oc * 512:tl * D + (oc + 1) * 512]
                            vs.tick(v.tensor_tensor(xa, xa, ps_o[tl][:, 0:512],
                                                    op=ALU.add))
                        ev_done[(g, tl, oc)] = vs.n

        for tl in range(4):
            @blk.gpsimd
            def _(gq, g=g, tl=tl):
                j = g * 4 + tl
                st_done[(g, tl)] = ysl.issue(
                    gq, j % 4, y_h[j * 128:(j + 1) * 128, :],
                    x_grp[:, tl * D:(tl + 1) * D],
                    pre_deps=[(vs, ev_done[(g, tl, 7)])])

    @blk.gpsimd
    def _(gq):
        for s_ in ysl.sems:
            if s_.n:
                gq.wait_ge(s_.h, s_.n)
    return [(s_, s_.n) for s_ in ysl.sems if s_.n]


def build_attn0(ctx=None):
    if ctx is None:
        p = Prog()
        nc = p.nc
        x_h = nc.dram_tensor("x", [TOK, D], F32, kind="ExternalInput").ap()
        qT_h = nc.dram_tensor("qT", [32, 128, TOK], BF16,
                              kind="ExternalInput").ap()
        kT_h = nc.dram_tensor("kT", [32, 128, 4096], BF16,
                              kind="ExternalInput").ap()
        vf_h = nc.dram_tensor("vf", [4096, D], BF16, kind="ExternalInput").ap()
        wo_h = nc.dram_tensor("wo", [128, 4, 8, 8, 512], F32,
                              kind="ExternalInput").ap()
        bo_h = nc.dram_tensor("bo", [1, D], F32, kind="ExternalInput").ap()
        y_h = nc.dram_tensor("y", [TOK, D], F32, kind="ExternalOutput").ap()
        kT_ag = None
        p.start_block()
        p.phase_open()
    else:
        p = ctx["p"]
        x_h, qT_h, vf_h = ctx["x"], ctx["qT"], ctx["vf"]
        kT_h = ctx.get("kT3")
        kT_ag = ctx.get("kT_ag")
        wo_h, bo_h, y_h = ctx["wo"], ctx["bo"], ctx["y"]

    ctxT = p.sbuf("ctxT", [128, 32 * TOK], BF16)
    ones_bf = p.sbuf("ones_bf", [1, 128], BF16)
    ones_col = p.sbuf("ones_col", [128, 1], BF16)
    ones_f32 = p.sbuf("ones_f32", [1, 128], F32)
    lnsc = p.sbuf("lnsc", [1, 1], F32)
    cb = p.sbuf("cb", [1, 16], F32)
    negC_rep = p.sbuf("negC_rep", [128, 1], F32)
    lv = p.sbuf("lv", [1, 512], F32)
    linv = p.sbuf("linv", [1, 512], F32)
    r_rep = p.sbuf("r_rep", [128, 512], F32)
    bo_sb = p.sbuf("bo_sb", [1, D], BF16)

    blk = p.block
    ldw = p.sem("ldw")
    vs, ss, ts = p.sem("vs"), p.sem("ss"), p.sem("ts")
    qsl = DmaSlots(p, "qsl", 2)
    ksl = DmaSlots(p, "ksl", 2)
    vsl = DmaSlots(p, "vsl", 2)

    @blk.gpsimd
    def _(g):
        ldw.tick(g.dma_start(out=bo_sb[:], in_=bo_h), 16)
    w_ld = ldw.n

    @blk.vector
    def _(v):
        v.memset(ones_bf[:], 1.0)
        v.memset(ones_col[:], 1.0)
        v.memset(ones_f32[:], 1.0)
        vs.tick(v.memset(lnsc[:], 0.0))
    init_t = vs.n

    p.phase_open()
    ps_s2 = [p.psum(f"ps_s{i}", [128, 1024]) for i in range(2)]
    ps_ctx2 = [p.psum(f"ps_ctx{q}", [128, 512]) for q in range(2)]
    ps_l2 = [p.psum(f"ps_l{q}", [128, 512]) for q in range(2)]
    qh_sb = [p.sbuf(f"qh{i}", [128, TOK], BF16) for i in range(2)]
    kh_sb = [p.sbuf(f"kh{i}", [128, 4096], BF16) for i in range(2)]
    vh_sb = [p.sbuf(f"vh{i}", [128, 32, 128], BF16) for i in range(2)]
    q2 = p.sbuf("q2", [128, TOK], BF16)
    k2 = p.sbuf("k2", [128, 4096], BF16)
    pT2 = [p.sbuf(f"pT{i}", [128, 1024], BF16) for i in range(2)]
    ps_misc = ps_s2[0]

    b = dict(ps_s2=ps_s2, ps_ctx2=ps_ctx2, ps_l2=ps_l2, pT2=pT2,
             ones_col=ones_col, ones_f32=ones_f32, negC_rep=negC_rep,
             lv=lv, linv=linv, r_rep=r_rep, ones_bf=ones_bf,
             state=dict(last_exp=0, last_av=0, last_ctxev=0, last_lev=0,
                        last_rrep=0))

    head_last_ts = [0] * 34
    head_last_ss = [0] * 34
    head_last_vs = [0] * 34
    ld_t = {}

    @blk.gpsimd
    def _(g):
        ld_t[("q", 0)] = qsl.issue(g, 0, qh_sb[0][:],
                                   qT_h[0])
        ld_t[("k", 0)] = ksl.issue(g, 0, kh_sb[0][:], kT_h[0])
        ld_t[("v", 0)] = vsl.issue(g, 0, vh_sb[0][:],
                                   vf_h[:, 0:128].rearrange(
                                       "(tt pp) d -> pp tt d", pp=128))

    for h in range(32):
        if h + 1 < 32:
            @blk.gpsimd
            def _(g, h=h):
                hp = h + 1
                pre = [(ts, head_last_ts[hp - 2])] if hp >= 2 else []
                ld_t[("q", hp)] = qsl.issue(g, hp, qh_sb[hp % 2][:], qT_h[hp],
                                            pre_deps=pre)
                ld_t[("k", hp)] = ksl.issue(g, hp, kh_sb[hp % 2][:], kT_h[hp],
                                            pre_deps=pre)
                ld_t[("v", hp)] = vsl.issue(
                    g, hp, vh_sb[hp % 2][:],
                    vf_h[:, hp * 128:(hp + 1) * 128].rearrange(
                        "(tt pp) d -> pp tt d", pp=128), pre_deps=pre)

        war = ([(ts, head_last_ts[h - 1]), (ss, head_last_ss[h - 1]),
                (vs, head_last_vs[h - 1])]
               if h >= 1 else [(vs, init_t)])
        kmax_t = emit_cbound_kmax(p, kh_sb[h % 2][:], k2, cb, ones_col,
                                  ps_misc, 4096, [ld_t[("k", h)]], war)
        negC_t, _mm = emit_cbound_q(p, qh_sb[h % 2][:], q2, cb, ones_col,
                                    ps_misc, lnsc, ones_f32, negC_rep,
                                    [ld_t[("q", h)]], war, kmax_t)
        emit_head(p, b, qh_sb[h % 2][:], kh_sb[h % 2][:],
                  vh_sb[h % 2][:].rearrange("p tt d -> p (tt d)"),
                  32, 1.0, negC_t,
                  [ctxT[:, h * TOK + qc * 512: h * TOK + (qc + 1) * 512]
                   for qc in range(2)],
                  [ld_t[("q", h)], (vs, negC_t)], [ld_t[("k", h)]],
                  [ld_t[("v", h)]])
        head_last_ts[h] = ts.n
        head_last_ss[h] = ss.n
        head_last_vs[h] = vs.n
    p.phase_close()

    p.phase_open()
    ps_o = [p.psum(f"ps_o{i}", [128, 512]) for i in range(4)]
    x_grp = p.sbuf("x_grp", [128, 4 * D], F32)
    wo_sb = [p.sbuf(f"wo_sb{i}", [128, 8, 512], BF16) for i in range(2)]
    b2 = dict(x_grp=x_grp, wo_sb=wo_sb, ps_o=ps_o, ones_bf=ones_bf,
              wo_ld=[None] * 64)
    phase_deps = [(ts, ts.n), (vs, vs.n), (ss, ss.n), (ldw, w_ld)]

    def ctx_chunk_fn(g, deps):
        def get_lhsT(c, tl):
            j = g * 4 + tl
            return ctxT[:, c * TOK + j * 128: c * TOK + (j + 1) * 128]
        return get_lhsT, [(vs, b["state"]["last_ctxev"])]

    ysl_st = emit_outproj(p, b2, wo_h, bo_sb, ctx_chunk_fn, x_h, y_h, True,
                          phase_deps)
    p.phase_close()
    if ctx is None:
        p.phase_close()
        return p.finish()
    ctx["stores"] = ysl_st
    return


WQ_SS = 256
NSTRIP_A = D // WQ_SS


def build_attn1(ctx=None):
    if ctx is None:
        p = Prog()
        nc = p.nc
        x_h = nc.dram_tensor("x", [TOK, D], F32, kind="ExternalInput").ap()
        wq_h = nc.dram_tensor("wq", [128, 32, DC, 128], F32,
                               kind="ExternalInput").ap()
        wo_h = nc.dram_tensor("wo", [128, 4, 8, 8, 512], F32,
                               kind="ExternalInput").ap()
        ktg_h = nc.dram_tensor("ktg", [8, 128, 8192], F32,
                               kind="ExternalInput").ap()
        vtg_h = nc.dram_tensor("vtg", [8, 8192, 128], F32,
                               kind="ExternalInput").ap()
        cos_h = nc.dram_tensor("cos", [64, TOK], F32, kind="ExternalInput").ap()
        sin_h = nc.dram_tensor("sin", [64, TOK], F32, kind="ExternalInput").ap()
        y_h = nc.dram_tensor("y", [TOK, D], F32, kind="ExternalOutput").ap()
        qT_int = nc.dram_tensor("qT_int", [32, 128, TOK], BF16).ap()
        ctx_int = nc.dram_tensor("ctx_int", [32, 128, TOK], BF16).ap()
        standalone = True
    else:
        p = ctx["p"]
        nc = p.nc
        x_h, y_h = ctx["x"], ctx["y"]
        wq_h, wo_h = ctx["wq"], ctx["wo"]
        ktg_h, vtg_h = ctx["ktg"], ctx["vtg"]
        cos_h, sin_h = ctx["cos"], ctx["sin"]
        qT_int, ctx_int = ctx["qT_int"], ctx["ctx_int"]
        standalone = False

    ones_bf = p.sbuf("ones_bf", [1, 128], BF16)
    ones_col = p.sbuf("ones_col", [128, 1], BF16)
    ones_f32 = p.sbuf("ones_f32", [1, 128], F32)
    lnsc = p.sbuf("lnsc", [1, 1], F32)
    cb = p.sbuf("cb", [1, 16], F32)
    negC_rep = p.sbuf("negC_rep", [128, 1], F32)
    lv = p.sbuf("lv", [1, 512], F32)
    linv = p.sbuf("linv", [1, 512], F32)
    r_rep = p.sbuf("r_rep", [128, 512], F32)
    stats = p.sbuf("stats", [128, 4], F32)
    eps_ln = p.sbuf("eps_ln", [128, 1], F32)
    cos_sb = p.sbuf("cos_sb", [64, TOK], F32)
    sin_sb = p.sbuf("sin_sb", [64, TOK], F32)

    blk = p.block if not standalone else p.start_block()
    if standalone:
        emit_act_table_pin(p)
        p.phase_open()
    ldw = p.sem("ldw")
    vs, ss, ts = p.sem("vs"), p.sem("ss"), p.sem("ts")
    tr = p.sem("tr")

    @blk.gpsimd
    def _(g):
        ldw.tick(g.dma_start(out=cos_sb[:], in_=cos_h), 16)
        ldw.tick(g.dma_start(out=sin_sb[:], in_=sin_h), 16)
    w_ld = ldw.n

    @blk.vector
    def _(v):
        v.memset(ones_bf[:], 1.0)
        v.memset(ones_col[:], 1.0)
        v.memset(ones_f32[:], 1.0)
        v.memset(eps_ln[:], LN_EPS)
        vs.tick(v.memset(lnsc[:], float(np.log(INV_SQRT_HD))))
    init_t = vs.n

    p.phase_open()
    xr = [p.sbuf(f"xr{i}", [128, D], F32) for i in range(2)]
    lnx = [p.sbuf(f"lnx{i}", [128, D], BF16) for i in range(2)]
    lnxT = p.sbuf("lnxT_full", [128, DC, TOK], BF16)
    sq = p.sbuf("sq", [128, D], BF16)
    wcol = [p.sbuf(f"wcol{i}", [128, DC, 128], BF16) for i in range(3)]
    tmp1 = p.sbuf("tmp1", [64, TOK], F32)
    tmp2 = p.sbuf("tmp2", [64, TOK], F32)
    qstage = [p.sbuf(f"qstage{i}", [128, TOK], BF16) for i in range(2)]
    ps_fm = [p.psum(f"ps_fm{i}", [128, 1024]) for i in range(2)]

    bufs = dict(stats=stats, sq=sq, eps_ln=eps_ln)
    xsl = DmaSlots(p, "xld", 2)
    wsl = DmaSlots(p, "wld", 3)
    osl = DmaSlots(p, "osl", 2)

    x_ld = [None] * NT
    apply_t = [0] * NT
    trL = [0] * NT

    @blk.gpsimd
    def _(g):
        x_ld[0] = xsl.issue(g, 0, xr[0][:], x_h[0:128, :])
        x_ld[1] = xsl.issue(g, 1, xr[1][:], x_h[128:256, :])

    for j in range(NT):
        v_deps = [x_ld[j], (vs, init_t)]
        s_deps = [x_ld[j]]
        if j >= 1:
            v_deps.append((vs, apply_t[j - 1]))
            s_deps.append((vs, apply_t[j - 1]))
        apply_deps = []
        if j >= 2:
            apply_deps.append((tr, trL[j - 2]))
        apply_t[j] = emit_norm_tile(p, xr[j % 2][:], lnx[j % 2][:], bufs,
                                    v_deps, s_deps, apply_deps)
        trL[j] = emit_xbar(p, lnxT[:, :, j * 128:(j + 1) * 128],
                           lnx[j % 2][:], [(vs, apply_t[j])])
        if j + 2 < NT:
            @blk.gpsimd
            def _(g, j=j):
                x_ld[j + 2] = xsl.issue(g, j + 2, xr[j % 2][:],
                                        x_h[(j + 2) * 128:(j + 3) * 128, :],
                                        pre_deps=[(vs, apply_t[j])])

    w_ld_t = [None] * 32
    mm_t = [0] * 32
    rope_t = [0] * 32
    qst = [None, None]

    for oc in range(32):
        @blk.gpsimd
        def _(g, oc=oc):
            w_ld_t[oc] = wsl.issue(
                g, oc, wcol[oc % 3][:], wq_h[:, oc],
                pre_deps=[(ts, mm_t[oc - 3] if oc >= 3 else 0)])

        @blk.tensor
        def _(t, oc=oc):
            waits(t, [w_ld_t[oc], (tr, trL[NT - 1]), (ldw, w_ld)])
            if oc >= 2:
                t.wait_ge(vs.h, rope_t[oc - 2])
            pb = ps_fm[oc % 2]
            mm = None
            for c in range(DC):
                t.matmul(pb[:, 0:512], wcol[oc % 3][:, c, :],
                         lnxT[:, c, 0:512], start=(c == 0), stop=(c == DC - 1))
                mm = t.matmul(pb[:, 512:1024], wcol[oc % 3][:, c, :],
                              lnxT[:, c, 512:1024], start=(c == 0),
                              stop=(c == DC - 1))
            ts.tick(mm)
        mm_t[oc] = ts.n

        @blk.vector
        def _(v, oc=oc):
            v.wait_ge(ts.h, mm_t[oc])
            if oc >= 1:
                v.wait_ge(vs.h, rope_t[oc - 1])
            if qst[oc % 2] is not None:
                waits(v, [qst[oc % 2]])
            pb = ps_fm[oc % 2]
            vs.tick(v.tensor_tensor(tmp1[:], pb[0:64, :], cos_sb[:], op=ALU.mult))
            vs.tick(v.tensor_tensor(tmp2[:], pb[64:128, :], sin_sb[:], op=ALU.mult))
            v.wait_ge(vs.h, vs.n)
            vs.tick(v.tensor_tensor(qstage[oc % 2][0:64, :], tmp1[:], tmp2[:],
                                    op=ALU.subtract))
            v.wait_ge(vs.h, vs.n)
            vs.tick(v.tensor_tensor(tmp1[:], pb[64:128, :], cos_sb[:], op=ALU.mult))
            vs.tick(v.tensor_tensor(tmp2[:], pb[0:64, :], sin_sb[:], op=ALU.mult))
            v.wait_ge(vs.h, vs.n)
            vs.tick(v.tensor_tensor(qstage[oc % 2][64:128, :], tmp1[:], tmp2[:],
                                    op=ALU.add))
        rope_t[oc] = vs.n

        @blk.gpsimd
        def _(g, oc=oc):
            qst[oc % 2] = osl.issue(g, oc, qT_int[oc], qstage[oc % 2][:],
                                    pre_deps=[(vs, rope_t[oc])])

    phaseA_end = [(ts, ts.n), (vs, vs.n), (ss, ss.n), (tr, tr.n)] + \
        [(s_, s_.n) for s_ in osl.sems if s_.n]
    p.phase_close()

    p.phase_open()
    ps_s2 = [p.psum(f"ps_s{i}", [128, 1024]) for i in range(2)]
    ps_ctx2 = [p.psum(f"ps_ctx{q}", [128, 512]) for q in range(2)]
    ps_l2 = [p.psum(f"ps_l{q}", [128, 512]) for q in range(2)]
    qh_sb = [p.sbuf(f"qh{i}", [128, TOK], BF16) for i in range(2)]
    kh_sb = [p.sbuf(f"kh{i}", [128, 8192], BF16) for i in range(2)]
    vh_sb = [p.sbuf(f"vh{i}", [128, 64, 128], BF16) for i in range(2)]
    q2 = p.sbuf("q2", [128, TOK], BF16)
    k2 = p.sbuf("k2", [128, 8192], BF16)
    pT2 = [p.sbuf(f"pT{i}", [128, 1024], BF16) for i in range(2)]
    ctx_ev = [p.sbuf(f"ctx_ev{i}", [128, 512], BF16) for i in range(2)]
    ps_misc = ps_s2[0]

    b = dict(ps_s2=ps_s2, ps_ctx2=ps_ctx2, ps_l2=ps_l2, pT2=pT2,
             ones_col=ones_col, ones_f32=ones_f32, negC_rep=negC_rep,
             lv=lv, linv=linv, r_rep=r_rep, ones_bf=ones_bf,
             state=dict(last_exp=0, last_av=0, last_ctxev=0, last_lev=0,
                        last_rrep=0))

    qsl = DmaSlots(p, "qsl", 2)
    ksl = DmaSlots(p, "ksl", 2)
    vsl = DmaSlots(p, "vsl", 2)
    csl = DmaSlots(p, "cst", 2)
    kv_last_ts = [0] * 10
    qh_last = {}
    ld_t = {}
    ctx_st = {}

    @blk.gpsimd
    def _(g):
        ld_t[("k", 0)] = ksl.issue(g, 0, kh_sb[0][:], ktg_h[0],
                                   pre_deps=phaseA_end)
        ld_t[("v", 0)] = vsl.issue(g, 0, vh_sb[0][:],
                                   vtg_h[0].rearrange("(tt pp) d -> pp tt d",
                                                      pp=128),
                                   pre_deps=phaseA_end)
        ld_t[("q", 0)] = qsl.issue(g, 0, qh_sb[0][:], qT_int[0],
                                   pre_deps=phaseA_end)

    hh_ss = [0] * 34
    hh_ts = [0] * 34
    hh_vs = [0] * 34
    for kv in range(8):
        if kv + 1 < 8:
            @blk.gpsimd
            def _(g, kv=kv):
                kp = kv + 1
                pre = [(ts, kv_last_ts[kp - 2])] if kp >= 2 else []
                ld_t[("k", kp)] = ksl.issue(g, kp, kh_sb[kp % 2][:], ktg_h[kp],
                                            pre_deps=pre)
                ld_t[("v", kp)] = vsl.issue(
                    g, kp, vh_sb[kp % 2][:],
                    vtg_h[kp].rearrange("(tt pp) d -> pp tt d", pp=128),
                    pre_deps=pre)

        war0 = ([(ts, hh_ts[kv * 4 - 1]), (ss, hh_ss[kv * 4 - 1]),
                 (vs, hh_vs[kv * 4 - 1])]
                if kv >= 1 else [(vs, init_t)])
        kmax_t = emit_cbound_kmax(p, kh_sb[kv % 2][:], k2, cb, ones_col,
                                  ps_misc, 8192, [ld_t[("k", kv)]], war0)
        for qi in range(4):
            h = kv * 4 + qi
            if h + 1 < 32:
                @blk.gpsimd
                def _(g, h=h):
                    hp = h + 1
                    pre = [(ts, hh_ts[hp - 2])] if hp >= 2 else []
                    ld_t[("q", hp)] = qsl.issue(g, hp, qh_sb[hp % 2][:],
                                                qT_int[hp], pre_deps=pre)

            war = ([(ts, hh_ts[h - 1]), (ss, hh_ss[h - 1]),
                    (vs, hh_vs[h - 1])]
                   if h >= 1 else [(vs, init_t)])
            negC_t, _mm = emit_cbound_q(p, qh_sb[h % 2][:], q2, cb, ones_col,
                                        ps_misc, lnsc, ones_f32, negC_rep,
                                        [ld_t[("q", h)]], war, kmax_t)
            prev_sts = [ctx_st.get((h - 1, qc)) for qc in range(2)]
            r = emit_head(p, b, qh_sb[h % 2][:], kh_sb[kv % 2][:],
                          vh_sb[kv % 2][:].rearrange("p tt d -> p (tt d)"),
                          64, INV_SQRT_HD, negC_t,
                          [ctx_ev[0][:], ctx_ev[1][:]],
                          [ld_t[("q", h)], (vs, negC_t)], [ld_t[("k", kv)]],
                          [ld_t[("v", kv)]],
                          ctx_war=[d for d in prev_sts if d])

            @blk.gpsimd
            def _(g, h=h, r=r):
                for qc in range(2):
                    ctx_st[(h, qc)] = csl.issue(
                        g, qc, ctx_int[h, :, qc * 512:(qc + 1) * 512],
                        ctx_ev[qc][:],
                        pre_deps=[(vs, r["ctxev"][qc])])
            hh_ts[h] = ts.n
            hh_ss[h] = ss.n
            hh_vs[h] = vs.n
        kv_last_ts[kv] = ts.n
    ctx_stores = [(s_, s_.n) for s_ in csl.sems if s_.n]
    phaseB_end = [(ts, ts.n), (vs, vs.n), (ss, ss.n)] + ctx_stores
    p.phase_close()

    p.phase_open()
    ps_o = [p.psum(f"ps_o{i}", [128, 512]) for i in range(4)]
    x_grp = p.sbuf("x_grp", [128, 4 * D], F32)
    wo_sb = [p.sbuf(f"wo_sb{i}", [128, 8, 512], BF16) for i in range(2)]
    ctx_grp = [p.sbuf(f"ctx_grp{i}", [128, 32, 512], BF16) for i in range(2)]
    b2 = dict(x_grp=x_grp, wo_sb=wo_sb, ps_o=ps_o, ones_bf=ones_bf,
              wo_ld=[None] * 64)
    cgsl = DmaSlots(p, "cgsl", 2)
    cg_ld = {}

    def ctx_chunk_fn(g, deps):
        @blk.gpsimd
        def _(gq, g=g):
            cg_ld[g] = cgsl.issue(
                gq, g, ctx_grp[g][:],
                ctx_int[:, :, g * 512:(g + 1) * 512].rearrange("h p r -> p h r"),
                pre_deps=list(phaseB_end))

        def get_lhsT(c, tl):
            return ctx_grp[g][:, c, tl * 128:(tl + 1) * 128]
        return get_lhsT, [cg_ld[g]]

    ysl_st = emit_outproj(p, b2, wo_h, None, ctx_chunk_fn, x_h, y_h, False,
                          phaseB_end)
    p.phase_close()
    if standalone:
        p.phase_close()
        return p.finish()
    ctx["stores"] = ysl_st
    return


ACT_COMBINED_SET = 6


def emit_act_table_pin(p):
    blk = p.block

    @blk.scalar
    def _(s):
        s.add_instruction(mybir.InstLoadActFuncSet(
            name=p.nc.get_next_instruction_name(),
            act_func_set_id=ACT_COMBINED_SET, ins=[], outs=[]))


def emit_stage_barrier(p):
    blk = p.block
    deps = p.stage_pre()

    for section in (blk.tensor, blk.vector, blk.scalar, blk.sync, blk.gpsimd):
        @section
        def _(eng):
            waits(eng, deps)
    return deps


def build_fused():
    p = Prog()
    nc = p.nc
    xb_h = nc.dram_tensor("xb", [4096, D], F32, kind="ExternalInput").ap()
    atwr_h = nc.dram_tensor("atwr", [D, 136], F32, kind="ExternalInput").ap()
    brow_h = nc.dram_tensor("brow", [1, 136], F32, kind="ExternalInput").ap()
    bflat_h = nc.dram_tensor("bflat", [128, D], F32, kind="ExternalInput").ap()
    wqkv_h = nc.dram_tensor("wqkv", [128, 96, DC + 1, 128], F32,
                            kind="ExternalInput").ap()
    gq_h = nc.dram_tensor("gq", [128, 1], F32, kind="ExternalInput").ap()
    gk_h = nc.dram_tensor("gk", [128, 1], F32, kind="ExternalInput").ap()
    wo0_h = nc.dram_tensor("wo0", [128, 4, 8, 8, 512], F32,
                           kind="ExternalInput").ap()
    bo0_h = nc.dram_tensor("bo0", [1, D], F32, kind="ExternalInput").ap()
    wallT_h = nc.dram_tensor("wallT", [D, 24], F32, kind="ExternalInput").ap()
    bases_h = nc.dram_tensor("bases", [1, 24], F32, kind="ExternalInput").ap()
    wb_h = nc.dram_tensor("wb", [1024, 1024], F32, kind="ExternalInput").ap()
    gate_h = nc.dram_tensor("gate", [1, 1], F32, kind="ExternalInput").ap()
    wq1_h = nc.dram_tensor("wq1", [128, 32, DC, 128], F32,
                           kind="ExternalInput").ap()
    wo1_h = nc.dram_tensor("wo1", [128, 4, 8, 8, 512], F32,
                           kind="ExternalInput").ap()
    ktg_h = nc.dram_tensor("ktg", [8, 128, 8192], F32, kind="ExternalInput").ap()
    vtg_h = nc.dram_tensor("vtg", [8, 8192, 128], F32, kind="ExternalInput").ap()
    cos_h = nc.dram_tensor("cos", [64, TOK], F32, kind="ExternalInput").ap()
    sin_h = nc.dram_tensor("sin", [64, TOK], F32, kind="ExternalInput").ap()
    y_h = nc.dram_tensor("y", [TOK, D], F32, kind="ExternalOutput").ap()

    x1b = nc.dram_tensor("x1b", [4096, D], F32).ap()
    xB = nc.dram_tensor("xB", [TOK, D], F32).ap()
    xC = nc.dram_tensor("xC", [TOK, D], F32).ap()
    qT0_int = nc.dram_tensor("qT0_int", [32, 128, TOK], BF16).ap()
    kTf_int = nc.dram_tensor("kTf_int", [32, 128, 4096], BF16).ap()
    vf_int = nc.dram_tensor("vf_int", [4096, D], BF16).ap()
    qT1_int = nc.dram_tensor("qT1_int", [32, 128, TOK], BF16).ap()
    ctx1_int = nc.dram_tensor("ctx1_int", [32, 128, TOK], BF16).ap()

    blk = p.start_block()
    emit_act_table_pin(p)

    def stage(fn, **aps):
        emit_stage_barrier(p)
        p.phase_open()
        c = dict(p=p, **aps)
        fn(c)
        p.phase_close()
        return c

    stage(build_ffn0, x=xb_h, y=x1b, atwr=atwr_h, brow=brow_h, bflat=bflat_h,
          ntile=32)
    stage(build_qkv0, x=x1b, wqkv=wqkv_h, gq=gq_h, gk=gk_h,
          qT=qT0_int, kT=kTf_int, v=vf_int, ngroups=4)
    stage(build_attn0, x=x1b[0:TOK, :], qT=qT0_int, kT3=kTf_int, vf=vf_int,
          wo=wo0_h, bo=bo0_h, y=xB)
    stage(build_ffn0, x=xB, y=xC, atwr=atwr_h, brow=brow_h, bflat=bflat_h)

    stage(build_ffn1, x=xC, y=xB, wallT=wallT_h, bases=bases_h, wb=wb_h,
          gate=gate_h)
    stage(build_attn1, x=xB, y=xC, wq=wq1_h, wo=wo1_h, ktg=ktg_h,
          vtg=vtg_h, cos=cos_h, sin=sin_h, qT_int=qT1_int, ctx_int=ctx1_int)
    c6 = stage(build_ffn1, x=xC, y=y_h, wallT=wallT_h, bases=bases_h,
               wb=wb_h, gate=gate_h)

    @blk.gpsimd
    def _(g):
        waits(g, c6["stores"])

    return p.finish()


def kernel_fused(inputs):
    nc = _get("fused", build_fused)
    x = np.ascontiguousarray(
        inputs["L0_attn_hidden_states"].reshape(2 * 4096, D).astype(np.float32))
    xbat = []
    for c in range(NCORES):
        bb = c // 4
        rows = np.concatenate(
            [x[c * TOK:(c + 1) * TOK]] +
            [x[cc * TOK:(cc + 1) * TOK] for cc in range(bb * 4, bb * 4 + 4)
             if cc != c], axis=0)
        xbat.append(np.ascontiguousarray(rows))
    A, B = inputs["L0_ffn_A"], inputs["L0_ffn_B"]
    AT = np.ascontiguousarray(A.reshape(128, D).T)
    WrT = np.ascontiguousarray(inputs["L0_ffn_Wr"].T)
    atwr = np.ascontiguousarray(np.concatenate([AT, WrT], axis=1))
    brow = np.ascontiguousarray(np.concatenate(
        [np.zeros(128, np.float32), inputs["L0_ffn_br"]]).reshape(1, 136))
    bflat = np.ascontiguousarray(B.transpose(0, 2, 1).reshape(128, D))
    wqkv = np.concatenate([inputs["L0_attn_Wq"], inputs["L0_attn_Wk"],
                           inputs["L0_attn_Wv"]], axis=1)
    bpad = np.zeros((128, 12288), np.float32)
    bpad[0] = np.concatenate([inputs["L0_attn_bq"], inputs["L0_attn_bk"],
                              inputs["L0_attn_bv"]])
    wqkv = _wcols(np.concatenate([wqkv, bpad], axis=0), 128)
    wall = np.concatenate([inputs["L1_ffn_W_res"], inputs["L1_ffn_W_pre"],
                           inputs["L1_ffn_W_post"]], axis=0)
    wallT = np.ascontiguousarray(wall.T)
    bases = np.ascontiguousarray(np.concatenate(
        [inputs["L1_ffn_H_res_base"].reshape(16), inputs["L1_ffn_H_pre_base"],
         inputs["L1_ffn_H_post_base"]]).reshape(1, 24))
    blocks = inputs["L1_attn_kv_cache"][inputs["L1_attn_block_table"]]
    Kg = blocks[:, 0].transpose(1, 0, 2, 3).reshape(8, 8192, 128)
    Vg = blocks[:, 1].transpose(1, 0, 2, 3).reshape(8, 8192, 128)
    ktg = np.ascontiguousarray(Kg.transpose(0, 2, 1))
    vtg = np.ascontiguousarray(Vg)
    inv_freq = (1.0 / 10000.0 ** (np.arange(0, 128, 2, dtype=np.float32) / 128.0)
                ).astype(np.float32)
    ang = (np.arange(4096, dtype=np.float32)[:, None] * inv_freq[None, :])
    cos_full = np.cos(ang).astype(np.float32)
    sin_full = np.sin(ang).astype(np.float32)
    common = {
        "atwr": atwr, "brow": brow, "bflat": bflat, "wqkv": wqkv,
        "gq": np.ascontiguousarray(inputs["L0_attn_gq"].reshape(128, 1)),
        "gk": np.ascontiguousarray(inputs["L0_attn_gk"].reshape(128, 1)),
        "wo0": np.ascontiguousarray(inputs["L0_attn_Wo"].reshape(
            4, 8, 128, 8, 512).transpose(2, 0, 3, 1, 4)),
        "bo0": np.ascontiguousarray(inputs["L0_attn_bo"].reshape(1, D)),
        "wallT": wallT, "bases": bases,
        "wb": np.ascontiguousarray(inputs["L1_ffn_W_branch"]),
        "gate": np.ascontiguousarray(inputs["L1_ffn_gate"].reshape(1, 1)),
        "wq1": _wcols(inputs["L1_attn_Wq"], 128),
        "wo1": np.ascontiguousarray(inputs["L1_attn_Wo"].reshape(
            4, 8, 128, 8, 512).transpose(2, 0, 3, 1, 4)),
        "ktg": ktg, "vtg": vtg,
    }
    in_maps = []
    for c in range(NCORES):
        n0 = (c % 4) * TOK
        m = dict(common)
        m["xb"] = xbat[c]
        m["cos"] = np.ascontiguousarray(cos_full[n0:n0 + TOK].T)
        m["sin"] = np.ascontiguousarray(sin_full[n0:n0 + TOK].T)
        in_maps.append(m)
    res = _spmd(nc, in_maps)
    y = np.concatenate([r["y"] for r in res.results], axis=0)
    return y.reshape(2, 4096, D)



_NC_CACHE = {}


def _get(name, builder):
    if name not in _NC_CACHE:
        _NC_CACHE[name] = builder()
    return _NC_CACHE[name]


def _wcols(W, blk):
    R, C = W.shape
    return np.ascontiguousarray(
        W.reshape(R // 128, 128, C // blk, blk).transpose(1, 2, 0, 3))


def _spmd(nc, in_maps):
    return run_bass_kernel_spmd(nc, in_maps, core_ids=list(range(NCORES)))


def run_ffn0(x, A, B, Wr, br):
    nc = _get("ffn0", build_ffn0)
    AT = np.ascontiguousarray(A.reshape(128, D).T)
    WrT = np.ascontiguousarray(Wr.T)
    atwr = np.ascontiguousarray(np.concatenate([AT, WrT], axis=1))
    brow = np.ascontiguousarray(
        np.concatenate([np.zeros(128, np.float32), br]).reshape(1, 136))
    bflat = np.ascontiguousarray(B.transpose(0, 2, 1).reshape(128, D))
    in_maps = [{"x": np.ascontiguousarray(x[c * TOK:(c + 1) * TOK]),
                "atwr": atwr, "brow": brow, "bflat": bflat}
               for c in range(NCORES)]
    res = _spmd(nc, in_maps)
    return np.concatenate([r["y"] for r in res.results], axis=0)


def run_ffn1(x, W_res, W_pre, W_post, Hrb, Hpb, Hqb, gate, Wb):
    nc = _get("ffn1", build_ffn1)
    wall = np.concatenate([W_res, W_pre, W_post], axis=0)
    wallT = np.ascontiguousarray(wall.T)
    bases = np.ascontiguousarray(
        np.concatenate([Hrb.reshape(16), Hpb, Hqb]).reshape(1, 24))
    gate2 = np.ascontiguousarray(gate.reshape(1, 1))
    wb = np.ascontiguousarray(Wb)
    in_maps = [{"x": np.ascontiguousarray(x[c * TOK:(c + 1) * TOK]),
                "wallT": wallT, "bases": bases, "wb": wb, "gate": gate2}
               for c in range(NCORES)]
    res = _spmd(nc, in_maps)
    return np.concatenate([r["y"] for r in res.results], axis=0)


def run_qkv0(x, Wq, bq, Wk, bk, Wv, bv, gq, gk):
    nc = _get("qkv0", build_qkv0)
    wqkv = np.concatenate([Wq, Wk, Wv], axis=1)
    bpad = np.zeros((128, 12288), np.float32)
    bpad[0] = np.concatenate([bq, bk, bv])
    wqkv = _wcols(np.concatenate([wqkv, bpad], axis=0), 128)
    gq2 = np.ascontiguousarray(gq.reshape(128, 1))
    gk2 = np.ascontiguousarray(gk.reshape(128, 1))
    in_maps = [{"x": np.ascontiguousarray(x[c * TOK:(c + 1) * TOK]),
                "wqkv": wqkv, "gq": gq2, "gk": gk2}
               for c in range(NCORES)]
    res = _spmd(nc, in_maps)
    qT = [r["qT"] for r in res.results]
    kT = [r["kT"] for r in res.results]
    v = [r["v"] for r in res.results]
    return qT, kT, v


def run_attn0(x, qT, kT, v, Wo, bo):
    nc = _get("attn0", build_attn0)
    kT_full = [np.ascontiguousarray(np.concatenate(kT[b * 4:(b + 1) * 4], axis=2))
               for b in range(2)]
    v_full = [np.ascontiguousarray(np.concatenate(v[b * 4:(b + 1) * 4], axis=0))
              for b in range(2)]
    wo = np.ascontiguousarray(Wo.reshape(4, 8, 128, 8, 512).transpose(2, 0, 3, 1, 4))
    bo2 = np.ascontiguousarray(bo.reshape(1, D))
    in_maps = [{"x": np.ascontiguousarray(x[c * TOK:(c + 1) * TOK]),
                "qT": qT[c], "kT": kT_full[c // 4], "vf": v_full[c // 4],
                "wo": wo, "bo": bo2}
               for c in range(NCORES)]
    res = _spmd(nc, in_maps)
    return np.concatenate([r["y"] for r in res.results], axis=0)


def run_attn1(x, Wq, Wo, kv_cache, block_table):
    nc = _get("attn1", build_attn1)
    blocks = kv_cache[block_table]
    K = blocks[:, 0].transpose(1, 0, 2, 3).reshape(8, 8192, 128)
    V = blocks[:, 1].transpose(1, 0, 2, 3).reshape(8, 8192, 128)
    ktg = np.ascontiguousarray(K.transpose(0, 2, 1))
    vtg = np.ascontiguousarray(V)
    wq = _wcols(Wq, 128)
    wo = np.ascontiguousarray(Wo.reshape(4, 8, 128, 8, 512).transpose(2, 0, 3, 1, 4))
    inv_freq = (1.0 / 10000.0 ** (np.arange(0, 128, 2, dtype=np.float32) / 128.0)
                ).astype(np.float32)
    ang = (np.arange(4096, dtype=np.float32)[:, None] * inv_freq[None, :])
    cos_full = np.cos(ang).astype(np.float32)
    sin_full = np.sin(ang).astype(np.float32)
    in_maps = []
    for c in range(NCORES):
        n0 = (c % 4) * TOK
        in_maps.append({"x": np.ascontiguousarray(x[c * TOK:(c + 1) * TOK]),
                        "wq": wq, "wo": wo, "ktg": ktg, "vtg": vtg,
                        "cos": np.ascontiguousarray(cos_full[n0:n0 + TOK].T),
                        "sin": np.ascontiguousarray(sin_full[n0:n0 + TOK].T)})
    res = _spmd(nc, in_maps)
    return np.concatenate([r["y"] for r in res.results], axis=0)


def kernel_unfused(**inputs):
    x = np.ascontiguousarray(
        inputs["L0_attn_hidden_states"].reshape(2 * 4096, D).astype(np.float32))
    x = run_ffn0(x, inputs["L0_ffn_A"], inputs["L0_ffn_B"],
                 inputs["L0_ffn_Wr"], inputs["L0_ffn_br"])
    qT, kT, v = run_qkv0(x, inputs["L0_attn_Wq"], inputs["L0_attn_bq"],
                         inputs["L0_attn_Wk"], inputs["L0_attn_bk"],
                         inputs["L0_attn_Wv"], inputs["L0_attn_bv"],
                         inputs["L0_attn_gq"], inputs["L0_attn_gk"])
    x = run_attn0(x, qT, kT, v, inputs["L0_attn_Wo"], inputs["L0_attn_bo"])
    x = run_ffn0(x, inputs["L0_ffn_A"], inputs["L0_ffn_B"],
                 inputs["L0_ffn_Wr"], inputs["L0_ffn_br"])
    x = run_ffn1(x, inputs["L1_ffn_W_res"], inputs["L1_ffn_W_pre"],
                 inputs["L1_ffn_W_post"], inputs["L1_ffn_H_res_base"],
                 inputs["L1_ffn_H_pre_base"], inputs["L1_ffn_H_post_base"],
                 inputs["L1_ffn_gate"], inputs["L1_ffn_W_branch"])
    x = run_attn1(x, inputs["L1_attn_Wq"], inputs["L1_attn_Wo"],
                  inputs["L1_attn_kv_cache"], inputs["L1_attn_block_table"])
    x = run_ffn1(x, inputs["L1_ffn_W_res"], inputs["L1_ffn_W_pre"],
                 inputs["L1_ffn_W_post"], inputs["L1_ffn_H_res_base"],
                 inputs["L1_ffn_H_pre_base"], inputs["L1_ffn_H_post_base"],
                 inputs["L1_ffn_gate"], inputs["L1_ffn_W_branch"])
    return x.reshape(2, 4096, D)


def kernel(**inputs):
    return kernel_fused(inputs)


# version 09
# speedup vs baseline: 1.2584x; 1.2584x over previous
import numpy as np
from contextlib import ExitStack

import concourse.bass as bass
import concourse.mybir as mybir
from concourse.bass_utils import run_bass_kernel_spmd

F32 = mybir.dt.float32
BF16 = mybir.dt.bfloat16
AF = mybir.ActivationFunctionType
ALU = mybir.AluOpType
AX = mybir.AxisListType

NCORES = 8
TOK = 1024
NT = TOK // 128
D = 4096
DC = D // 128
LN_EPS = 1e-5
RMS_EPS_FFN1 = 1.1920929e-07
QK_EPS = 1e-6
INV_SQRT_HD = float(1.0 / np.sqrt(128.0))
C_MARGIN = 30.0


class Sem:
    def __init__(self, nc, es, name):
        self.h = es.enter_context(nc.semaphore(name))
        self.n = 0

    def tick(self, instr, k=1):
        instr.then_inc(self.h, k)
        self.n += k
        return self.n


def waits(eng, deps):
    for sem, n in deps:
        if n > 0:
            eng.wait_ge(sem.h, n)


class DmaSlots:

    def __init__(self, p, name, nslots):
        self.sems = [p.sem(f"{name}{i}") for i in range(nslots)]
        self.nslots = nslots

    def issue(self, eng, slot, out, in_, pre_deps=()):
        s = self.sems[slot % self.nslots]
        if s.n > 0:
            eng.wait_ge(s.h, s.n)
        waits(eng, pre_deps)
        s.tick(eng.dma_start(out=out, in_=in_), 16)
        return (s, s.n)

    def issue_multi(self, eng, slot, pairs, pre_deps=()):
        s = self.sems[slot % self.nslots]
        if s.n > 0:
            eng.wait_ge(s.h, s.n)
        waits(eng, pre_deps)
        for out, in_ in pairs:
            s.tick(eng.dma_start(out=out, in_=in_), 16)
        return (s, s.n)


class Prog:
    def __init__(self):
        self.nc = bass.Bass("TRN2", target_bir_lowering=False)
        self.es = ExitStack()
        self.block = None
        self._sems = {}
        self._phases = []
        self._uid = 0

    def uid(self):
        self._uid += 1
        return self._uid

    def sem(self, name):
        if name not in self._sems:
            self._sems[name] = Sem(self.nc, self.es, name)
        return self._sems[name]

    def sbuf(self, name, shape, dtype):
        es = self._phases[-1] if self._phases else self.es
        if self._phases:
            name = f"{name}_u{self.uid()}"
        return es.enter_context(self.nc.sbuf_tensor(name, shape, dtype))

    def psum(self, name, shape, dtype=F32):
        es = self._phases[-1] if self._phases else self.es
        if self._phases:
            name = f"{name}_u{self.uid()}"
        return es.enter_context(self.nc.psum_tensor(name, shape, dtype))

    def phase_open(self):
        self._phases.append(ExitStack())

    def phase_close(self):
        self._phases.pop().close()

    def stage_pre(self):
        deps = []
        for s in self._sems.values():
            if s.n > 0:
                deps.append((s, s.n))
        return deps

    def start_block(self):
        self.block = self.es.enter_context(self.nc.Block())
        return self.block

    def finish(self):
        self.es.close()
        return self.nc


def emit_norm_tile(p, xa, out_bf, bufs, v_deps, s_deps, apply_deps,
                   rms_only=False, eps=LN_EPS, nfree=D):
    blk = p.block
    vs, ss = p.sem("vs"), p.sem("ss")
    stats, sq, eps_ap = bufs["stats"], bufs["sq"], bufs["eps_" + ("rms" if rms_only else "ln")]
    inv_n = 1.0 / nfree

    if not rms_only:
        @blk.vector
        def _(v):
            waits(v, v_deps)
            vs.tick(v.tensor_reduce(stats[:, 0:1], xa, axis=AX.X, op=ALU.add))
        red_t = vs.n

    @blk.scalar
    def _(s):
        waits(s, s_deps)
        ss.tick(s.activation(out=sq[:, 0:nfree], in_=xa, func=AF.Square,
                             accum_out=stats[:, 1:2]))
    sq_t = ss.n

    if rms_only:
        @blk.scalar
        def _(s):
            s.wait_ge(ss.h, sq_t)
            waits(s, v_deps)
            ss.tick(s.activation(out=stats[:, 3:4], in_=stats[:, 1:2],
                                 func=AF.Ln, bias=eps_ap[:], scale=inv_n))
            s.wait_ge(ss.h, ss.n)
            ss.tick(s.activation(out=stats[:, 3:4], in_=stats[:, 3:4],
                                 func=AF.Exp, scale=-0.5))
        r_t = ss.n

        @blk.vector
        def _(v):
            v.wait_ge(ss.h, r_t)
            waits(v, apply_deps)
            vs.tick(v.tensor_scalar(out=out_bf, in0=xa, scalar1=stats[:, 3:4],
                                    scalar2=None, op0=ALU.mult))
        return vs.n

    @blk.vector
    def _(v):
        v.wait_ge(vs.h, red_t)
        v.wait_ge(ss.h, sq_t)
        vs.tick(v.tensor_scalar(out=stats[:, 0:2], in0=stats[:, 0:2],
                                scalar1=inv_n, scalar2=None, op0=ALU.mult))
        v.wait_ge(vs.h, vs.n)
        vs.tick(v.scalar_tensor_tensor(out=stats[:, 2:3], in0=stats[:, 0:1],
                                       scalar=-1.0, in1=stats[:, 0:1],
                                       op0=ALU.mult, op1=ALU.mult))
        v.wait_ge(vs.h, vs.n)
        vs.tick(v.tensor_tensor(stats[:, 2:3], stats[:, 1:2], stats[:, 2:3],
                                op=ALU.add))
    var_t = vs.n

    @blk.scalar
    def _(s):
        s.wait_ge(vs.h, var_t)
        ss.tick(s.activation(out=stats[:, 3:4], in_=stats[:, 2:3],
                             func=AF.Ln, bias=eps_ap[:], scale=1.0))
        s.wait_ge(ss.h, ss.n)
        ss.tick(s.activation(out=stats[:, 3:4], in_=stats[:, 3:4],
                             func=AF.Exp, scale=-0.5))
    r_t = ss.n

    @blk.vector
    def _(v):
        v.wait_ge(ss.h, r_t)
        waits(v, apply_deps)
        vs.tick(v.tensor_scalar(out=out_bf, in0=xa, scalar1=stats[:, 0:1],
                                scalar2=stats[:, 3:4],
                                op0=ALU.subtract, op1=ALU.mult))
    return vs.n


def emit_xbar(p, out_ap, in_ap, pre_deps):
    blk = p.block
    tr = p.sem("tr")
    prev = tr.n

    @blk.sync
    def _(sync):
        waits(sync, pre_deps)
        if prev > 0:
            sync.wait_ge(tr.h, prev)
        tr.tick(sync.dma_start_transpose(out_ap, in_ap), 16)
    return tr.n



def build_ffn0(ctx=None, ntile=NT):
    if ctx is None:
        p = Prog()
        nc = p.nc
        x_h = nc.dram_tensor("x", [TOK, D], F32, kind="ExternalInput").ap()
        atwr_h = nc.dram_tensor("atwr", [D, 136], F32, kind="ExternalInput").ap()
        brow_h = nc.dram_tensor("brow", [1, 136], F32, kind="ExternalInput").ap()
        bflat_h = nc.dram_tensor("bflat", [128, D], F32, kind="ExternalInput").ap()
        y_h = nc.dram_tensor("y", [TOK, D], F32, kind="ExternalOutput").ap()
        p.start_block()
        emit_act_table_pin(p)
        p.phase_open()
    else:
        p = ctx["p"]
        x_h, y_h = ctx["x"], ctx["y"]
        atwr_h, brow_h, bflat_h = ctx["atwr"], ctx["brow"], ctx["bflat"]
        ntile = ctx.get("ntile", ntile)

    xr = [p.sbuf(f"xr{i}", [128, D], F32) for i in range(3)]
    lnx = [p.sbuf(f"lnx{i}", [128, D], BF16) for i in range(2)]
    lnxT = [p.sbuf(f"lnxT{i}", [128, DC, 128], BF16) for i in range(2)]
    sq = p.sbuf("sq", [128, D], BF16)
    stats = p.sbuf("stats", [128, 4], F32)
    eps_ln = p.sbuf("eps_ln", [128, 1], F32)
    atwr_sb = p.sbuf("atwr_sb", [128, DC * 136], BF16)
    brow_sb = p.sbuf("brow_sb", [1, 136], BF16)
    bflat_sb = p.sbuf("bflat_sb", [128, D], BF16)
    ones_bf = p.sbuf("ones_bf", [1, 128], BF16)
    er = p.sbuf("er", [128, 8], F32)
    shat = p.sbuf("shat", [128, 8], F32)
    smax = p.sbuf("smax", [128, 1], F32)
    ssum = p.sbuf("ssum", [128, 1], F32)
    rr = p.sbuf("rr", [128, 1], F32)
    slow = [p.sbuf(f"slow{i}", [128, 128], BF16) for i in range(2)]
    slowT = [p.sbuf(f"slowT{i}", [128, 128], BF16) for i in range(2)]

    ps_low = [p.psum(f"ps_low{i}", [128, 512]) for i in range(2)]
    ps_up = [p.psum(f"ps_up{i}", [128, 512]) for i in range(2)]

    blk = p.block
    ldw = p.sem("ldw")
    tr = p.sem("tr")
    ts = p.sem("ts")
    vs, ss = p.sem("vs"), p.sem("ss")
    xsl = DmaSlots(p, "xld", 3)
    ysl = DmaSlots(p, "yst", 3)

    @blk.gpsimd
    def _(g):
        ldw.tick(g.dma_start(out=atwr_sb[:].rearrange("p (c n) -> p c n", c=DC),
                             in_=atwr_h.rearrange("(c p) n -> p c n", p=128)), 16)
        ldw.tick(g.dma_start(out=brow_sb[:], in_=brow_h), 16)
        ldw.tick(g.dma_start(out=bflat_sb[:], in_=bflat_h), 16)
    w_ld = ldw.n

    @blk.vector
    def _(v):
        v.memset(ones_bf[:], 1.0)
        vs.tick(v.memset(eps_ln[:], LN_EPS))
    init_t = vs.n

    bufs = dict(stats=stats, sq=sq, eps_ln=eps_ln)

    x_ld = [None] * ntile
    apply_t = [0] * ntile
    low_t = [0] * ntile
    exp_t = [0] * ntile
    slow_t = [0] * ntile
    trL = [0] * ntile
    trS = [0] * ntile
    up_t = [[0] * 8 for _ in range(ntile)]
    evac_t = [[0] * 8 for _ in range(ntile)]
    st_t = [None] * ntile

    @blk.gpsimd
    def _(g):
        for j in range(min(3, ntile)):
            x_ld[j] = xsl.issue(g, j, xr[j % 3][:],
                                x_h[j * 128:(j + 1) * 128, :])

    for j in range(ntile):
        xa = xr[j % 3][:]
        v_deps = [x_ld[j], (vs, init_t)]
        s_deps = [x_ld[j]]
        if j >= 1:
            v_deps.append((vs, apply_t[j - 1]))
            s_deps.append((vs, apply_t[j - 1]))
        apply_deps = []
        if j >= 2:
            apply_deps.append((tr, trL[j - 2]))
        apply_t[j] = emit_norm_tile(p, xa, lnx[j % 2][:], bufs,
                                    v_deps, s_deps, apply_deps)

        pre = [(vs, apply_t[j])]
        if j >= 2:
            pre.append((ts, low_t[j - 2]))
        trL[j] = emit_xbar(p, lnxT[j % 2][:], lnx[j % 2][:], pre)

        @blk.tensor
        def _(t, j=j):
            t.wait_ge(tr.h, trL[j])
            t.wait_ge(ldw.h, w_ld)
            if j >= 2:
                t.wait_ge(vs.h, slow_t[j - 2])
                t.wait_ge(ss.h, exp_t[j - 2])
            pl = ps_low[j % 2][:, 0:136]
            for c in range(DC):
                t.matmul(pl, lnxT[j % 2][:, c, :],
                         atwr_sb[:, c * 136:(c + 1) * 136],
                         start=(c == 0), stop=False)
            ts.tick(t.matmul(pl, ones_bf[:, 0:128], brow_sb[:],
                             start=False, stop=True))
        low_t[j] = ts.n

        @blk.vector
        def _(v, j=j):
            v.wait_ge(ts.h, low_t[j])
            if j >= 1:
                v.wait_ge(ss.h, exp_t[j - 1])
            vs.tick(v.tensor_reduce(smax[:], ps_low[j % 2][:, 128:136],
                                    axis=AX.X, op=ALU.max, negate=True))
        smax_t = vs.n

        @blk.scalar
        def _(s, j=j, smax_t=smax_t):
            s.wait_ge(vs.h, smax_t)
            if j >= 1:
                s.wait_ge(vs.h, slow_t[j - 1])
            ss.tick(s.activation(out=er[:], in_=ps_low[j % 2][:, 128:136],
                                 func=AF.Exp, bias=smax[:], accum_out=ssum[:]))
        exp_t[j] = ss.n

        @blk.vector
        def _(v, j=j):
            v.wait_ge(ss.h, exp_t[j])
            vs.tick(v.reciprocal(rr[:], ssum[:]))
            v.wait_ge(vs.h, vs.n)
            vs.tick(v.tensor_scalar(out=shat[:], in0=er[:], scalar1=rr[:],
                                    scalar2=8.0, op0=ALU.mult, op1=ALU.mult))
            v.wait_ge(vs.h, vs.n)
            if j >= 2:
                v.wait_ge(tr.h, trS[j - 2])
            vs.tick(v.tensor_tensor(
                slow[j % 2][:].rearrange("p (h r) -> p h r", h=8),
                ps_low[j % 2][:, 0:128].rearrange("p (h r) -> p h r", h=8),
                shat[:].unsqueeze(-1).broadcast_to([128, 8, 16]),
                op=ALU.mult))
        slow_t[j] = vs.n

        pre = [(vs, slow_t[j])]
        if j >= 2:
            pre.append((ts, up_t[j - 2][7]))
        trS[j] = emit_xbar(p, slowT[j % 2][:], slow[j % 2][:], pre)

        for oc in range(8):
            @blk.tensor
            def _(t, j=j, oc=oc):
                t.wait_ge(tr.h, trS[j])
                if oc >= 2:
                    t.wait_ge(vs.h, evac_t[j][oc - 2])
                elif j >= 1:
                    t.wait_ge(vs.h, evac_t[j - 1][6 + oc])
                ts.tick(t.matmul(ps_up[oc % 2][:], slowT[j % 2][:],
                                 bflat_sb[:, oc * 512:(oc + 1) * 512],
                                 start=True, stop=True))
            up_t[j][oc] = ts.n

            @blk.vector
            def _(v, j=j, oc=oc):
                v.wait_ge(ts.h, up_t[j][oc])
                xa_s = xr[j % 3][:, oc * 512:(oc + 1) * 512]
                vs.tick(v.tensor_tensor(xa_s, xa_s, ps_up[oc % 2][:], op=ALU.add))
            evac_t[j][oc] = vs.n

        @blk.scalar
        def _(s_, j=j):
            st_t[j] = ysl.issue(s_, j, y_h[j * 128:(j + 1) * 128, :],
                                xr[j % 3][:], pre_deps=[(vs, evac_t[j][7])])

        if j + 3 < ntile:
            @blk.gpsimd
            def _(g, j=j):
                x_ld[j + 3] = xsl.issue(g, j + 3, xr[j % 3][:],
                                        x_h[(j + 3) * 128:(j + 4) * 128, :],
                                        pre_deps=[st_t[j]])

    if ctx is None:
        @blk.scalar
        def _(g):
            for s_ in ysl.sems:
                if s_.n:
                    g.wait_ge(s_.h, s_.n)
        p.phase_close()
        return p.finish()
    ctx["stores"] = [(s_, s_.n) for s_ in ysl.sems if s_.n]
    return



def build_ffn1(ctx=None):
    if ctx is None:
        p = Prog()
        nc = p.nc
        x_h = nc.dram_tensor("x", [TOK, D], F32, kind="ExternalInput").ap()
        wallT_h = nc.dram_tensor("wallT", [D, 24], F32, kind="ExternalInput").ap()
        bases_h = nc.dram_tensor("bases", [1, 24], F32, kind="ExternalInput").ap()
        wb_h = nc.dram_tensor("wb", [1024, 1024], F32, kind="ExternalInput").ap()
        gate_h = nc.dram_tensor("gate", [1, 1], F32, kind="ExternalInput").ap()
        y_h = nc.dram_tensor("y", [TOK, D], F32, kind="ExternalOutput").ap()
        p.start_block()
        emit_act_table_pin(p)
        p.phase_open()
    else:
        p = ctx["p"]
        x_h, y_h = ctx["x"], ctx["y"]
        wallT_h, bases_h = ctx["wallT"], ctx["bases"]
        wb_h, gate_h = ctx["wb"], ctx["gate"]

    xr = [p.sbuf(f"xr{i}", [128, D], F32) for i in range(2)]
    lnf = [p.sbuf(f"lnf{i}", [128, D], F32) for i in range(2)]
    lnx = [p.sbuf(f"lnx{i}", [128, D], BF16) for i in range(2)]
    lnxT = [p.sbuf(f"lnxT{i}", [128, DC, 128], BF16) for i in range(2)]
    sq = p.sbuf("sq", [128, D], BF16)
    stats = p.sbuf("stats", [128, 4], F32)
    eps_rms = p.sbuf("eps_rms", [128, 1], F32)
    eps_ln = p.sbuf("eps_ln", [128, 1], F32)
    wallT_sb = p.sbuf("wallT_sb", [128, DC * 24], BF16)
    bases_sb = p.sbuf("bases_sb", [1, 24], BF16)
    ones_bf = p.sbuf("ones_bf", [1, 128], BF16)
    ones_f32 = p.sbuf("ones_f32", [1, 128], F32)
    Z = p.sbuf("Z", [128, NT * 16], F32)
    Hpre = p.sbuf("Hpre", [128, NT * 4], F32)
    Hpost = p.sbuf("Hpost", [128, NT * 4], F32)
    uu = p.sbuf("uu", [128, NT * 4], F32)
    vv = p.sbuf("vv", [128, NT * 4], F32)
    M1 = p.sbuf("M1", [128, NT * 16], F32)
    M2 = p.sbuf("M2", [128, NT * 16], F32)
    mx = p.sbuf("mx", [128, NT * 4], F32)
    sgum = p.sbuf("sgum", [128, NT * 4], F32)
    smax1 = p.sbuf("smax1", [128, 1], F32)
    er4a = p.sbuf("er4a", [128, 4], F32)
    ssum1 = p.sbuf("ssum1", [128, 1], F32)
    rr1 = p.sbuf("rr1", [128, 1], F32)
    smax2 = p.sbuf("smax2", [128, 1], F32)
    er4b = p.sbuf("er4b", [128, 4], F32)
    ssum2 = p.sbuf("ssum2", [128, 1], F32)
    rr2 = p.sbuf("rr2", [128, 1], F32)
    xpre = p.sbuf("xpre", [128, 1024], F32)
    xpre_bf = p.sbuf("xpre_bf", [128, 1024], BF16)
    xpT = p.sbuf("xpT", [128, 8, NT, 128], BF16)
    wb_sb = p.sbuf("wb_sb", [128, 8 * 1024], BF16)
    y_all = p.sbuf("y_all", [128, NT * 1024], F32)
    gsc = p.sbuf("gsc", [1, 4], F32)
    grep = p.sbuf("grep", [128, 2], F32)
    acc = p.sbuf("acc", [128, 1024], F32)

    ps_lg = [p.psum(f"ps_lg{i}", [128, 512]) for i in range(2)]
    ps_y = [p.psum(f"ps_y{i}", [128, 512]) for i in range(2)]
    ps_g = p.psum("ps_g", [128, 512])

    blk = p.block
    ldw = p.sem("ldw")
    stg = p.sem("stg")
    tr = p.sem("tr")
    ts = p.sem("ts")
    vs, ss = p.sem("vs"), p.sem("ss")
    xsl = DmaSlots(p, "xld", 2)
    xsl3 = DmaSlots(p, "xld3", 2)
    ysl = DmaSlots(p, "yst", 2)

    @blk.gpsimd
    def _(g):
        ldw.tick(g.dma_start(out=wallT_sb[:].rearrange("p (c n) -> p c n", c=DC),
                             in_=wallT_h.rearrange("(c p) n -> p c n", p=128)), 16)
        ldw.tick(g.dma_start(out=bases_sb[:], in_=bases_h), 16)
        ldw.tick(g.dma_start(out=wb_sb[:].rearrange("p (c n) -> p c n", c=8),
                             in_=wb_h.rearrange("(c p) n -> p c n", p=128)), 16)
        ldw.tick(g.dma_start(out=gsc[:, 0:1], in_=gate_h), 16)
    w_ld = ldw.n

    @blk.vector
    def _(v):
        v.memset(ones_bf[:], 1.0)
        v.memset(ones_f32[:], 1.0)
        v.memset(eps_rms[:], RMS_EPS_FFN1)
        v.memset(eps_ln[:], LN_EPS)
        v.memset(uu[:], 0.0)
        vs.tick(v.memset(vv[:], 0.0))
    init_t = vs.n

    bufs = dict(stats=stats, sq=sq, eps_rms=eps_rms, eps_ln=eps_ln)

    @blk.scalar
    def _(s):
        s.wait_ge(ldw.h, w_ld)
        ss.tick(s.activation(out=gsc[:, 1:2], in_=gsc[:, 0:1], func=AF.Exp,
                             scale=-1.0))
    g_e = ss.n

    @blk.vector
    def _(v):
        v.wait_ge(ss.h, g_e)
        vs.tick(v.tensor_scalar(out=gsc[:, 1:2], in0=gsc[:, 1:2], scalar1=1.0,
                                scalar2=None, op0=ALU.add))
        v.wait_ge(vs.h, vs.n)
        vs.tick(v.reciprocal(gsc[:, 1:2], gsc[:, 1:2]))
        v.wait_ge(vs.h, vs.n)
        vs.tick(v.tensor_scalar(out=gsc[:, 2:3], in0=gsc[:, 1:2], scalar1=0.5,
                                scalar2=None, op0=ALU.mult))
        v.wait_ge(vs.h, vs.n)
        vs.tick(v.tensor_scalar(out=gsc[:, 3:4], in0=gsc[:, 1:2], scalar1=-0.5,
                                scalar2=0.5, op0=ALU.mult, op1=ALU.add))
    g_pair = vs.n

    @blk.tensor
    def _(t):
        t.wait_ge(vs.h, g_pair)
        ts.tick(t.matmul(ps_g[:, 0:2], ones_f32[:], gsc[:, 2:4],
                         start=True, stop=True))
    g_mm = ts.n

    @blk.vector
    def _(v):
        v.wait_ge(ts.h, g_mm)
        vs.tick(v.tensor_copy(grep[:], ps_g[:, 0:2]))
    g_rep = vs.n

    apply_t = [0] * NT
    lg_t = [0] * NT
    z_t = [0] * NT
    pre_t = [0] * NT
    pe_t = [0] * NT
    qe_t = [0] * NT
    post_t = [0] * NT
    xpre_t = [0] * NT
    trL = [0] * NT
    trX = [0] * NT
    y_t = [[0, 0] for _ in range(NT)]
    yev_t = [[0, 0] for _ in range(NT)]
    x_ld = [None] * NT
    xlast_t = [0] * NT

    @blk.gpsimd
    def _(g):
        x_ld[0] = xsl.issue(g, 0, xr[0][:], x_h[0:128, :])
        x_ld[1] = xsl.issue(g, 1, xr[1][:], x_h[128:256, :])

    lnf_t = [0] * NT
    for j in range(NT):
        v_deps = [x_ld[j], (vs, init_t)]
        s_deps = [x_ld[j], (ldw, w_ld)]
        if j >= 1:
            v_deps.append((vs, apply_t[j - 1]))
            s_deps.append((vs, apply_t[j - 1]))
        lnf_deps = []
        if j >= 2:
            lnf_deps.append((vs, xpre_t[j - 2]))
        lnf_t[j] = emit_norm_tile(p, xr[j % 2][:], lnf[j % 2][:], bufs,
                                  v_deps, s_deps, lnf_deps)
        v_deps2 = [(vs, lnf_t[j])]
        s_deps2 = [(vs, lnf_t[j])]
        apply_deps = []
        if j >= 2:
            apply_deps.append((tr, trL[j - 2]))
        apply_t[j] = emit_norm_tile(p, lnf[j % 2][:], lnx[j % 2][:], bufs,
                                    v_deps2, s_deps2, apply_deps, rms_only=True)

        pre = [(vs, apply_t[j])]
        if j >= 2:
            pre.append((ts, lg_t[j - 2]))
        trL[j] = emit_xbar(p, lnxT[j % 2][:], lnx[j % 2][:], pre)

        @blk.tensor
        def _(t, j=j):
            t.wait_ge(tr.h, trL[j])
            t.wait_ge(ldw.h, w_ld)
            if j >= 2:
                t.wait_ge(vs.h, post_t[j - 2])
                t.wait_ge(ss.h, qe_t[j - 2])
            pl = ps_lg[j % 2][:, 0:24]
            for c in range(DC):
                t.matmul(pl, lnxT[j % 2][:, c, :],
                         wallT_sb[:, c * 24:(c + 1) * 24],
                         start=(c == 0), stop=False)
            ts.tick(t.matmul(pl, ones_bf[:, 0:128], bases_sb[:],
                             start=False, stop=True))
        lg_t[j] = ts.n

        @blk.vector
        def _(v, j=j):
            v.wait_ge(ts.h, lg_t[j])
            vs.tick(v.tensor_scalar(out=Z[:, j * 16:(j + 1) * 16],
                                    in0=ps_lg[j % 2][:, 0:16],
                                    scalar1=10.0, scalar2=None, op0=ALU.mult))
        z_t[j] = vs.n

        @blk.vector
        def _(v, j=j):
            v.wait_ge(ts.h, lg_t[j])
            if j >= 1:
                v.wait_ge(ss.h, pe_t[j - 1])
            vs.tick(v.tensor_reduce(smax1[:], ps_lg[j % 2][:, 16:20],
                                    axis=AX.X, op=ALU.max, negate=True))
        pm = vs.n

        @blk.scalar
        def _(s, j=j, pm=pm):
            s.wait_ge(vs.h, pm)
            if j >= 1:
                s.wait_ge(vs.h, pre_t[j - 1])
            ss.tick(s.activation(out=er4a[:], in_=ps_lg[j % 2][:, 16:20],
                                 func=AF.Exp, bias=smax1[:], accum_out=ssum1[:]))
        pe_t[j] = ss.n

        @blk.vector
        def _(v, j=j, pe=pe_t[j]):
            v.wait_ge(ss.h, pe)
            vs.tick(v.reciprocal(rr1[:], ssum1[:]))
            v.wait_ge(vs.h, vs.n)
            vs.tick(v.tensor_scalar(out=Hpre[:, j * 4:(j + 1) * 4], in0=er4a[:],
                                    scalar1=rr1[:], scalar2=None, op0=ALU.mult))
        pre_t[j] = vs.n

        @blk.vector
        def _(v, j=j):
            v.wait_ge(ts.h, lg_t[j])
            if j >= 1:
                v.wait_ge(ss.h, qe_t[j - 1])
            vs.tick(v.tensor_reduce(smax2[:], ps_lg[j % 2][:, 20:24],
                                    axis=AX.X, op=ALU.max, negate=True))
        qm = vs.n

        @blk.scalar
        def _(s, j=j, qm=qm):
            s.wait_ge(vs.h, qm)
            if j >= 1:
                s.wait_ge(vs.h, post_t[j - 1])
            ss.tick(s.activation(out=er4b[:], in_=ps_lg[j % 2][:, 20:24],
                                 func=AF.Exp, bias=smax2[:], accum_out=ssum2[:]))
        qe_t[j] = ss.n

        @blk.vector
        def _(v, j=j):
            v.wait_ge(ss.h, qe_t[j])
            vs.tick(v.reciprocal(rr2[:], ssum2[:]))
            v.wait_ge(vs.h, vs.n)
            vs.tick(v.tensor_scalar(out=Hpost[:, j * 4:(j + 1) * 4], in0=er4b[:],
                                    scalar1=rr2[:], scalar2=0.5,
                                    op0=ALU.mult, op1=ALU.mult))
        post_t[j] = vs.n

        @blk.vector
        def _(v, j=j):
            v.wait_ge(vs.h, pre_t[j])
            if j >= 1:
                v.wait_ge(tr.h, trX[j - 1])
            vs.tick(v.tensor_scalar(out=xpre[:], in0=lnf[j % 2][:, 0:1024],
                                    scalar1=Hpre[:, j * 4:j * 4 + 1],
                                    scalar2=None, op0=ALU.mult))
            for n in range(1, 4):
                v.wait_ge(vs.h, vs.n)
                out_ap = xpre_bf[:] if n == 3 else xpre[:]
                vs.tick(v.scalar_tensor_tensor(
                    out=out_ap, in0=lnf[j % 2][:, n * 1024:(n + 1) * 1024],
                    scalar=Hpre[:, j * 4 + n:j * 4 + n + 1], in1=xpre[:],
                    op0=ALU.mult, op1=ALU.add))
        xpre_t[j] = vs.n
        xlast_t[j] = vs.n

        trX[j] = emit_xbar(p, xpT[:, :, j, :], xpre_bf[:], [(vs, xpre_t[j])])

        if j + 2 < NT:
            @blk.gpsimd
            def _(g, j=j):
                x_ld[j + 2] = xsl.issue(g, j + 2, xr[j % 2][:],
                                        x_h[(j + 2) * 128:(j + 3) * 128, :],
                                        pre_deps=[(vs, lnf_t[j])])

        for oc in range(2):
            @blk.tensor
            def _(t, j=j, oc=oc):
                t.wait_ge(tr.h, trX[j])
                t.wait_ge(ldw.h, w_ld)
                if j >= 1:
                    t.wait_ge(vs.h, yev_t[j - 1][oc])
                for c in range(8):
                    mm = t.matmul(ps_y[oc][:], xpT[:, c, j, :],
                                  wb_sb[:, c * 1024 + oc * 512:c * 1024 + (oc + 1) * 512],
                                  start=(c == 0), stop=(c == 7))
                ts.tick(mm)
            y_t[j][oc] = ts.n

            @blk.vector
            def _(v, j=j, oc=oc):
                v.wait_ge(ts.h, y_t[j][oc])
                vs.tick(v.tensor_copy(
                    y_all[:, j * 1024 + oc * 512:j * 1024 + (oc + 1) * 512],
                    ps_y[oc][:]))
            yev_t[j][oc] = vs.n

    Z4 = Z[:].rearrange("p (t i j) -> p t i j", t=NT, i=4)
    M14 = M1[:].rearrange("p (t i j) -> p t i j", t=NT, i=4)
    M24 = M2[:].rearrange("p (t i j) -> p t i j", t=NT, i=4)
    M2p = M2[:].rearrange("p (t i j) -> p t j i", t=NT, i=4)
    mx3 = mx[:].rearrange("p (t i) -> p t i", t=NT)
    sg3 = sgum[:].rearrange("p (t i) -> p t i", t=NT)
    uu3 = uu[:].rearrange("p (t i) -> p t i", t=NT)
    vv3 = vv[:].rearrange("p (t i) -> p t i", t=NT)
    BCAST = [128, NT, 4, 4]

    sink_start = [(vs, max(z_t)), (vs, init_t)]

    for it in range(50):
        @blk.vector
        def _(v, it=it):
            if it == 0:
                waits(v, sink_start)
            else:
                v.wait_ge(vs.h, vs.n)
            vs.tick(v.tensor_tensor(M14, Z4,
                                    vv3.unsqueeze(2).broadcast_to(BCAST),
                                    op=ALU.add))
            v.wait_ge(vs.h, vs.n)
            vs.tick(v.tensor_reduce(mx3, M14, axis=AX.X, op=ALU.max))
            v.wait_ge(vs.h, vs.n)
            vs.tick(v.tensor_tensor(M14, M14,
                                    mx3.unsqueeze(-1).broadcast_to(BCAST),
                                    op=ALU.subtract))
        m1_t = vs.n

        @blk.scalar
        def _(s, m1_t=m1_t):
            s.wait_ge(vs.h, m1_t)
            ss.tick(s.activation(out=M1[:], in_=M1[:], func=AF.Exp))
        e1_t = ss.n

        @blk.vector
        def _(v, e1_t=e1_t):
            v.wait_ge(ss.h, e1_t)
            vs.tick(v.tensor_reduce(sg3, M14, axis=AX.X, op=ALU.add))
        s1_t = vs.n

        @blk.scalar
        def _(s, s1_t=s1_t):
            s.wait_ge(vs.h, s1_t)
            ss.tick(s.activation(out=sgum[:], in_=sgum[:], func=AF.Ln))
        l1_t = ss.n

        @blk.vector
        def _(v, l1_t=l1_t):
            v.wait_ge(ss.h, l1_t)
            vs.tick(v.scalar_tensor_tensor(out=uu[:], in0=sgum[:], scalar=-1.0,
                                           in1=mx[:], op0=ALU.mult,
                                           op1=ALU.subtract))
            v.wait_ge(vs.h, vs.n)
            vs.tick(v.tensor_tensor(M24, Z4,
                                    uu3.unsqueeze(-1).broadcast_to(BCAST),
                                    op=ALU.add))
            v.wait_ge(vs.h, vs.n)
            vs.tick(v.tensor_reduce(mx3, M2p, axis=AX.X, op=ALU.max))
            v.wait_ge(vs.h, vs.n)
            vs.tick(v.tensor_tensor(M2p, M2p,
                                    mx3.unsqueeze(-1).broadcast_to(BCAST),
                                    op=ALU.subtract))
        m2_t = vs.n

        @blk.scalar
        def _(s, m2_t=m2_t):
            s.wait_ge(vs.h, m2_t)
            ss.tick(s.activation(out=M2[:], in_=M2[:], func=AF.Exp))
        e2_t = ss.n

        @blk.vector
        def _(v, e2_t=e2_t):
            v.wait_ge(ss.h, e2_t)
            vs.tick(v.tensor_reduce(sg3, M2p, axis=AX.X, op=ALU.add))
        s2_t = vs.n

        @blk.scalar
        def _(s, s2_t=s2_t):
            s.wait_ge(vs.h, s2_t)
            ss.tick(s.activation(out=sgum[:], in_=sgum[:], func=AF.Ln))
        l2_t = ss.n

        @blk.vector
        def _(v, l2_t=l2_t):
            v.wait_ge(ss.h, l2_t)
            vs.tick(v.scalar_tensor_tensor(out=vv[:], in0=sgum[:], scalar=-1.0,
                                           in1=mx[:], op0=ALU.mult,
                                           op1=ALU.subtract))
    @blk.vector
    def _(v):
        v.wait_ge(vs.h, vs.n)
        vs.tick(v.tensor_tensor(M14, Z4, uu3.unsqueeze(-1).broadcast_to(BCAST),
                                op=ALU.add))
        v.wait_ge(vs.h, vs.n)
        vs.tick(v.tensor_tensor(M14, M14, vv3.unsqueeze(2).broadcast_to(BCAST),
                                op=ALU.add))
    hsum_t = vs.n

    @blk.scalar
    def _(s):
        s.wait_ge(vs.h, hsum_t)
        ss.tick(s.activation(out=M1[:], in_=M1[:], func=AF.Exp))
    he_t = ss.n

    @blk.vector
    def _(v):
        v.wait_ge(ss.h, he_t)
        v.wait_ge(vs.h, g_rep)
        vs.tick(v.tensor_scalar(out=M1[:], in0=M1[:], scalar1=grep[:, 0:1],
                                scalar2=None, op0=ALU.mult))
        v.wait_ge(vs.h, vs.n)
        M116 = M1[:].rearrange("p (t d) -> p t d", t=NT)
        last = None
        for k in (0, 5, 10, 15):
            last = v.tensor_scalar(out=M116[:, :, k:k + 1],
                                   in0=M116[:, :, k:k + 1],
                                   scalar1=grep[:, 1:2], scalar2=None,
                                   op0=ALU.add)
        vs.tick(last)
    H_t = vs.n

    x3_ld = [None] * NT
    mix_t = [0] * NT

    @blk.gpsimd
    def _(g):
        x3_ld[0] = xsl3.issue(g, 0, xr[0][:], x_h[0:128, :],
                              pre_deps=[(vs, lnf_t[NT - 2])])
        x3_ld[1] = xsl3.issue(g, 1, xr[1][:], x_h[128:256, :],
                              pre_deps=[(vs, lnf_t[NT - 1])])

    st_t = [None] * NT
    ln3_t = [0] * NT
    for j in range(NT):
        v_deps = [x3_ld[j], (vs, H_t)]
        s_deps = [x3_ld[j], (vs, H_t)]
        lnf_deps = []
        if j >= 2:
            lnf_deps.append((vs, mix_t[j - 2]))
        ln3_t[j] = emit_norm_tile(p, xr[j % 2][:], lnf[j % 2][:], bufs,
                                  v_deps, s_deps, lnf_deps)

        @blk.vector
        def _(v, j=j):
            v.wait_ge(vs.h, ln3_t[j])
            v.wait_ge(vs.h, max(yev_t[j]))
            for i in range(4):
                v.wait_ge(vs.h, vs.n)
                vs.tick(v.tensor_scalar(out=acc[:],
                                        in0=lnf[j % 2][:, 0:1024],
                                        scalar1=M1[:, j * 16 + i * 4:j * 16 + i * 4 + 1],
                                        scalar2=None, op0=ALU.mult))
                for n in range(1, 4):
                    v.wait_ge(vs.h, vs.n)
                    vs.tick(v.scalar_tensor_tensor(
                        out=acc[:], in0=lnf[j % 2][:, n * 1024:(n + 1) * 1024],
                        scalar=M1[:, j * 16 + i * 4 + n:j * 16 + i * 4 + n + 1],
                        in1=acc[:], op0=ALU.mult, op1=ALU.add))
                v.wait_ge(vs.h, vs.n)
                vs.tick(v.scalar_tensor_tensor(
                    out=acc[:], in0=y_all[:, j * 1024:(j + 1) * 1024],
                    scalar=Hpost[:, j * 4 + i:j * 4 + i + 1],
                    in1=acc[:], op0=ALU.mult, op1=ALU.add))
                v.wait_ge(vs.h, vs.n)
                xa_i = xr[j % 2][:, i * 1024:(i + 1) * 1024]
                vs.tick(v.tensor_tensor(xa_i, xa_i, acc[:], op=ALU.add))
        mix_t[j] = vs.n

        @blk.scalar
        def _(s_, j=j):
            st_t[j] = ysl.issue(s_, j, y_h[j * 128:(j + 1) * 128, :],
                                xr[j % 2][:], pre_deps=[(vs, mix_t[j])])

        if j + 2 < NT:
            @blk.gpsimd
            def _(g, j=j):
                x3_ld[j + 2] = xsl3.issue(g, j + 2, xr[j % 2][:],
                                          x_h[(j + 2) * 128:(j + 3) * 128, :],
                                          pre_deps=[st_t[j]])

    if ctx is None:
        @blk.scalar
        def _(g):
            for s_ in ysl.sems:
                if s_.n:
                    g.wait_ge(s_.h, s_.n)
        p.phase_close()
        return p.finish()
    ctx["stores"] = [(s_, s_.n) for s_ in ysl.sems if s_.n]
    return



QKV_SS = 256
NSTRIP = 12288 // QKV_SS


def build_qkv0(ctx=None):
    if ctx is None:
        p = Prog()
        nc = p.nc
        x_h = nc.dram_tensor("x", [TOK, D], F32, kind="ExternalInput").ap()
        wqkv_h = nc.dram_tensor("wqkv", [128, 96, DC + 1, 128], F32,
                                kind="ExternalInput").ap()
        gq_h = nc.dram_tensor("gq", [128, 1], F32, kind="ExternalInput").ap()
        gk_h = nc.dram_tensor("gk", [128, 1], F32, kind="ExternalInput").ap()
        qT_o = nc.dram_tensor("qT", [32, 128, TOK], BF16,
                              kind="ExternalOutput").ap()
        kT_o = nc.dram_tensor("kT", [32, 128, TOK], BF16,
                              kind="ExternalOutput").ap()
        v_o = nc.dram_tensor("v", [TOK, D], BF16, kind="ExternalOutput").ap()
        ngroups = 1
        p.start_block()
        emit_act_table_pin(p)
        p.phase_open()
    else:
        p = ctx["p"]
        x_h, wqkv_h = ctx["x"], ctx["wqkv"]
        gq_h, gk_h = ctx["gq"], ctx["gk"]
        qT_o, kT_o, v_o = ctx["qT"], ctx["kT"], ctx["v"]
        ngroups = ctx.get("ngroups", 1)

    ntile = 8

    xr = [p.sbuf(f"xr{i}", [128, D], F32) for i in range(2)]
    lnx = [p.sbuf(f"lnx{i}", [128, D], BF16) for i in range(2)]
    lnxT = p.sbuf("lnxT_full", [128, DC, 1024], BF16)
    sq = p.sbuf("sq", [128, D], BF16)
    stats = p.sbuf("stats", [128, 4], F32)
    gktmp = p.sbuf("gktmp", [128, 1], F32)
    eps_ln = p.sbuf("eps_ln", [128, 1], F32)
    eps_qk = p.sbuf("eps_qk", [128, 1], F32)
    ones_bf = p.sbuf("ones_bf", [1, 512], BF16)
    ones_col = p.sbuf("ones_col", [128, 1], BF16)
    ones128 = p.sbuf("ones128", [128, 128], BF16)
    ones_f32 = p.sbuf("ones_f32", [1, 128], F32)
    gqks = p.sbuf("gqks", [128, 1], F32)
    one_col_f = p.sbuf("one_col_f", [128, 1], F32)
    wcol = [p.sbuf(f"wcol{i}", [128, DC + 1, 128], BF16) for i in range(2)]
    q2f = [p.sbuf(f"q2f{i}", [128, 1024], BF16) for i in range(2)]
    rstd = [p.sbuf(f"rstd{i}", [128, 1024], F32) for i in range(2)]
    ostage = [p.sbuf(f"ostage{i}", [128, 1024], BF16) for i in range(2)]
    vostage = [p.sbuf(f"vostage{i}", [128, 1024], BF16) for i in range(2)]
    vstage = [p.sbuf(f"vstage{i}", [128, ntile, 128], BF16) for i in range(2)]

    ps_fm = [p.psum(f"ps_fm{i}", [128, 1024]) for i in range(2)]
    ps_n = [p.psum(f"ps_n{i}", [128, 1024]) for i in range(2)]

    blk = p.block
    ldw = p.sem("ldw")
    tr = p.sem("tr")
    ts = p.sem("ts")
    vs, ss = p.sem("vs"), p.sem("ss")
    xsl = DmaSlots(p, "xld", 2)
    wsl = DmaSlots(p, "wld", 2)
    osl = DmaSlots(p, "osl", 4)

    @blk.gpsimd
    def _(g):
        ldw.tick(g.dma_start(out=gqks[:], in_=gq_h), 16)
        ldw.tick(g.dma_start(out=gktmp[:], in_=gk_h), 16)
    w_ld = ldw.n

    @blk.vector
    def _(v):
        v.memset(ones_bf[:], 1.0)
        v.memset(ones_col[:], 1.0)
        v.memset(ones128[:], 1.0)
        v.memset(ones_f32[:], 1.0)
        v.memset(one_col_f[:], 1.0)
        v.memset(eps_ln[:], LN_EPS)
        vs.tick(v.memset(eps_qk[:], QK_EPS))
        v.wait_ge(ldw.h, w_ld)
        vs.tick(v.tensor_tensor(gqks[:], gqks[:], gktmp[:], op=ALU.mult))
        v.wait_ge(vs.h, vs.n)
        vs.tick(v.tensor_scalar(out=gqks[:], in0=gqks[:], scalar1=INV_SQRT_HD,
                                scalar2=None, op0=ALU.mult))
    init_t = vs.n

    bufs = dict(stats=stats, sq=sq, eps_ln=eps_ln)

    apply_prev = [0]
    w_ld_t = [None, None]
    trL_hist = []
    gi = [0]
    mm_hist = {}
    ev_hist = {}
    nmm_hist = {}
    rstd_hist = {}
    rep_hist = {}
    repv_hist = {}
    sq_prev = [0]
    last_ost = [None, None]
    last_vxb = [None, None]
    last_vst = [None, None]
    x_ld_prev = [None, None]
    last_group_mm = [0]

    for grp in range(ngroups):
        trL = [0] * ntile
        apply_t = [0] * ntile
        x_ld = [None] * ntile

        @blk.gpsimd
        def _(g, grp=grp):
            for jj in range(2):
                jg = grp * ntile + jj
                pre = []
                if x_ld_prev[jj] is not None:
                    pre = [(vs, apply_prev[0])]
                x_ld[jj] = xsl.issue(g, jj, xr[jj][:],
                                     x_h[jg * 128:(jg + 1) * 128, :],
                                     pre_deps=pre)
                x_ld_prev[jj] = x_ld[jj]

        for j in range(ntile):
            jg = grp * ntile + j
            v_deps = [x_ld[j], (vs, init_t)]
            s_deps = [x_ld[j]]
            if j >= 1 or grp >= 1:
                v_deps.append((vs, apply_prev[0]))
                s_deps.append((vs, apply_prev[0]))
            if grp >= 1 and j == 0:
                s_deps.append((vs, sq_prev[0]))
                s_deps.append((ss, rstd_hist.get(gi[0] - 1, 0)))
            apply_deps = []
            if len(trL_hist) >= 2:
                apply_deps.append((tr, trL_hist[-2]))
            apply_t[j] = emit_norm_tile(p, xr[j % 2][:], lnx[j % 2][:], bufs,
                                        v_deps, s_deps, apply_deps)
            apply_prev[0] = apply_t[j]
            pre = [(vs, apply_t[j])]
            if grp >= 1 and j == 0:
                pre.append((ts, last_group_mm[0]))
            trL[j] = emit_xbar(p, lnxT[:, :, j * 128:(j + 1) * 128],
                               lnx[j % 2][:], pre)
            trL_hist.append(trL[j])
            if j + 2 < ntile:
                @blk.gpsimd
                def _(g, j=j, grp=grp):
                    jg2 = grp * ntile + j + 2
                    x_ld[j + 2] = xsl.issue(g, j + 2, xr[j % 2][:],
                                            x_h[jg2 * 128:(jg2 + 1) * 128, :],
                                            pre_deps=[(vs, apply_t[j])])

        heavies = (list(range(0, 32)) + list(range(32, 64))) if grp == 0 \
            else list(range(32, 64))
        lights = list(range(64, 96))
        oc_list = []
        hi = vi = 0
        while hi < len(heavies) or vi < len(lights):
            if hi < len(heavies):
                oc_list.append(heavies[hi]); hi += 1
            if vi < len(lights):
                oc_list.append(lights[vi]); vi += 1
            if hi < len(heavies):
                oc_list.append(heavies[hi]); hi += 1
        for oc in oc_list:
            kind = "q" if oc < 32 else ("k" if oc < 64 else "v")
            g_i = gi[0]
            gi[0] += 1

            @blk.gpsimd
            def _(g, oc=oc, g_i=g_i):
                w_ld_t[g_i % 2] = wsl.issue(
                    g, g_i, wcol[g_i % 2][:], wqkv_h[:, oc],
                    pre_deps=[(ts, mm_hist.get(g_i - 2, 0))])

            @blk.tensor
            def _(t, oc=oc, g_i=g_i, grp=grp):
                waits(t, [w_ld_t[g_i % 2], (tr, trL[ntile - 1])])
                if g_i >= 2:
                    t.wait_ge(vs.h, ev_hist.get(g_i - 2, 0))
                pb = ps_fm[g_i % 2]
                for c in range(DC):
                    t.matmul(pb[:, 0:512], wcol[g_i % 2][:, c, :],
                             lnxT[:, c, 0:512], start=(c == 0), stop=False)
                    t.matmul(pb[:, 512:1024], wcol[g_i % 2][:, c, :],
                             lnxT[:, c, 512:1024], start=(c == 0), stop=False)
                t.matmul(pb[:, 0:512], wcol[g_i % 2][0:1, DC, :],
                         ones_bf[:, 0:512], start=False, stop=True)
                ts.tick(t.matmul(pb[:, 512:1024], wcol[g_i % 2][0:1, DC, :],
                                 ones_bf[:, 0:512], start=False, stop=True))
            mm_hist[g_i] = ts.n
            last_group_mm[0] = ts.n

            if kind in ("q", "k"):
                @blk.scalar
                def _(s, g_i=g_i):
                    s.wait_ge(ts.h, mm_hist[g_i])
                    if g_i >= 2:
                        s.wait_ge(ts.h, nmm_hist.get(g_i - 2, 0))
                    ss.tick(s.activation(out=q2f[g_i % 2][:],
                                         in_=ps_fm[g_i % 2][:],
                                         func=AF.Square))
                sq_hist = ss.n
                sq_prev[0] = ss.n

                @blk.tensor
                def _(t, g_i=g_i, sq_hist=sq_hist):
                    t.wait_ge(ss.h, sq_hist)
                    if g_i >= 2:
                        t.wait_ge(ss.h, rstd_hist.get(g_i - 2, 0))
                    pn = ps_n[g_i % 2]
                    t.matmul(pn[:, 0:512], ones128[:],
                             q2f[g_i % 2][:, 0:512], start=True, stop=True)
                    ts.tick(t.matmul(pn[:, 512:1024], ones128[:],
                                     q2f[g_i % 2][:, 512:1024],
                                     start=True, stop=True))
                nmm_hist[g_i] = ts.n

                @blk.scalar
                def _(s, g_i=g_i):
                    s.wait_ge(ts.h, nmm_hist[g_i])
                    if g_i >= 2:
                        s.wait_ge(vs.h, ev_hist.get(g_i - 2, 0))
                    ss.tick(s.activation(out=rstd[g_i % 2][:],
                                          in_=ps_n[g_i % 2][:],
                                          func=AF.Ln, bias=eps_qk[:],
                                          scale=1.0 / 128))
                    s.wait_ge(ss.h, ss.n)
                    ss.tick(s.activation(out=rstd[g_i % 2][:],
                                         in_=rstd[g_i % 2][:],
                                         func=AF.Exp, scale=-0.5))
                rstd_hist[g_i] = ss.n

                @blk.vector
                def _(v, g_i=g_i, kind=kind):
                    v.wait_ge(ss.h, rstd_hist[g_i])
                    if last_ost[g_i % 2] is not None:
                        waits(v, [last_ost[g_i % 2]])
                    vs.tick(v.scalar_tensor_tensor(
                        out=ostage[g_i % 2][:], in0=ps_fm[g_i % 2][:],
                        scalar=(gqks[:] if kind == "k" else one_col_f[:]),
                        in1=rstd[g_i % 2][:], op0=ALU.mult, op1=ALU.mult))
                ev_hist[g_i] = vs.n

                @blk.scalar
                def _(s_, oc=oc, g_i=g_i, kind=kind, grp=grp):
                    dst = qT_o if kind == "q" else kT_o
                    hh = oc if kind == "q" else oc - 32
                    last_ost[g_i % 2] = osl.issue(
                        s_, g_i, dst[hh][:, grp * 1024:(grp + 1) * 1024],
                        ostage[g_i % 2][:],
                        pre_deps=[(vs, ev_hist[g_i])])
            else:
                @blk.vector
                def _(v, g_i=g_i):
                    v.wait_ge(ts.h, mm_hist[g_i])
                    if last_vxb[g_i % 2] is not None:
                        waits(v, [last_vxb[g_i % 2]])
                    vs.tick(v.tensor_copy(vostage[g_i % 2][:],
                                          ps_fm[g_i % 2][:]))
                ev_hist[g_i] = vs.n

                pre = [(vs, ev_hist[g_i])]
                if last_vst[g_i % 2] is not None:
                    pre.append(last_vst[g_i % 2])
                tr_n = emit_xbar(p, vstage[g_i % 2][:], vostage[g_i % 2][:], pre)
                last_vxb[g_i % 2] = (tr, tr_n)

                @blk.scalar
                def _(s_, oc=oc, g_i=g_i, tr_n=tr_n, grp=grp):
                    s_.wait_ge(tr.h, tr_n)
                    vd = (oc - 64) * 128
                    r0 = grp * 1024
                    last_vst[g_i % 2] = osl.issue(
                        s_, g_i, v_o[r0:r0 + 1024, vd:vd + 128].rearrange(
                            "(c p) d -> p c d", p=128),
                        vstage[g_i % 2][:])

    if ctx is None:
        @blk.gpsimd
        def _(g):
            for s_ in osl.sems:
                if s_.n:
                    g.wait_ge(s_.h, s_.n)
        p.phase_close()
        return p.finish()
    ctx["stores"] = [(s_, s_.n) for s_ in osl.sems if s_.n]
    return



def emit_cbound_kmax(p, kh_ap, k2, cb, ones_col, ps_misc, Tlen, deps, war_deps):
    blk = p.block
    vs, ss, ts = p.sem("vs"), p.sem("ss"), p.sem("ts")
    nch = Tlen // 512

    @blk.scalar
    def _(s):
        waits(s, deps + war_deps)
        ss.tick(s.activation(out=k2[:, 0:Tlen], in_=kh_ap, func=AF.Square))
    sq_t = ss.n

    red_ts = []
    for ch in range(nch):
        @blk.tensor
        def _(t, ch=ch):
            t.wait_ge(ss.h, sq_t)
            if ch == 0:
                waits(t, war_deps)
            else:
                t.wait_ge(vs.h, red_ts[-1])
            ts.tick(t.matmul(ps_misc[0:1, 0:512], ones_col[:],
                             k2[:, ch * 512:(ch + 1) * 512],
                             start=True, stop=True))
        mmn = ts.n

        @blk.vector
        def _(v, ch=ch, mmn=mmn):
            v.wait_ge(ts.h, mmn)
            vs.tick(v.tensor_reduce(cb[0:1, ch:ch + 1], ps_misc[0:1, 0:512],
                                    axis=AX.X, op=ALU.max))
        red_ts.append(vs.n)

    @blk.vector
    def _(v):
        v.wait_ge(vs.h, red_ts[-1])
        vs.tick(v.tensor_reduce(cb[0:1, 13:14], cb[0:1, 0:nch],
                                axis=AX.X, op=ALU.max))
    return vs.n


def emit_cbound_q(p, qh_ap, q2, cb, ones_col, ps_misc, lnsc, ones_f32,
                  negC_rep, deps, war_deps, kmax_t):
    blk = p.block
    vs, ss, ts = p.sem("vs"), p.sem("ss"), p.sem("ts")

    @blk.scalar
    def _(s):
        waits(s, deps + war_deps)
        ss.tick(s.activation(out=q2[:], in_=qh_ap, func=AF.Square))
    sq_t = ss.n

    red_ts = []
    for ch in range(2):
        @blk.tensor
        def _(t, ch=ch):
            t.wait_ge(ss.h, sq_t)
            if ch == 0:
                waits(t, war_deps)
                t.wait_ge(vs.h, kmax_t)
            else:
                t.wait_ge(vs.h, red_ts[-1])
            ts.tick(t.matmul(ps_misc[0:1, 0:512], ones_col[:],
                             q2[:, ch * 512:(ch + 1) * 512],
                             start=True, stop=True))
        mmn = ts.n

        @blk.vector
        def _(v, ch=ch, mmn=mmn):
            v.wait_ge(ts.h, mmn)
            vs.tick(v.tensor_reduce(cb[0:1, 10 + ch:11 + ch],
                                    ps_misc[0:1, 0:512], axis=AX.X, op=ALU.max))
        red_ts.append(vs.n)

    @blk.vector
    def _(v):
        v.wait_ge(vs.h, red_ts[-1])
        vs.tick(v.tensor_reduce(cb[0:1, 12:13], cb[0:1, 10:12],
                                axis=AX.X, op=ALU.max))
        v.wait_ge(vs.h, vs.n)
        v.wait_ge(vs.h, kmax_t)
        vs.tick(v.tensor_tensor(cb[0:1, 14:15], cb[0:1, 12:13], cb[0:1, 13:14],
                                op=ALU.mult))
    m2_t = vs.n

    @blk.scalar
    def _(s):
        s.wait_ge(vs.h, m2_t)
        ss.tick(s.activation(out=cb[0:1, 15:16], in_=cb[0:1, 14:15], func=AF.Ln))
        s.wait_ge(ss.h, ss.n)
        ss.tick(s.activation(out=cb[0:1, 15:16], in_=cb[0:1, 15:16],
                             func=AF.Exp, scale=0.5, bias=lnsc[:]))
    c_t = ss.n

    @blk.vector
    def _(v):
        v.wait_ge(ss.h, c_t)
        vs.tick(v.tensor_scalar(out=cb[0:1, 15:16], in0=cb[0:1, 15:16],
                                scalar1=-C_MARGIN, scalar2=0.0,
                                op0=ALU.add, op1=ALU.max))
        v.wait_ge(vs.h, vs.n)
        vs.tick(v.tensor_scalar(out=cb[0:1, 15:16], in0=cb[0:1, 15:16],
                                scalar1=-1.0, scalar2=None, op0=ALU.mult))
    neg_t = vs.n

    @blk.tensor
    def _(t):
        t.wait_ge(vs.h, neg_t)
        ts.tick(t.matmul(ps_misc[:, 0:1], ones_f32[:], cb[0:1, 15:16],
                         start=True, stop=True))
    rep_mm = ts.n

    @blk.vector
    def _(v):
        v.wait_ge(ts.h, rep_mm)
        waits(v, war_deps)
        vs.tick(v.tensor_copy(negC_rep[:], ps_misc[:, 0:1]))
    return vs.n, rep_mm


def emit_head(p, b, qh_full, kh_sb, vh_sb, nTT, exp_scale, negC_t, ctx_dsts,
              q_deps, k_deps, v_deps, ctx_war=()):
    blk = p.block
    vs, ss, ts = p.sem("vs"), p.sem("ss"), p.sem("ts")
    ps_s, ps_ctx, ps_l = b["ps_s2"], b["ps_ctx2"], b["ps_l2"]
    pT = b["pT2"]
    st = b["state"]

    s_t = [0] * nTT
    exp_t = [0] * nTT
    av_t = [0] * nTT

    def emit_S(tt):
        @blk.tensor
        def _(t, tt=tt):
            if tt == 0:
                waits(t, q_deps + k_deps)
            if tt >= 2:
                t.wait_ge(ss.h, exp_t[tt - 2])
            elif st["last_exp"]:
                t.wait_ge(ss.h, st["last_exp"])
            kT_tile = kh_sb[:, tt * 128:(tt + 1) * 128]
            t.matmul(ps_s[tt % 2][:, 0:512], kT_tile, qh_full[:, 0:512],
                     start=True, stop=True)
            ts.tick(t.matmul(ps_s[tt % 2][:, 512:1024], kT_tile,
                             qh_full[:, 512:1024], start=True, stop=True))
        s_t[tt] = ts.n

        @blk.scalar
        def _(s, tt=tt):
            s.wait_ge(ts.h, s_t[tt])
            s.wait_ge(vs.h, negC_t)
            if tt >= 2:
                s.wait_ge(ts.h, av_t[tt - 2])
            elif st["last_av"]:
                s.wait_ge(ts.h, st["last_av"])
            ss.tick(s.activation(out=pT[tt % 2][:], in_=ps_s[tt % 2][:],
                                 func=AF.Exp, bias=b["negC_rep"][:],
                                 scale=exp_scale))
        exp_t[tt] = ss.n

    def emit_AV(tt):
        @blk.tensor
        def _(t, tt=tt):
            t.wait_ge(ss.h, exp_t[tt])
            if tt == 0:
                waits(t, v_deps)
                if st["last_ctxev"]:
                    t.wait_ge(vs.h, st["last_ctxev"])
                if st["last_lev"]:
                    t.wait_ge(vs.h, st["last_lev"])
            v_tile = vh_sb[:, tt * 128:(tt + 1) * 128]
            first, last = tt == 0, tt == nTT - 1
            t.matmul(ps_ctx[0][:, 0:512], v_tile, pT[tt % 2][:, 0:512],
                     start=first, stop=last)
            t.matmul(ps_ctx[1][:, 0:512], v_tile, pT[tt % 2][:, 512:1024],
                     start=first, stop=last)
            t.matmul(ps_l[0][0:1, 0:512], b["ones_col"][:], pT[tt % 2][:, 0:512],
                     start=first, stop=last)
            ts.tick(t.matmul(ps_l[1][0:1, 0:512], b["ones_col"][:],
                             pT[tt % 2][:, 512:1024], start=first, stop=last))
        av_t[tt] = ts.n

    for tt in range(nTT + 1):
        if tt < nTT:
            emit_S(tt)
        if tt >= 1:
            emit_AV(tt - 1)

    st["last_exp"] = exp_t[-1]
    st["last_av"] = av_t[-1]

    ctxev = []
    for qc in range(2):
        @blk.vector
        def _(v, qc=qc):
            v.wait_ge(ts.h, av_t[nTT - 1])
            vs.tick(v.tensor_scalar(out=b["lv"][0:1, :], in0=ps_l[qc][0:1, 0:512],
                                    scalar1=1e-30, scalar2=None, op0=ALU.max))
            v.wait_ge(vs.h, vs.n)
            vs.tick(v.reciprocal(b["linv"][0:1, :], b["lv"][0:1, :]))
        lev = vs.n
        st["last_lev"] = lev

        @blk.tensor
        def _(t, qc=qc, lev=lev):
            t.wait_ge(vs.h, lev)
            if st["last_rrep"]:
                t.wait_ge(vs.h, st["last_rrep"])
            ts.tick(t.matmul(ps_s[0][:, qc * 512:(qc + 1) * 512],
                             b["ones_f32"][:], b["linv"][0:1, :],
                             start=True, stop=True))
        rep_mm = ts.n

        @blk.vector
        def _(v, qc=qc, rep_mm=rep_mm):
            v.wait_ge(ts.h, rep_mm)
            vs.tick(v.tensor_copy(b["r_rep"][:], ps_s[0][:, qc * 512:(qc + 1) * 512]))
            v.wait_ge(vs.h, vs.n)
            waits(v, ctx_war)
            vs.tick(v.tensor_tensor(ctx_dsts[qc], ps_ctx[qc][:, 0:512],
                                    b["r_rep"][:], op=ALU.mult))
        st["last_rrep"] = vs.n - 1
        st["last_ctxev"] = vs.n
        ctxev.append(vs.n)
    return dict(ctxev=ctxev, last_ts=ts.n)


def emit_outproj(p, b, w_h, bias_sb, ctx_chunk_fn, x_h, y_h, has_bias,
                 phase_deps):
    blk = p.block
    nc = p.nc
    vs, ss, ts = p.sem("vs"), p.sem("ss"), p.sem("ts")
    stg = p.sem("stg")
    x_grp = b["x_grp"]
    wo_sb = b["wo_sb"]
    ps_o = b["ps_o"]
    ones_bf = b["ones_bf"]
    xgsl = DmaSlots(p, "xgsl", 2)
    wosl = DmaSlots(p, "wosl", 2)
    ysl = DmaSlots(p, "ysl", 4)

    strip_mm_done = {}
    ev_done = {}
    st_done = {}
    for g in range(2):
        x_ld = [None] * 4

        @blk.scalar
        def _(gq, g=g):
            for tl in range(4):
                j = g * 4 + tl
                pre = list(phase_deps) if (g == 0 and tl < 2) else []
                if g == 1:
                    pre.append(st_done[(0, tl)])
                x_ld[tl] = xgsl.issue(gq, tl % 2, x_grp[:, tl * D:(tl + 1) * D],
                                      x_h[j * 128:(j + 1) * 128, :],
                                      pre_deps=pre)
        get_lhsT, ctx_deps = ctx_chunk_fn(g, phase_deps)

        for oc in range(8):
            for cb4 in range(4):
                sid = (g * 8 + oc) * 4 + cb4

                @blk.gpsimd
                def _(gq, oc=oc, cb4=cb4, sid=sid):
                    pre = list(phase_deps) if sid < 2 else \
                        [(ts, strip_mm_done.get(sid - 2, 0))]
                    b["wo_ld"][sid] = wosl.issue(
                        gq, sid, wo_sb[sid % 2][:], w_h[:, cb4, oc],
                        pre_deps=pre)

                for tl in range(4):
                    @blk.tensor
                    def _(t, g=g, oc=oc, cb4=cb4, tl=tl, sid=sid):
                        waits(t, [b["wo_ld"][sid]] + ctx_deps)
                        if cb4 == 0:
                            prev = ev_done.get((g, tl, oc - 1)) if oc >= 1 else \
                                (ev_done.get((g - 1, tl, 7)) if g >= 1 else None)
                            if prev:
                                t.wait_ge(vs.h, prev)
                        mm = None
                        for c8 in range(8):
                            c = cb4 * 8 + c8
                            last = (cb4 == 3 and c8 == 7 and not has_bias)
                            mm = t.matmul(ps_o[tl][:, 0:512], get_lhsT(c, tl),
                                          wo_sb[sid % 2][:, c8, :],
                                          start=(cb4 == 0 and c8 == 0),
                                          stop=last)
                        if cb4 == 3 and has_bias:
                            mm = t.matmul(ps_o[tl][:, 0:512], ones_bf[:, 0:128],
                                          bias_sb[0:1, oc * 512:(oc + 1) * 512],
                                          start=False, stop=True)
                        ts.tick(mm)
                    strip_mm_done[sid] = ts.n
                    if cb4 == 3:
                        mmn = ts.n

                        @blk.vector
                        def _(v, g=g, oc=oc, tl=tl, mmn=mmn):
                            v.wait_ge(ts.h, mmn)
                            waits(v, [x_ld[tl]])
                            xa = x_grp[:, tl * D + oc * 512:tl * D + (oc + 1) * 512]
                            vs.tick(v.tensor_tensor(xa, xa, ps_o[tl][:, 0:512],
                                                    op=ALU.add))
                        ev_done[(g, tl, oc)] = vs.n

        for tl in range(4):
            @blk.scalar
            def _(gq, g=g, tl=tl):
                j = g * 4 + tl
                st_done[(g, tl)] = ysl.issue(
                    gq, j % 4, y_h[j * 128:(j + 1) * 128, :],
                    x_grp[:, tl * D:(tl + 1) * D],
                    pre_deps=[(vs, ev_done[(g, tl, 7)])])

    @blk.scalar
    def _(gq):
        for s_ in ysl.sems:
            if s_.n:
                gq.wait_ge(s_.h, s_.n)
    return [(s_, s_.n) for s_ in ysl.sems if s_.n]


def build_attn0(ctx=None):
    if ctx is None:
        p = Prog()
        nc = p.nc
        x_h = nc.dram_tensor("x", [TOK, D], F32, kind="ExternalInput").ap()
        qT_h = nc.dram_tensor("qT", [32, 128, TOK], BF16,
                              kind="ExternalInput").ap()
        kT_h = nc.dram_tensor("kT", [32, 128, 4096], BF16,
                              kind="ExternalInput").ap()
        vf_h = nc.dram_tensor("vf", [4096, D], BF16, kind="ExternalInput").ap()
        wo_h = nc.dram_tensor("wo", [128, 4, 8, 8, 512], F32,
                              kind="ExternalInput").ap()
        bo_h = nc.dram_tensor("bo", [1, D], F32, kind="ExternalInput").ap()
        y_h = nc.dram_tensor("y", [TOK, D], F32, kind="ExternalOutput").ap()
        kT_ag = None
        p.start_block()
        p.phase_open()
    else:
        p = ctx["p"]
        x_h, qT_h, vf_h = ctx["x"], ctx["qT"], ctx["vf"]
        kT_h = ctx.get("kT3")
        kT_ag = ctx.get("kT_ag")
        wo_h, bo_h, y_h = ctx["wo"], ctx["bo"], ctx["y"]

    ctxT = p.sbuf("ctxT", [128, 32 * TOK], BF16)
    ones_bf = p.sbuf("ones_bf", [1, 128], BF16)
    ones_col = p.sbuf("ones_col", [128, 1], BF16)
    ones_f32 = p.sbuf("ones_f32", [1, 128], F32)
    lnsc = p.sbuf("lnsc", [1, 1], F32)
    cb = p.sbuf("cb", [1, 16], F32)
    negC_rep = p.sbuf("negC_rep", [128, 1], F32)
    lv = p.sbuf("lv", [1, 512], F32)
    linv = p.sbuf("linv", [1, 512], F32)
    r_rep = p.sbuf("r_rep", [128, 512], F32)
    bo_sb = p.sbuf("bo_sb", [1, D], BF16)

    blk = p.block
    ldw = p.sem("ldw")
    vs, ss, ts = p.sem("vs"), p.sem("ss"), p.sem("ts")
    qsl = DmaSlots(p, "qsl", 2)
    ksl = DmaSlots(p, "ksl", 2)
    vsl = DmaSlots(p, "vsl", 2)

    @blk.gpsimd
    def _(g):
        ldw.tick(g.dma_start(out=bo_sb[:], in_=bo_h), 16)
    w_ld = ldw.n

    @blk.vector
    def _(v):
        v.memset(ones_bf[:], 1.0)
        v.memset(ones_col[:], 1.0)
        v.memset(ones_f32[:], 1.0)
        vs.tick(v.memset(lnsc[:], 0.0))
    init_t = vs.n

    p.phase_open()
    ps_s2 = [p.psum(f"ps_s{i}", [128, 1024]) for i in range(2)]
    ps_ctx2 = [p.psum(f"ps_ctx{q}", [128, 512]) for q in range(2)]
    ps_l2 = [p.psum(f"ps_l{q}", [128, 512]) for q in range(2)]
    qh_sb = [p.sbuf(f"qh{i}", [128, TOK], BF16) for i in range(2)]
    kh_sb = [p.sbuf(f"kh{i}", [128, 4096], BF16) for i in range(2)]
    vh_sb = [p.sbuf(f"vh{i}", [128, 32, 128], BF16) for i in range(2)]
    q2 = p.sbuf("q2", [128, TOK], BF16)
    k2 = p.sbuf("k2", [128, 4096], BF16)
    pT2 = [p.sbuf(f"pT{i}", [128, 1024], BF16) for i in range(2)]
    ps_misc = ps_s2[0]

    b = dict(ps_s2=ps_s2, ps_ctx2=ps_ctx2, ps_l2=ps_l2, pT2=pT2,
             ones_col=ones_col, ones_f32=ones_f32, negC_rep=negC_rep,
             lv=lv, linv=linv, r_rep=r_rep, ones_bf=ones_bf,
             state=dict(last_exp=0, last_av=0, last_ctxev=0, last_lev=0,
                        last_rrep=0))

    head_last_ts = [0] * 34
    head_last_ss = [0] * 34
    head_last_vs = [0] * 34
    ld_t = {}

    @blk.gpsimd
    def _(g):
        ld_t[("q", 0)] = qsl.issue(g, 0, qh_sb[0][:],
                                   qT_h[0])
        ld_t[("k", 0)] = ksl.issue(g, 0, kh_sb[0][:], kT_h[0])
        ld_t[("v", 0)] = vsl.issue(g, 0, vh_sb[0][:],
                                   vf_h[:, 0:128].rearrange(
                                       "(tt pp) d -> pp tt d", pp=128))

    for h in range(32):
        if h + 1 < 32:
            @blk.gpsimd
            def _(g, h=h):
                hp = h + 1
                pre = [(ts, head_last_ts[hp - 2])] if hp >= 2 else []
                ld_t[("q", hp)] = qsl.issue(g, hp, qh_sb[hp % 2][:], qT_h[hp],
                                            pre_deps=pre)
                ld_t[("k", hp)] = ksl.issue(g, hp, kh_sb[hp % 2][:], kT_h[hp],
                                            pre_deps=pre)
                ld_t[("v", hp)] = vsl.issue(
                    g, hp, vh_sb[hp % 2][:],
                    vf_h[:, hp * 128:(hp + 1) * 128].rearrange(
                        "(tt pp) d -> pp tt d", pp=128), pre_deps=pre)

        war = ([(ts, head_last_ts[h - 1]), (ss, head_last_ss[h - 1]),
                (vs, head_last_vs[h - 1])]
               if h >= 1 else [(vs, init_t)])
        kmax_t = emit_cbound_kmax(p, kh_sb[h % 2][:], k2, cb, ones_col,
                                  ps_misc, 4096, [ld_t[("k", h)]], war)
        negC_t, _mm = emit_cbound_q(p, qh_sb[h % 2][:], q2, cb, ones_col,
                                    ps_misc, lnsc, ones_f32, negC_rep,
                                    [ld_t[("q", h)]], war, kmax_t)
        emit_head(p, b, qh_sb[h % 2][:], kh_sb[h % 2][:],
                  vh_sb[h % 2][:].rearrange("p tt d -> p (tt d)"),
                  32, 1.0, negC_t,
                  [ctxT[:, h * TOK + qc * 512: h * TOK + (qc + 1) * 512]
                   for qc in range(2)],
                  [ld_t[("q", h)], (vs, negC_t)], [ld_t[("k", h)]],
                  [ld_t[("v", h)]])
        head_last_ts[h] = ts.n
        head_last_ss[h] = ss.n
        head_last_vs[h] = vs.n
    p.phase_close()

    p.phase_open()
    ps_o = [p.psum(f"ps_o{i}", [128, 512]) for i in range(4)]
    x_grp = p.sbuf("x_grp", [128, 4 * D], F32)
    wo_sb = [p.sbuf(f"wo_sb{i}", [128, 8, 512], BF16) for i in range(2)]
    b2 = dict(x_grp=x_grp, wo_sb=wo_sb, ps_o=ps_o, ones_bf=ones_bf,
              wo_ld=[None] * 64)
    phase_deps = [(ts, ts.n), (vs, vs.n), (ss, ss.n), (ldw, w_ld)]

    def ctx_chunk_fn(g, deps):
        def get_lhsT(c, tl):
            j = g * 4 + tl
            return ctxT[:, c * TOK + j * 128: c * TOK + (j + 1) * 128]
        return get_lhsT, [(vs, b["state"]["last_ctxev"])]

    ysl_st = emit_outproj(p, b2, wo_h, bo_sb, ctx_chunk_fn, x_h, y_h, True,
                          phase_deps)
    p.phase_close()
    if ctx is None:
        p.phase_close()
        return p.finish()
    ctx["stores"] = ysl_st
    return


WQ_SS = 256
NSTRIP_A = D // WQ_SS


def build_attn1(ctx=None):
    if ctx is None:
        p = Prog()
        nc = p.nc
        x_h = nc.dram_tensor("x", [TOK, D], F32, kind="ExternalInput").ap()
        wq_h = nc.dram_tensor("wq", [128, 32, DC, 128], F32,
                               kind="ExternalInput").ap()
        wo_h = nc.dram_tensor("wo", [128, 4, 8, 8, 512], F32,
                               kind="ExternalInput").ap()
        ktg_h = nc.dram_tensor("ktg", [8, 128, 8192], F32,
                               kind="ExternalInput").ap()
        vtg_h = nc.dram_tensor("vtg", [8, 8192, 128], F32,
                               kind="ExternalInput").ap()
        cos_h = nc.dram_tensor("cos", [64, TOK], F32, kind="ExternalInput").ap()
        sin_h = nc.dram_tensor("sin", [64, TOK], F32, kind="ExternalInput").ap()
        y_h = nc.dram_tensor("y", [TOK, D], F32, kind="ExternalOutput").ap()
        qT_int = nc.dram_tensor("qT_int", [32, 128, TOK], BF16).ap()
        ctx_int = nc.dram_tensor("ctx_int", [32, 128, TOK], BF16).ap()
        standalone = True
    else:
        p = ctx["p"]
        nc = p.nc
        x_h, y_h = ctx["x"], ctx["y"]
        wq_h, wo_h = ctx["wq"], ctx["wo"]
        ktg_h, vtg_h = ctx["ktg"], ctx["vtg"]
        cos_h, sin_h = ctx["cos"], ctx["sin"]
        qT_int, ctx_int = ctx["qT_int"], ctx["ctx_int"]
        standalone = False

    ones_bf = p.sbuf("ones_bf", [1, 128], BF16)
    ones_col = p.sbuf("ones_col", [128, 1], BF16)
    ones_f32 = p.sbuf("ones_f32", [1, 128], F32)
    lnsc = p.sbuf("lnsc", [1, 1], F32)
    cb = p.sbuf("cb", [1, 16], F32)
    negC_rep = p.sbuf("negC_rep", [128, 1], F32)
    lv = p.sbuf("lv", [1, 512], F32)
    linv = p.sbuf("linv", [1, 512], F32)
    r_rep = p.sbuf("r_rep", [128, 512], F32)
    stats = p.sbuf("stats", [128, 4], F32)
    eps_ln = p.sbuf("eps_ln", [128, 1], F32)
    cos_sb = p.sbuf("cos_sb", [64, TOK], F32)
    sin_sb = p.sbuf("sin_sb", [64, TOK], F32)

    blk = p.block if not standalone else p.start_block()
    if standalone:
        emit_act_table_pin(p)
        p.phase_open()
    ldw = p.sem("ldw")
    vs, ss, ts = p.sem("vs"), p.sem("ss"), p.sem("ts")
    tr = p.sem("tr")

    @blk.gpsimd
    def _(g):
        ldw.tick(g.dma_start(out=cos_sb[:], in_=cos_h), 16)
        ldw.tick(g.dma_start(out=sin_sb[:], in_=sin_h), 16)
    w_ld = ldw.n

    @blk.vector
    def _(v):
        v.memset(ones_bf[:], 1.0)
        v.memset(ones_col[:], 1.0)
        v.memset(ones_f32[:], 1.0)
        v.memset(eps_ln[:], LN_EPS)
        vs.tick(v.memset(lnsc[:], float(np.log(INV_SQRT_HD))))
    init_t = vs.n

    p.phase_open()
    xr = [p.sbuf(f"xr{i}", [128, D], F32) for i in range(2)]
    lnx = [p.sbuf(f"lnx{i}", [128, D], BF16) for i in range(2)]
    lnxT = p.sbuf("lnxT_full", [128, DC, TOK], BF16)
    sq = p.sbuf("sq", [128, D], BF16)
    wcol = [p.sbuf(f"wcol{i}", [128, DC, 128], BF16) for i in range(2)]
    tmp1 = p.sbuf("tmp1", [64, TOK], F32)
    tmp2 = p.sbuf("tmp2", [64, TOK], F32)
    qstage = [p.sbuf(f"qstage{i}", [128, TOK], BF16) for i in range(2)]
    ps_fm = [p.psum(f"ps_fm{i}", [128, 1024]) for i in range(2)]

    bufs = dict(stats=stats, sq=sq, eps_ln=eps_ln)
    xsl = DmaSlots(p, "xld", 2)
    wsl = DmaSlots(p, "wld", 2)
    osl = DmaSlots(p, "osl", 2)

    x_ld = [None] * NT
    apply_t = [0] * NT
    trL = [0] * NT

    @blk.gpsimd
    def _(g):
        x_ld[0] = xsl.issue(g, 0, xr[0][:], x_h[0:128, :])
        x_ld[1] = xsl.issue(g, 1, xr[1][:], x_h[128:256, :])

    for j in range(NT):
        v_deps = [x_ld[j], (vs, init_t)]
        s_deps = [x_ld[j]]
        if j >= 1:
            v_deps.append((vs, apply_t[j - 1]))
            s_deps.append((vs, apply_t[j - 1]))
        apply_deps = []
        if j >= 2:
            apply_deps.append((tr, trL[j - 2]))
        apply_t[j] = emit_norm_tile(p, xr[j % 2][:], lnx[j % 2][:], bufs,
                                    v_deps, s_deps, apply_deps)
        trL[j] = emit_xbar(p, lnxT[:, :, j * 128:(j + 1) * 128],
                           lnx[j % 2][:], [(vs, apply_t[j])])
        if j + 2 < NT:
            @blk.gpsimd
            def _(g, j=j):
                x_ld[j + 2] = xsl.issue(g, j + 2, xr[j % 2][:],
                                        x_h[(j + 2) * 128:(j + 3) * 128, :],
                                        pre_deps=[(vs, apply_t[j])])

    w_ld_t = [None] * 32
    mm_t = [0] * 32
    rope_t = [0] * 32
    qst = [None, None]

    for oc in range(32):
        @blk.gpsimd
        def _(g, oc=oc):
            w_ld_t[oc] = wsl.issue(
                g, oc, wcol[oc % 2][:], wq_h[:, oc],
                pre_deps=[(ts, mm_t[oc - 2] if oc >= 2 else 0)])

        @blk.tensor
        def _(t, oc=oc):
            waits(t, [w_ld_t[oc], (tr, trL[NT - 1]), (ldw, w_ld)])
            if oc >= 2:
                t.wait_ge(vs.h, rope_t[oc - 2])
            pb = ps_fm[oc % 2]
            mm = None
            for c in range(DC):
                t.matmul(pb[:, 0:512], wcol[oc % 2][:, c, :],
                         lnxT[:, c, 0:512], start=(c == 0), stop=(c == DC - 1))
                mm = t.matmul(pb[:, 512:1024], wcol[oc % 2][:, c, :],
                              lnxT[:, c, 512:1024], start=(c == 0),
                              stop=(c == DC - 1))
            ts.tick(mm)
        mm_t[oc] = ts.n

        @blk.vector
        def _(v, oc=oc):
            v.wait_ge(ts.h, mm_t[oc])
            if oc >= 1:
                v.wait_ge(vs.h, rope_t[oc - 1])
            if qst[oc % 2] is not None:
                waits(v, [qst[oc % 2]])
            pb = ps_fm[oc % 2]
            vs.tick(v.tensor_tensor(tmp1[:], pb[0:64, :], cos_sb[:], op=ALU.mult))
            vs.tick(v.tensor_tensor(tmp2[:], pb[64:128, :], sin_sb[:], op=ALU.mult))
            v.wait_ge(vs.h, vs.n)
            vs.tick(v.tensor_tensor(qstage[oc % 2][0:64, :], tmp1[:], tmp2[:],
                                    op=ALU.subtract))
            v.wait_ge(vs.h, vs.n)
            vs.tick(v.tensor_tensor(tmp1[:], pb[64:128, :], cos_sb[:], op=ALU.mult))
            vs.tick(v.tensor_tensor(tmp2[:], pb[0:64, :], sin_sb[:], op=ALU.mult))
            v.wait_ge(vs.h, vs.n)
            vs.tick(v.tensor_tensor(qstage[oc % 2][64:128, :], tmp1[:], tmp2[:],
                                    op=ALU.add))
        rope_t[oc] = vs.n

        @blk.scalar
        def _(s_, oc=oc):
            qst[oc % 2] = osl.issue(s_, oc, qT_int[oc], qstage[oc % 2][:],
                                    pre_deps=[(vs, rope_t[oc])])

    phaseA_end = [(ts, ts.n), (vs, vs.n), (ss, ss.n), (tr, tr.n)] + \
        [(s_, s_.n) for s_ in osl.sems if s_.n]
    p.phase_close()

    p.phase_open()
    ps_s2 = [p.psum(f"ps_s{i}", [128, 1024]) for i in range(2)]
    ps_ctx2 = [p.psum(f"ps_ctx{q}", [128, 512]) for q in range(2)]
    ps_l2 = [p.psum(f"ps_l{q}", [128, 512]) for q in range(2)]
    qh_sb = [p.sbuf(f"qh{i}", [128, TOK], BF16) for i in range(2)]
    kh_sb = [p.sbuf(f"kh{i}", [128, 8192], BF16) for i in range(2)]
    vh_sb = [p.sbuf(f"vh{i}", [128, 64, 128], BF16) for i in range(2)]
    q2 = p.sbuf("q2", [128, TOK], BF16)
    k2 = p.sbuf("k2", [128, 8192], BF16)
    pT2 = [p.sbuf(f"pT{i}", [128, 1024], BF16) for i in range(2)]
    ctx_ev = [p.sbuf(f"ctx_ev{i}", [128, 512], BF16) for i in range(2)]
    ps_misc = ps_s2[0]

    b = dict(ps_s2=ps_s2, ps_ctx2=ps_ctx2, ps_l2=ps_l2, pT2=pT2,
             ones_col=ones_col, ones_f32=ones_f32, negC_rep=negC_rep,
             lv=lv, linv=linv, r_rep=r_rep, ones_bf=ones_bf,
             state=dict(last_exp=0, last_av=0, last_ctxev=0, last_lev=0,
                        last_rrep=0))

    qsl = DmaSlots(p, "qsl", 2)
    ksl = DmaSlots(p, "ksl", 2)
    vsl = DmaSlots(p, "vsl", 2)
    csl = DmaSlots(p, "cst", 2)
    kv_last_ts = [0] * 10
    qh_last = {}
    ld_t = {}
    ctx_st = {}

    @blk.gpsimd
    def _(g):
        ld_t[("k", 0)] = ksl.issue(g, 0, kh_sb[0][:], ktg_h[0],
                                   pre_deps=phaseA_end)
        ld_t[("v", 0)] = vsl.issue(g, 0, vh_sb[0][:],
                                   vtg_h[0].rearrange("(tt pp) d -> pp tt d",
                                                      pp=128),
                                   pre_deps=phaseA_end)
        ld_t[("q", 0)] = qsl.issue(g, 0, qh_sb[0][:], qT_int[0],
                                   pre_deps=phaseA_end)

    hh_ss = [0] * 34
    hh_ts = [0] * 34
    hh_vs = [0] * 34
    for kv in range(8):
        if kv + 1 < 8:
            @blk.gpsimd
            def _(g, kv=kv):
                kp = kv + 1
                pre = [(ts, kv_last_ts[kp - 2])] if kp >= 2 else []
                ld_t[("k", kp)] = ksl.issue(g, kp, kh_sb[kp % 2][:], ktg_h[kp],
                                            pre_deps=pre)
                ld_t[("v", kp)] = vsl.issue(
                    g, kp, vh_sb[kp % 2][:],
                    vtg_h[kp].rearrange("(tt pp) d -> pp tt d", pp=128),
                    pre_deps=pre)

        war0 = ([(ts, hh_ts[kv * 4 - 1]), (ss, hh_ss[kv * 4 - 1]),
                 (vs, hh_vs[kv * 4 - 1])]
                if kv >= 1 else [(vs, init_t)])
        kmax_t = emit_cbound_kmax(p, kh_sb[kv % 2][:], k2, cb, ones_col,
                                  ps_misc, 8192, [ld_t[("k", kv)]], war0)
        for qi in range(4):
            h = kv * 4 + qi
            if h + 1 < 32:
                @blk.gpsimd
                def _(g, h=h):
                    hp = h + 1
                    pre = [(ts, hh_ts[hp - 2])] if hp >= 2 else []
                    ld_t[("q", hp)] = qsl.issue(g, hp, qh_sb[hp % 2][:],
                                                qT_int[hp], pre_deps=pre)

            war = ([(ts, hh_ts[h - 1]), (ss, hh_ss[h - 1]),
                    (vs, hh_vs[h - 1])]
                   if h >= 1 else [(vs, init_t)])
            negC_t, _mm = emit_cbound_q(p, qh_sb[h % 2][:], q2, cb, ones_col,
                                        ps_misc, lnsc, ones_f32, negC_rep,
                                        [ld_t[("q", h)]], war, kmax_t)
            prev_sts = [ctx_st.get((h - 1, qc)) for qc in range(2)]
            r = emit_head(p, b, qh_sb[h % 2][:], kh_sb[kv % 2][:],
                          vh_sb[kv % 2][:].rearrange("p tt d -> p (tt d)"),
                          64, INV_SQRT_HD, negC_t,
                          [ctx_ev[0][:], ctx_ev[1][:]],
                          [ld_t[("q", h)], (vs, negC_t)], [ld_t[("k", kv)]],
                          [ld_t[("v", kv)]],
                          ctx_war=[d for d in prev_sts if d])

            @blk.scalar
            def _(s_, h=h, r=r):
                for qc in range(2):
                    ctx_st[(h, qc)] = csl.issue(
                        s_, qc, ctx_int[h, :, qc * 512:(qc + 1) * 512],
                        ctx_ev[qc][:],
                        pre_deps=[(vs, r["ctxev"][qc])])
            hh_ts[h] = ts.n
            hh_ss[h] = ss.n
            hh_vs[h] = vs.n
        kv_last_ts[kv] = ts.n
    ctx_stores = [(s_, s_.n) for s_ in csl.sems if s_.n]
    phaseB_end = [(ts, ts.n), (vs, vs.n), (ss, ss.n)] + ctx_stores
    p.phase_close()

    p.phase_open()
    ps_o = [p.psum(f"ps_o{i}", [128, 512]) for i in range(4)]
    x_grp = p.sbuf("x_grp", [128, 4 * D], F32)
    wo_sb = [p.sbuf(f"wo_sb{i}", [128, 8, 512], BF16) for i in range(2)]
    ctx_grp = [p.sbuf(f"ctx_grp{i}", [128, 32, 512], BF16) for i in range(2)]
    b2 = dict(x_grp=x_grp, wo_sb=wo_sb, ps_o=ps_o, ones_bf=ones_bf,
              wo_ld=[None] * 64)
    cgsl = DmaSlots(p, "cgsl", 2)
    cg_ld = {}

    def ctx_chunk_fn(g, deps):
        @blk.gpsimd
        def _(gq, g=g):
            cg_ld[g] = cgsl.issue(
                gq, g, ctx_grp[g][:],
                ctx_int[:, :, g * 512:(g + 1) * 512].rearrange("h p r -> p h r"),
                pre_deps=list(phaseB_end))

        def get_lhsT(c, tl):
            return ctx_grp[g][:, c, tl * 128:(tl + 1) * 128]
        return get_lhsT, [cg_ld[g]]

    ysl_st = emit_outproj(p, b2, wo_h, None, ctx_chunk_fn, x_h, y_h, False,
                          phaseB_end)
    p.phase_close()
    if standalone:
        p.phase_close()
        return p.finish()
    ctx["stores"] = ysl_st
    return


ACT_COMBINED_SET = 6


def emit_act_table_pin(p):
    blk = p.block

    @blk.scalar
    def _(s):
        s.add_instruction(mybir.InstLoadActFuncSet(
            name=p.nc.get_next_instruction_name(),
            act_func_set_id=ACT_COMBINED_SET, ins=[], outs=[]))


def emit_stage_barrier(p):
    blk = p.block
    deps = p.stage_pre()

    for section in (blk.tensor, blk.vector, blk.scalar, blk.sync, blk.gpsimd):
        @section
        def _(eng):
            waits(eng, deps)
    return deps


def build_fused():
    p = Prog()
    nc = p.nc
    xb_h = nc.dram_tensor("xb", [4096, D], F32, kind="ExternalInput").ap()
    atwr_h = nc.dram_tensor("atwr", [D, 136], F32, kind="ExternalInput").ap()
    brow_h = nc.dram_tensor("brow", [1, 136], F32, kind="ExternalInput").ap()
    bflat_h = nc.dram_tensor("bflat", [128, D], F32, kind="ExternalInput").ap()
    wqkv_h = nc.dram_tensor("wqkv", [128, 96, DC + 1, 128], F32,
                            kind="ExternalInput").ap()
    gq_h = nc.dram_tensor("gq", [128, 1], F32, kind="ExternalInput").ap()
    gk_h = nc.dram_tensor("gk", [128, 1], F32, kind="ExternalInput").ap()
    wo0_h = nc.dram_tensor("wo0", [128, 4, 8, 8, 512], F32,
                           kind="ExternalInput").ap()
    bo0_h = nc.dram_tensor("bo0", [1, D], F32, kind="ExternalInput").ap()
    wallT_h = nc.dram_tensor("wallT", [D, 24], F32, kind="ExternalInput").ap()
    bases_h = nc.dram_tensor("bases", [1, 24], F32, kind="ExternalInput").ap()
    wb_h = nc.dram_tensor("wb", [1024, 1024], F32, kind="ExternalInput").ap()
    gate_h = nc.dram_tensor("gate", [1, 1], F32, kind="ExternalInput").ap()
    wq1_h = nc.dram_tensor("wq1", [128, 32, DC, 128], F32,
                           kind="ExternalInput").ap()
    wo1_h = nc.dram_tensor("wo1", [128, 4, 8, 8, 512], F32,
                           kind="ExternalInput").ap()
    ktg_h = nc.dram_tensor("ktg", [8, 128, 8192], F32, kind="ExternalInput").ap()
    vtg_h = nc.dram_tensor("vtg", [8, 8192, 128], F32, kind="ExternalInput").ap()
    cos_h = nc.dram_tensor("cos", [64, TOK], F32, kind="ExternalInput").ap()
    sin_h = nc.dram_tensor("sin", [64, TOK], F32, kind="ExternalInput").ap()
    y_h = nc.dram_tensor("y", [TOK, D], F32, kind="ExternalOutput").ap()

    x1b = nc.dram_tensor("x1b", [4096, D], F32).ap()
    xB = nc.dram_tensor("xB", [TOK, D], F32).ap()
    xC = nc.dram_tensor("xC", [TOK, D], F32).ap()
    qT0_int = nc.dram_tensor("qT0_int", [32, 128, TOK], BF16).ap()
    kTf_int = nc.dram_tensor("kTf_int", [32, 128, 4096], BF16).ap()
    vf_int = nc.dram_tensor("vf_int", [4096, D], BF16).ap()
    qT1_int = nc.dram_tensor("qT1_int", [32, 128, TOK], BF16).ap()
    ctx1_int = nc.dram_tensor("ctx1_int", [32, 128, TOK], BF16).ap()

    blk = p.start_block()
    emit_act_table_pin(p)

    def stage(fn, **aps):
        emit_stage_barrier(p)
        p.phase_open()
        c = dict(p=p, **aps)
        fn(c)
        p.phase_close()
        return c

    stage(build_ffn0, x=xb_h, y=x1b, atwr=atwr_h, brow=brow_h, bflat=bflat_h,
          ntile=32)
    stage(build_qkv0, x=x1b, wqkv=wqkv_h, gq=gq_h, gk=gk_h,
          qT=qT0_int, kT=kTf_int, v=vf_int, ngroups=4)
    stage(build_attn0, x=x1b[0:TOK, :], qT=qT0_int, kT3=kTf_int, vf=vf_int,
          wo=wo0_h, bo=bo0_h, y=xB)
    stage(build_ffn0, x=xB, y=xC, atwr=atwr_h, brow=brow_h, bflat=bflat_h)

    stage(build_ffn1, x=xC, y=xB, wallT=wallT_h, bases=bases_h, wb=wb_h,
          gate=gate_h)
    stage(build_attn1, x=xB, y=xC, wq=wq1_h, wo=wo1_h, ktg=ktg_h,
          vtg=vtg_h, cos=cos_h, sin=sin_h, qT_int=qT1_int, ctx_int=ctx1_int)
    c6 = stage(build_ffn1, x=xC, y=y_h, wallT=wallT_h, bases=bases_h,
               wb=wb_h, gate=gate_h)

    @blk.gpsimd
    def _(g):
        waits(g, c6["stores"])

    return p.finish()


def kernel_fused(inputs):
    nc = _get("fused", build_fused)
    x = np.ascontiguousarray(
        inputs["L0_attn_hidden_states"].reshape(2 * 4096, D).astype(np.float32))
    xbat = []
    for c in range(NCORES):
        bb = c // 4
        rows = np.concatenate(
            [x[c * TOK:(c + 1) * TOK]] +
            [x[cc * TOK:(cc + 1) * TOK] for cc in range(bb * 4, bb * 4 + 4)
             if cc != c], axis=0)
        xbat.append(np.ascontiguousarray(rows))
    A, B = inputs["L0_ffn_A"], inputs["L0_ffn_B"]
    AT = np.ascontiguousarray(A.reshape(128, D).T)
    WrT = np.ascontiguousarray(inputs["L0_ffn_Wr"].T)
    atwr = np.ascontiguousarray(np.concatenate([AT, WrT], axis=1))
    brow = np.ascontiguousarray(np.concatenate(
        [np.zeros(128, np.float32), inputs["L0_ffn_br"]]).reshape(1, 136))
    bflat = np.ascontiguousarray(B.transpose(0, 2, 1).reshape(128, D))
    wqkv = np.concatenate([inputs["L0_attn_Wq"], inputs["L0_attn_Wk"],
                           inputs["L0_attn_Wv"]], axis=1)
    bpad = np.zeros((128, 12288), np.float32)
    bpad[0] = np.concatenate([inputs["L0_attn_bq"], inputs["L0_attn_bk"],
                              inputs["L0_attn_bv"]])
    wqkv = _wcols(np.concatenate([wqkv, bpad], axis=0), 128)
    wall = np.concatenate([inputs["L1_ffn_W_res"], inputs["L1_ffn_W_pre"],
                           inputs["L1_ffn_W_post"]], axis=0)
    wallT = np.ascontiguousarray(wall.T)
    bases = np.ascontiguousarray(np.concatenate(
        [inputs["L1_ffn_H_res_base"].reshape(16), inputs["L1_ffn_H_pre_base"],
         inputs["L1_ffn_H_post_base"]]).reshape(1, 24))
    blocks = inputs["L1_attn_kv_cache"][inputs["L1_attn_block_table"]]
    Kg = blocks[:, 0].transpose(1, 0, 2, 3).reshape(8, 8192, 128)
    Vg = blocks[:, 1].transpose(1, 0, 2, 3).reshape(8, 8192, 128)
    ktg = np.ascontiguousarray(Kg.transpose(0, 2, 1))
    vtg = np.ascontiguousarray(Vg)
    inv_freq = (1.0 / 10000.0 ** (np.arange(0, 128, 2, dtype=np.float32) / 128.0)
                ).astype(np.float32)
    ang = (np.arange(4096, dtype=np.float32)[:, None] * inv_freq[None, :])
    cos_full = np.cos(ang).astype(np.float32)
    sin_full = np.sin(ang).astype(np.float32)
    common = {
        "atwr": atwr, "brow": brow, "bflat": bflat, "wqkv": wqkv,
        "gq": np.ascontiguousarray(inputs["L0_attn_gq"].reshape(128, 1)),
        "gk": np.ascontiguousarray(inputs["L0_attn_gk"].reshape(128, 1)),
        "wo0": np.ascontiguousarray(inputs["L0_attn_Wo"].reshape(
            4, 8, 128, 8, 512).transpose(2, 0, 3, 1, 4)),
        "bo0": np.ascontiguousarray(inputs["L0_attn_bo"].reshape(1, D)),
        "wallT": wallT, "bases": bases,
        "wb": np.ascontiguousarray(inputs["L1_ffn_W_branch"]),
        "gate": np.ascontiguousarray(inputs["L1_ffn_gate"].reshape(1, 1)),
        "wq1": _wcols(inputs["L1_attn_Wq"], 128),
        "wo1": np.ascontiguousarray(inputs["L1_attn_Wo"].reshape(
            4, 8, 128, 8, 512).transpose(2, 0, 3, 1, 4)),
        "ktg": ktg, "vtg": vtg,
    }
    in_maps = []
    for c in range(NCORES):
        n0 = (c % 4) * TOK
        m = dict(common)
        m["xb"] = xbat[c]
        m["cos"] = np.ascontiguousarray(cos_full[n0:n0 + TOK].T)
        m["sin"] = np.ascontiguousarray(sin_full[n0:n0 + TOK].T)
        in_maps.append(m)
    res = _spmd(nc, in_maps)
    y = np.concatenate([r["y"] for r in res.results], axis=0)
    return y.reshape(2, 4096, D)



_NC_CACHE = {}


def _get(name, builder):
    if name not in _NC_CACHE:
        _NC_CACHE[name] = builder()
    return _NC_CACHE[name]


def _wcols(W, blk):
    R, C = W.shape
    return np.ascontiguousarray(
        W.reshape(R // 128, 128, C // blk, blk).transpose(1, 2, 0, 3))


def _spmd(nc, in_maps):
    return run_bass_kernel_spmd(nc, in_maps, core_ids=list(range(NCORES)))


def run_ffn0(x, A, B, Wr, br):
    nc = _get("ffn0", build_ffn0)
    AT = np.ascontiguousarray(A.reshape(128, D).T)
    WrT = np.ascontiguousarray(Wr.T)
    atwr = np.ascontiguousarray(np.concatenate([AT, WrT], axis=1))
    brow = np.ascontiguousarray(
        np.concatenate([np.zeros(128, np.float32), br]).reshape(1, 136))
    bflat = np.ascontiguousarray(B.transpose(0, 2, 1).reshape(128, D))
    in_maps = [{"x": np.ascontiguousarray(x[c * TOK:(c + 1) * TOK]),
                "atwr": atwr, "brow": brow, "bflat": bflat}
               for c in range(NCORES)]
    res = _spmd(nc, in_maps)
    return np.concatenate([r["y"] for r in res.results], axis=0)


def run_ffn1(x, W_res, W_pre, W_post, Hrb, Hpb, Hqb, gate, Wb):
    nc = _get("ffn1", build_ffn1)
    wall = np.concatenate([W_res, W_pre, W_post], axis=0)
    wallT = np.ascontiguousarray(wall.T)
    bases = np.ascontiguousarray(
        np.concatenate([Hrb.reshape(16), Hpb, Hqb]).reshape(1, 24))
    gate2 = np.ascontiguousarray(gate.reshape(1, 1))
    wb = np.ascontiguousarray(Wb)
    in_maps = [{"x": np.ascontiguousarray(x[c * TOK:(c + 1) * TOK]),
                "wallT": wallT, "bases": bases, "wb": wb, "gate": gate2}
               for c in range(NCORES)]
    res = _spmd(nc, in_maps)
    return np.concatenate([r["y"] for r in res.results], axis=0)


def run_qkv0(x, Wq, bq, Wk, bk, Wv, bv, gq, gk):
    nc = _get("qkv0", build_qkv0)
    wqkv = np.concatenate([Wq, Wk, Wv], axis=1)
    bpad = np.zeros((128, 12288), np.float32)
    bpad[0] = np.concatenate([bq, bk, bv])
    wqkv = _wcols(np.concatenate([wqkv, bpad], axis=0), 128)
    gq2 = np.ascontiguousarray(gq.reshape(128, 1))
    gk2 = np.ascontiguousarray(gk.reshape(128, 1))
    in_maps = [{"x": np.ascontiguousarray(x[c * TOK:(c + 1) * TOK]),
                "wqkv": wqkv, "gq": gq2, "gk": gk2}
               for c in range(NCORES)]
    res = _spmd(nc, in_maps)
    qT = [r["qT"] for r in res.results]
    kT = [r["kT"] for r in res.results]
    v = [r["v"] for r in res.results]
    return qT, kT, v


def run_attn0(x, qT, kT, v, Wo, bo):
    nc = _get("attn0", build_attn0)
    kT_full = [np.ascontiguousarray(np.concatenate(kT[b * 4:(b + 1) * 4], axis=2))
               for b in range(2)]
    v_full = [np.ascontiguousarray(np.concatenate(v[b * 4:(b + 1) * 4], axis=0))
              for b in range(2)]
    wo = np.ascontiguousarray(Wo.reshape(4, 8, 128, 8, 512).transpose(2, 0, 3, 1, 4))
    bo2 = np.ascontiguousarray(bo.reshape(1, D))
    in_maps = [{"x": np.ascontiguousarray(x[c * TOK:(c + 1) * TOK]),
                "qT": qT[c], "kT": kT_full[c // 4], "vf": v_full[c // 4],
                "wo": wo, "bo": bo2}
               for c in range(NCORES)]
    res = _spmd(nc, in_maps)
    return np.concatenate([r["y"] for r in res.results], axis=0)


def run_attn1(x, Wq, Wo, kv_cache, block_table):
    nc = _get("attn1", build_attn1)
    blocks = kv_cache[block_table]
    K = blocks[:, 0].transpose(1, 0, 2, 3).reshape(8, 8192, 128)
    V = blocks[:, 1].transpose(1, 0, 2, 3).reshape(8, 8192, 128)
    ktg = np.ascontiguousarray(K.transpose(0, 2, 1))
    vtg = np.ascontiguousarray(V)
    wq = _wcols(Wq, 128)
    wo = np.ascontiguousarray(Wo.reshape(4, 8, 128, 8, 512).transpose(2, 0, 3, 1, 4))
    inv_freq = (1.0 / 10000.0 ** (np.arange(0, 128, 2, dtype=np.float32) / 128.0)
                ).astype(np.float32)
    ang = (np.arange(4096, dtype=np.float32)[:, None] * inv_freq[None, :])
    cos_full = np.cos(ang).astype(np.float32)
    sin_full = np.sin(ang).astype(np.float32)
    in_maps = []
    for c in range(NCORES):
        n0 = (c % 4) * TOK
        in_maps.append({"x": np.ascontiguousarray(x[c * TOK:(c + 1) * TOK]),
                        "wq": wq, "wo": wo, "ktg": ktg, "vtg": vtg,
                        "cos": np.ascontiguousarray(cos_full[n0:n0 + TOK].T),
                        "sin": np.ascontiguousarray(sin_full[n0:n0 + TOK].T)})
    res = _spmd(nc, in_maps)
    return np.concatenate([r["y"] for r in res.results], axis=0)


def kernel_unfused(**inputs):
    x = np.ascontiguousarray(
        inputs["L0_attn_hidden_states"].reshape(2 * 4096, D).astype(np.float32))
    x = run_ffn0(x, inputs["L0_ffn_A"], inputs["L0_ffn_B"],
                 inputs["L0_ffn_Wr"], inputs["L0_ffn_br"])
    qT, kT, v = run_qkv0(x, inputs["L0_attn_Wq"], inputs["L0_attn_bq"],
                         inputs["L0_attn_Wk"], inputs["L0_attn_bk"],
                         inputs["L0_attn_Wv"], inputs["L0_attn_bv"],
                         inputs["L0_attn_gq"], inputs["L0_attn_gk"])
    x = run_attn0(x, qT, kT, v, inputs["L0_attn_Wo"], inputs["L0_attn_bo"])
    x = run_ffn0(x, inputs["L0_ffn_A"], inputs["L0_ffn_B"],
                 inputs["L0_ffn_Wr"], inputs["L0_ffn_br"])
    x = run_ffn1(x, inputs["L1_ffn_W_res"], inputs["L1_ffn_W_pre"],
                 inputs["L1_ffn_W_post"], inputs["L1_ffn_H_res_base"],
                 inputs["L1_ffn_H_pre_base"], inputs["L1_ffn_H_post_base"],
                 inputs["L1_ffn_gate"], inputs["L1_ffn_W_branch"])
    x = run_attn1(x, inputs["L1_attn_Wq"], inputs["L1_attn_Wo"],
                  inputs["L1_attn_kv_cache"], inputs["L1_attn_block_table"])
    x = run_ffn1(x, inputs["L1_ffn_W_res"], inputs["L1_ffn_W_pre"],
                 inputs["L1_ffn_W_post"], inputs["L1_ffn_H_res_base"],
                 inputs["L1_ffn_H_pre_base"], inputs["L1_ffn_H_post_base"],
                 inputs["L1_ffn_gate"], inputs["L1_ffn_W_branch"])
    return x.reshape(2, 4096, D)


def kernel(**inputs):
    return kernel_fused(inputs)


# version 11
# speedup vs baseline: 1.3032x; 1.0356x over previous
import numpy as np
from contextlib import ExitStack

import concourse.bass as bass
import concourse.mybir as mybir
from concourse.bass_utils import run_bass_kernel_spmd

F32 = mybir.dt.float32
BF16 = mybir.dt.bfloat16
AF = mybir.ActivationFunctionType
ALU = mybir.AluOpType
AX = mybir.AxisListType

NCORES = 8
TOK = 1024
NT = TOK // 128
D = 4096
DC = D // 128
LN_EPS = 1e-5
RMS_EPS_FFN1 = 1.1920929e-07
QK_EPS = 1e-6
INV_SQRT_HD = float(1.0 / np.sqrt(128.0))
C_MARGIN = 30.0


class Sem:
    def __init__(self, nc, es, name):
        self.h = es.enter_context(nc.semaphore(name))
        self.n = 0

    def tick(self, instr, k=1):
        instr.then_inc(self.h, k)
        self.n += k
        return self.n


def waits(eng, deps):
    for sem, n in deps:
        if n > 0:
            eng.wait_ge(sem.h, n)


class DmaSlots:

    def __init__(self, p, name, nslots):
        self.sems = [p.sem(f"{name}{i}") for i in range(nslots)]
        self.nslots = nslots

    def issue(self, eng, slot, out, in_, pre_deps=()):
        s = self.sems[slot % self.nslots]
        if s.n > 0:
            eng.wait_ge(s.h, s.n)
        waits(eng, pre_deps)
        s.tick(eng.dma_start(out=out, in_=in_), 16)
        return (s, s.n)

    def issue_multi(self, eng, slot, pairs, pre_deps=()):
        s = self.sems[slot % self.nslots]
        if s.n > 0:
            eng.wait_ge(s.h, s.n)
        waits(eng, pre_deps)
        for out, in_ in pairs:
            s.tick(eng.dma_start(out=out, in_=in_), 16)
        return (s, s.n)


class Prog:
    def __init__(self):
        self.nc = bass.Bass("TRN2", target_bir_lowering=False)
        self.es = ExitStack()
        self.block = None
        self._sems = {}
        self._phases = []
        self._uid = 0

    def uid(self):
        self._uid += 1
        return self._uid

    def sem(self, name):
        if name not in self._sems:
            self._sems[name] = Sem(self.nc, self.es, name)
        return self._sems[name]

    def sbuf(self, name, shape, dtype):
        es = self._phases[-1] if self._phases else self.es
        if self._phases:
            name = f"{name}_u{self.uid()}"
        return es.enter_context(self.nc.sbuf_tensor(name, shape, dtype))

    def psum(self, name, shape, dtype=F32):
        es = self._phases[-1] if self._phases else self.es
        if self._phases:
            name = f"{name}_u{self.uid()}"
        return es.enter_context(self.nc.psum_tensor(name, shape, dtype))

    def phase_open(self):
        self._phases.append(ExitStack())

    def phase_close(self):
        self._phases.pop().close()

    def stage_pre(self):
        deps = []
        for s in self._sems.values():
            if s.n > 0:
                deps.append((s, s.n))
        return deps

    def start_block(self):
        self.block = self.es.enter_context(self.nc.Block())
        return self.block

    def finish(self):
        self.es.close()
        return self.nc


def emit_norm_tile(p, xa, out_bf, bufs, v_deps, s_deps, apply_deps,
                   rms_only=False, eps=LN_EPS, nfree=D):
    blk = p.block
    vs, ss = p.sem("vs"), p.sem("ss")
    stats, sq, eps_ap = bufs["stats"], bufs["sq"], bufs["eps_" + ("rms" if rms_only else "ln")]
    inv_n = 1.0 / nfree

    if not rms_only:
        @blk.vector
        def _(v):
            waits(v, v_deps)
            vs.tick(v.tensor_reduce(stats[:, 0:1], xa, axis=AX.X, op=ALU.add))
        red_t = vs.n

    @blk.scalar
    def _(s):
        waits(s, s_deps)
        ss.tick(s.activation(out=sq[:, 0:nfree], in_=xa, func=AF.Square,
                             accum_out=stats[:, 1:2]))
    sq_t = ss.n

    if rms_only:
        @blk.scalar
        def _(s):
            s.wait_ge(ss.h, sq_t)
            waits(s, v_deps)
            ss.tick(s.activation(out=stats[:, 3:4], in_=stats[:, 1:2],
                                 func=AF.Ln, bias=eps_ap[:], scale=inv_n))
            s.wait_ge(ss.h, ss.n)
            ss.tick(s.activation(out=stats[:, 3:4], in_=stats[:, 3:4],
                                 func=AF.Exp, scale=-0.5))
        r_t = ss.n

        @blk.vector
        def _(v):
            v.wait_ge(ss.h, r_t)
            waits(v, apply_deps)
            vs.tick(v.tensor_scalar(out=out_bf, in0=xa, scalar1=stats[:, 3:4],
                                    scalar2=None, op0=ALU.mult))
        return vs.n

    @blk.vector
    def _(v):
        v.wait_ge(vs.h, red_t)
        v.wait_ge(ss.h, sq_t)
        vs.tick(v.tensor_scalar(out=stats[:, 0:2], in0=stats[:, 0:2],
                                scalar1=inv_n, scalar2=None, op0=ALU.mult))
        v.wait_ge(vs.h, vs.n)
        vs.tick(v.scalar_tensor_tensor(out=stats[:, 2:3], in0=stats[:, 0:1],
                                       scalar=-1.0, in1=stats[:, 0:1],
                                       op0=ALU.mult, op1=ALU.mult))
        v.wait_ge(vs.h, vs.n)
        vs.tick(v.tensor_tensor(stats[:, 2:3], stats[:, 1:2], stats[:, 2:3],
                                op=ALU.add))
    var_t = vs.n

    @blk.scalar
    def _(s):
        s.wait_ge(vs.h, var_t)
        ss.tick(s.activation(out=stats[:, 3:4], in_=stats[:, 2:3],
                             func=AF.Ln, bias=eps_ap[:], scale=1.0))
        s.wait_ge(ss.h, ss.n)
        ss.tick(s.activation(out=stats[:, 3:4], in_=stats[:, 3:4],
                             func=AF.Exp, scale=-0.5))
    r_t = ss.n

    @blk.vector
    def _(v):
        v.wait_ge(ss.h, r_t)
        waits(v, apply_deps)
        vs.tick(v.tensor_scalar(out=out_bf, in0=xa, scalar1=stats[:, 0:1],
                                scalar2=stats[:, 3:4],
                                op0=ALU.subtract, op1=ALU.mult))
    return vs.n


def emit_xbar(p, out_ap, in_ap, pre_deps):
    blk = p.block
    tr = p.sem("tr")
    prev = tr.n

    @blk.sync
    def _(sync):
        waits(sync, pre_deps)
        if prev > 0:
            sync.wait_ge(tr.h, prev)
        tr.tick(sync.dma_start_transpose(out_ap, in_ap), 16)
    return tr.n



def build_ffn0(ctx=None, ntile=NT):
    if ctx is None:
        p = Prog()
        nc = p.nc
        x_h = nc.dram_tensor("x", [TOK, D], F32, kind="ExternalInput").ap()
        atwr_h = nc.dram_tensor("atwr", [D, 136], F32, kind="ExternalInput").ap()
        brow_h = nc.dram_tensor("brow", [1, 136], F32, kind="ExternalInput").ap()
        bflat_h = nc.dram_tensor("bflat", [128, D], F32, kind="ExternalInput").ap()
        y_h = nc.dram_tensor("y", [TOK, D], F32, kind="ExternalOutput").ap()
        p.start_block()
        emit_act_table_pin(p)
        p.phase_open()
    else:
        p = ctx["p"]
        x_h, y_h = ctx["x"], ctx["y"]
        atwr_h, brow_h, bflat_h = ctx["atwr"], ctx["brow"], ctx["bflat"]
        ntile = ctx.get("ntile", ntile)

    xr = [p.sbuf(f"xr{i}", [128, D], F32) for i in range(3)]
    lnx = [p.sbuf(f"lnx{i}", [128, D], BF16) for i in range(2)]
    lnxT = [p.sbuf(f"lnxT{i}", [128, DC, 128], BF16) for i in range(2)]
    sq = p.sbuf("sq", [128, D], BF16)
    stats = p.sbuf("stats", [128, 4], F32)
    eps_ln = p.sbuf("eps_ln", [128, 1], F32)
    atwr_sb = p.sbuf("atwr_sb", [128, DC * 136], BF16)
    brow_sb = p.sbuf("brow_sb", [1, 136], BF16)
    bflat_sb = p.sbuf("bflat_sb", [128, D], BF16)
    ones_bf = p.sbuf("ones_bf", [1, 128], BF16)
    er = p.sbuf("er", [128, 8], F32)
    shat = p.sbuf("shat", [128, 8], F32)
    smax = p.sbuf("smax", [128, 1], F32)
    ssum = p.sbuf("ssum", [128, 1], F32)
    rr = p.sbuf("rr", [128, 1], F32)
    slow = [p.sbuf(f"slow{i}", [128, 128], BF16) for i in range(2)]
    slowT = [p.sbuf(f"slowT{i}", [128, 128], BF16) for i in range(2)]

    ps_low = [p.psum(f"ps_low{i}", [128, 512]) for i in range(2)]
    ps_up = [p.psum(f"ps_up{i}", [128, 512]) for i in range(2)]

    blk = p.block
    ldw = p.sem("ldw")
    tr = p.sem("tr")
    ts = p.sem("ts")
    vs, ss = p.sem("vs"), p.sem("ss")
    xsl = DmaSlots(p, "xld", 3)
    ysl = DmaSlots(p, "yst", 3)

    @blk.gpsimd
    def _(g):
        ldw.tick(g.dma_start(out=atwr_sb[:].rearrange("p (c n) -> p c n", c=DC),
                             in_=atwr_h.rearrange("(c p) n -> p c n", p=128)), 16)
        ldw.tick(g.dma_start(out=brow_sb[:], in_=brow_h), 16)
        ldw.tick(g.dma_start(out=bflat_sb[:], in_=bflat_h), 16)
    w_ld = ldw.n

    @blk.vector
    def _(v):
        v.memset(ones_bf[:], 1.0)
        vs.tick(v.memset(eps_ln[:], LN_EPS))
    init_t = vs.n

    bufs = dict(stats=stats, sq=sq, eps_ln=eps_ln)

    x_ld = [None] * ntile
    apply_t = [0] * ntile
    low_t = [0] * ntile
    exp_t = [0] * ntile
    slow_t = [0] * ntile
    trL = [0] * ntile
    trS = [0] * ntile
    up_t = [[0] * 8 for _ in range(ntile)]
    evac_t = [[0] * 8 for _ in range(ntile)]
    st_t = [None] * ntile

    @blk.gpsimd
    def _(g):
        for j in range(min(3, ntile)):
            x_ld[j] = xsl.issue(g, j, xr[j % 3][:],
                                x_h[j * 128:(j + 1) * 128, :])

    for j in range(ntile):
        xa = xr[j % 3][:]
        v_deps = [x_ld[j], (vs, init_t)]
        s_deps = [x_ld[j]]
        if j >= 1:
            v_deps.append((vs, apply_t[j - 1]))
            s_deps.append((vs, apply_t[j - 1]))
        apply_deps = []
        if j >= 2:
            apply_deps.append((tr, trL[j - 2]))
        apply_t[j] = emit_norm_tile(p, xa, lnx[j % 2][:], bufs,
                                    v_deps, s_deps, apply_deps)

        pre = [(vs, apply_t[j])]
        if j >= 2:
            pre.append((ts, low_t[j - 2]))
        trL[j] = emit_xbar(p, lnxT[j % 2][:], lnx[j % 2][:], pre)

        @blk.tensor
        def _(t, j=j):
            t.wait_ge(tr.h, trL[j])
            t.wait_ge(ldw.h, w_ld)
            if j >= 2:
                t.wait_ge(vs.h, slow_t[j - 2])
                t.wait_ge(ss.h, exp_t[j - 2])
            pl = ps_low[j % 2][:, 0:136]
            for c in range(DC):
                t.matmul(pl, lnxT[j % 2][:, c, :],
                         atwr_sb[:, c * 136:(c + 1) * 136],
                         start=(c == 0), stop=False)
            ts.tick(t.matmul(pl, ones_bf[:, 0:128], brow_sb[:],
                             start=False, stop=True))
        low_t[j] = ts.n

        @blk.vector
        def _(v, j=j):
            v.wait_ge(ts.h, low_t[j])
            if j >= 1:
                v.wait_ge(ss.h, exp_t[j - 1])
            vs.tick(v.tensor_reduce(smax[:], ps_low[j % 2][:, 128:136],
                                    axis=AX.X, op=ALU.max, negate=True))
        smax_t = vs.n

        @blk.scalar
        def _(s, j=j, smax_t=smax_t):
            s.wait_ge(vs.h, smax_t)
            if j >= 1:
                s.wait_ge(vs.h, slow_t[j - 1])
            ss.tick(s.activation(out=er[:], in_=ps_low[j % 2][:, 128:136],
                                 func=AF.Exp, bias=smax[:], accum_out=ssum[:]))
        exp_t[j] = ss.n

        @blk.vector
        def _(v, j=j):
            v.wait_ge(ss.h, exp_t[j])
            vs.tick(v.reciprocal(rr[:], ssum[:]))
            v.wait_ge(vs.h, vs.n)
            vs.tick(v.tensor_scalar(out=shat[:], in0=er[:], scalar1=rr[:],
                                    scalar2=8.0, op0=ALU.mult, op1=ALU.mult))
            v.wait_ge(vs.h, vs.n)
            if j >= 2:
                v.wait_ge(tr.h, trS[j - 2])
            vs.tick(v.tensor_tensor(
                slow[j % 2][:].rearrange("p (h r) -> p h r", h=8),
                ps_low[j % 2][:, 0:128].rearrange("p (h r) -> p h r", h=8),
                shat[:].unsqueeze(-1).broadcast_to([128, 8, 16]),
                op=ALU.mult))
        slow_t[j] = vs.n

        pre = [(vs, slow_t[j])]
        if j >= 2:
            pre.append((ts, up_t[j - 2][7]))
        trS[j] = emit_xbar(p, slowT[j % 2][:], slow[j % 2][:], pre)

        for oc in range(8):
            @blk.tensor
            def _(t, j=j, oc=oc):
                t.wait_ge(tr.h, trS[j])
                if oc >= 2:
                    t.wait_ge(vs.h, evac_t[j][oc - 2])
                elif j >= 1:
                    t.wait_ge(vs.h, evac_t[j - 1][6 + oc])
                ts.tick(t.matmul(ps_up[oc % 2][:], slowT[j % 2][:],
                                 bflat_sb[:, oc * 512:(oc + 1) * 512],
                                 start=True, stop=True))
            up_t[j][oc] = ts.n

            @blk.vector
            def _(v, j=j, oc=oc):
                v.wait_ge(ts.h, up_t[j][oc])
                xa_s = xr[j % 3][:, oc * 512:(oc + 1) * 512]
                vs.tick(v.tensor_tensor(xa_s, xa_s, ps_up[oc % 2][:], op=ALU.add))
            evac_t[j][oc] = vs.n

        @blk.scalar
        def _(s_, j=j):
            st_t[j] = ysl.issue(s_, j, y_h[j * 128:(j + 1) * 128, :],
                                xr[j % 3][:], pre_deps=[(vs, evac_t[j][7])])

        if j + 3 < ntile:
            @blk.gpsimd
            def _(g, j=j):
                x_ld[j + 3] = xsl.issue(g, j + 3, xr[j % 3][:],
                                        x_h[(j + 3) * 128:(j + 4) * 128, :],
                                        pre_deps=[st_t[j]])

    if ctx is None:
        @blk.scalar
        def _(g):
            for s_ in ysl.sems:
                if s_.n:
                    g.wait_ge(s_.h, s_.n)
        p.phase_close()
        return p.finish()
    ctx["stores"] = [(s_, s_.n) for s_ in ysl.sems if s_.n]
    return



def build_ffn1(ctx=None):
    if ctx is None:
        p = Prog()
        nc = p.nc
        x_h = nc.dram_tensor("x", [TOK, D], F32, kind="ExternalInput").ap()
        wallT_h = nc.dram_tensor("wallT", [D, 24], F32, kind="ExternalInput").ap()
        bases_h = nc.dram_tensor("bases", [1, 24], F32, kind="ExternalInput").ap()
        wb_h = nc.dram_tensor("wb", [1024, 1024], F32, kind="ExternalInput").ap()
        gate_h = nc.dram_tensor("gate", [1, 1], F32, kind="ExternalInput").ap()
        y_h = nc.dram_tensor("y", [TOK, D], F32, kind="ExternalOutput").ap()
        p.start_block()
        emit_act_table_pin(p)
        p.phase_open()
    else:
        p = ctx["p"]
        x_h, y_h = ctx["x"], ctx["y"]
        wallT_h, bases_h = ctx["wallT"], ctx["bases"]
        wb_h, gate_h = ctx["wb"], ctx["gate"]

    xr = [p.sbuf(f"xr{i}", [128, D], F32) for i in range(2)]
    lnf = [p.sbuf(f"lnf{i}", [128, D], F32) for i in range(2)]
    lnx = [p.sbuf(f"lnx{i}", [128, D], BF16) for i in range(2)]
    lnxT = [p.sbuf(f"lnxT{i}", [128, DC, 128], BF16) for i in range(2)]
    sq = p.sbuf("sq", [128, D], BF16)
    stats = p.sbuf("stats", [128, 4], F32)
    eps_rms = p.sbuf("eps_rms", [128, 1], F32)
    eps_ln = p.sbuf("eps_ln", [128, 1], F32)
    wallT_sb = p.sbuf("wallT_sb", [128, DC * 24], BF16)
    bases_sb = p.sbuf("bases_sb", [1, 24], BF16)
    ones_bf = p.sbuf("ones_bf", [1, 128], BF16)
    ones_f32 = p.sbuf("ones_f32", [1, 128], F32)
    Z = p.sbuf("Z", [128, NT * 16], F32)
    Hpre = p.sbuf("Hpre", [128, NT * 4], F32)
    Hpost = p.sbuf("Hpost", [128, NT * 4], F32)
    uu = p.sbuf("uu", [128, NT * 4], F32)
    vv = p.sbuf("vv", [128, NT * 4], F32)
    M1 = p.sbuf("M1", [128, NT * 16], F32)
    M2 = p.sbuf("M2", [128, NT * 16], F32)
    mx = p.sbuf("mx", [128, NT * 4], F32)
    sgum = p.sbuf("sgum", [128, NT * 4], F32)
    smax1 = p.sbuf("smax1", [128, 1], F32)
    er4a = p.sbuf("er4a", [128, 4], F32)
    ssum1 = p.sbuf("ssum1", [128, 1], F32)
    rr1 = p.sbuf("rr1", [128, 1], F32)
    smax2 = p.sbuf("smax2", [128, 1], F32)
    er4b = p.sbuf("er4b", [128, 4], F32)
    ssum2 = p.sbuf("ssum2", [128, 1], F32)
    rr2 = p.sbuf("rr2", [128, 1], F32)
    xpre = p.sbuf("xpre", [128, 1024], F32)
    xpre_bf = p.sbuf("xpre_bf", [128, 1024], BF16)
    xpT = p.sbuf("xpT", [128, 8, NT, 128], BF16)
    wb_sb = p.sbuf("wb_sb", [128, 8 * 1024], BF16)
    y_all = p.sbuf("y_all", [128, NT * 1024], F32)
    gsc = p.sbuf("gsc", [1, 4], F32)
    grep = p.sbuf("grep", [128, 2], F32)
    acc = p.sbuf("acc", [128, 1024], F32)

    ps_lg = [p.psum(f"ps_lg{i}", [128, 512]) for i in range(2)]
    ps_y = [p.psum(f"ps_y{i}", [128, 512]) for i in range(2)]
    ps_g = p.psum("ps_g", [128, 512])

    blk = p.block
    ldw = p.sem("ldw")
    stg = p.sem("stg")
    tr = p.sem("tr")
    ts = p.sem("ts")
    vs, ss = p.sem("vs"), p.sem("ss")
    xsl = DmaSlots(p, "xld", 2)
    xsl3 = DmaSlots(p, "xld3", 2)
    ysl = DmaSlots(p, "yst", 2)

    @blk.gpsimd
    def _(g):
        ldw.tick(g.dma_start(out=wallT_sb[:].rearrange("p (c n) -> p c n", c=DC),
                             in_=wallT_h.rearrange("(c p) n -> p c n", p=128)), 16)
        ldw.tick(g.dma_start(out=bases_sb[:], in_=bases_h), 16)
        ldw.tick(g.dma_start(out=wb_sb[:].rearrange("p (c n) -> p c n", c=8),
                             in_=wb_h.rearrange("(c p) n -> p c n", p=128)), 16)
        ldw.tick(g.dma_start(out=gsc[:, 0:1], in_=gate_h), 16)
    w_ld = ldw.n

    @blk.vector
    def _(v):
        v.memset(ones_bf[:], 1.0)
        v.memset(ones_f32[:], 1.0)
        v.memset(eps_rms[:], RMS_EPS_FFN1)
        v.memset(eps_ln[:], LN_EPS)
        v.memset(uu[:], 0.0)
        vs.tick(v.memset(vv[:], 0.0))
    init_t = vs.n

    bufs = dict(stats=stats, sq=sq, eps_rms=eps_rms, eps_ln=eps_ln)

    @blk.scalar
    def _(s):
        s.wait_ge(ldw.h, w_ld)
        ss.tick(s.activation(out=gsc[:, 1:2], in_=gsc[:, 0:1], func=AF.Exp,
                             scale=-1.0))
    g_e = ss.n

    @blk.vector
    def _(v):
        v.wait_ge(ss.h, g_e)
        vs.tick(v.tensor_scalar(out=gsc[:, 1:2], in0=gsc[:, 1:2], scalar1=1.0,
                                scalar2=None, op0=ALU.add))
        v.wait_ge(vs.h, vs.n)
        vs.tick(v.reciprocal(gsc[:, 1:2], gsc[:, 1:2]))
        v.wait_ge(vs.h, vs.n)
        vs.tick(v.tensor_scalar(out=gsc[:, 2:3], in0=gsc[:, 1:2], scalar1=0.5,
                                scalar2=None, op0=ALU.mult))
        v.wait_ge(vs.h, vs.n)
        vs.tick(v.tensor_scalar(out=gsc[:, 3:4], in0=gsc[:, 1:2], scalar1=-0.5,
                                scalar2=0.5, op0=ALU.mult, op1=ALU.add))
    g_pair = vs.n

    @blk.tensor
    def _(t):
        t.wait_ge(vs.h, g_pair)
        ts.tick(t.matmul(ps_g[:, 0:2], ones_f32[:], gsc[:, 2:4],
                         start=True, stop=True))
    g_mm = ts.n

    @blk.vector
    def _(v):
        v.wait_ge(ts.h, g_mm)
        vs.tick(v.tensor_copy(grep[:], ps_g[:, 0:2]))
    g_rep = vs.n

    apply_t = [0] * NT
    lg_t = [0] * NT
    z_t = [0] * NT
    pre_t = [0] * NT
    pe_t = [0] * NT
    qe_t = [0] * NT
    post_t = [0] * NT
    xpre_t = [0] * NT
    trL = [0] * NT
    trX = [0] * NT
    y_t = [[0, 0] for _ in range(NT)]
    yev_t = [[0, 0] for _ in range(NT)]
    x_ld = [None] * NT
    xlast_t = [0] * NT

    @blk.gpsimd
    def _(g):
        x_ld[0] = xsl.issue(g, 0, xr[0][:], x_h[0:128, :])
        x_ld[1] = xsl.issue(g, 1, xr[1][:], x_h[128:256, :])

    lnf_t = [0] * NT
    for j in range(NT):
        v_deps = [x_ld[j], (vs, init_t)]
        s_deps = [x_ld[j], (ldw, w_ld)]
        if j >= 1:
            v_deps.append((vs, apply_t[j - 1]))
            s_deps.append((vs, apply_t[j - 1]))
        lnf_deps = []
        if j >= 2:
            lnf_deps.append((vs, xpre_t[j - 2]))
        lnf_t[j] = emit_norm_tile(p, xr[j % 2][:], lnf[j % 2][:], bufs,
                                  v_deps, s_deps, lnf_deps)
        v_deps2 = [(vs, lnf_t[j])]
        s_deps2 = [(vs, lnf_t[j])]
        apply_deps = []
        if j >= 2:
            apply_deps.append((tr, trL[j - 2]))
        apply_t[j] = emit_norm_tile(p, lnf[j % 2][:], lnx[j % 2][:], bufs,
                                    v_deps2, s_deps2, apply_deps, rms_only=True)

        pre = [(vs, apply_t[j])]
        if j >= 2:
            pre.append((ts, lg_t[j - 2]))
        trL[j] = emit_xbar(p, lnxT[j % 2][:], lnx[j % 2][:], pre)

        @blk.tensor
        def _(t, j=j):
            t.wait_ge(tr.h, trL[j])
            t.wait_ge(ldw.h, w_ld)
            if j >= 2:
                t.wait_ge(vs.h, post_t[j - 2])
                t.wait_ge(ss.h, qe_t[j - 2])
            pl = ps_lg[j % 2][:, 0:24]
            for c in range(DC):
                t.matmul(pl, lnxT[j % 2][:, c, :],
                         wallT_sb[:, c * 24:(c + 1) * 24],
                         start=(c == 0), stop=False)
            ts.tick(t.matmul(pl, ones_bf[:, 0:128], bases_sb[:],
                             start=False, stop=True))
        lg_t[j] = ts.n

        @blk.vector
        def _(v, j=j):
            v.wait_ge(ts.h, lg_t[j])
            vs.tick(v.tensor_scalar(out=Z[:, j * 16:(j + 1) * 16],
                                    in0=ps_lg[j % 2][:, 0:16],
                                    scalar1=10.0, scalar2=None, op0=ALU.mult))
        z_t[j] = vs.n

        @blk.vector
        def _(v, j=j):
            v.wait_ge(ts.h, lg_t[j])
            if j >= 1:
                v.wait_ge(ss.h, pe_t[j - 1])
            vs.tick(v.tensor_reduce(smax1[:], ps_lg[j % 2][:, 16:20],
                                    axis=AX.X, op=ALU.max, negate=True))
        pm = vs.n

        @blk.scalar
        def _(s, j=j, pm=pm):
            s.wait_ge(vs.h, pm)
            if j >= 1:
                s.wait_ge(vs.h, pre_t[j - 1])
            ss.tick(s.activation(out=er4a[:], in_=ps_lg[j % 2][:, 16:20],
                                 func=AF.Exp, bias=smax1[:], accum_out=ssum1[:]))
        pe_t[j] = ss.n

        @blk.vector
        def _(v, j=j, pe=pe_t[j]):
            v.wait_ge(ss.h, pe)
            vs.tick(v.reciprocal(rr1[:], ssum1[:]))
            v.wait_ge(vs.h, vs.n)
            vs.tick(v.tensor_scalar(out=Hpre[:, j * 4:(j + 1) * 4], in0=er4a[:],
                                    scalar1=rr1[:], scalar2=None, op0=ALU.mult))
        pre_t[j] = vs.n

        @blk.vector
        def _(v, j=j):
            v.wait_ge(ts.h, lg_t[j])
            if j >= 1:
                v.wait_ge(ss.h, qe_t[j - 1])
            vs.tick(v.tensor_reduce(smax2[:], ps_lg[j % 2][:, 20:24],
                                    axis=AX.X, op=ALU.max, negate=True))
        qm = vs.n

        @blk.scalar
        def _(s, j=j, qm=qm):
            s.wait_ge(vs.h, qm)
            if j >= 1:
                s.wait_ge(vs.h, post_t[j - 1])
            ss.tick(s.activation(out=er4b[:], in_=ps_lg[j % 2][:, 20:24],
                                 func=AF.Exp, bias=smax2[:], accum_out=ssum2[:]))
        qe_t[j] = ss.n

        @blk.vector
        def _(v, j=j):
            v.wait_ge(ss.h, qe_t[j])
            vs.tick(v.reciprocal(rr2[:], ssum2[:]))
            v.wait_ge(vs.h, vs.n)
            vs.tick(v.tensor_scalar(out=Hpost[:, j * 4:(j + 1) * 4], in0=er4b[:],
                                    scalar1=rr2[:], scalar2=0.5,
                                    op0=ALU.mult, op1=ALU.mult))
        post_t[j] = vs.n

        @blk.vector
        def _(v, j=j):
            v.wait_ge(vs.h, pre_t[j])
            if j >= 1:
                v.wait_ge(tr.h, trX[j - 1])
            vs.tick(v.tensor_scalar(out=xpre[:], in0=lnf[j % 2][:, 0:1024],
                                    scalar1=Hpre[:, j * 4:j * 4 + 1],
                                    scalar2=None, op0=ALU.mult))
            for n in range(1, 4):
                v.wait_ge(vs.h, vs.n)
                out_ap = xpre_bf[:] if n == 3 else xpre[:]
                vs.tick(v.scalar_tensor_tensor(
                    out=out_ap, in0=lnf[j % 2][:, n * 1024:(n + 1) * 1024],
                    scalar=Hpre[:, j * 4 + n:j * 4 + n + 1], in1=xpre[:],
                    op0=ALU.mult, op1=ALU.add))
        xpre_t[j] = vs.n
        xlast_t[j] = vs.n

        trX[j] = emit_xbar(p, xpT[:, :, j, :], xpre_bf[:], [(vs, xpre_t[j])])

        if j + 2 < NT:
            @blk.gpsimd
            def _(g, j=j):
                x_ld[j + 2] = xsl.issue(g, j + 2, xr[j % 2][:],
                                        x_h[(j + 2) * 128:(j + 3) * 128, :],
                                        pre_deps=[(vs, lnf_t[j])])

        for oc in range(2):
            @blk.tensor
            def _(t, j=j, oc=oc):
                t.wait_ge(tr.h, trX[j])
                t.wait_ge(ldw.h, w_ld)
                if j >= 1:
                    t.wait_ge(vs.h, yev_t[j - 1][oc])
                for c in range(8):
                    mm = t.matmul(ps_y[oc][:], xpT[:, c, j, :],
                                  wb_sb[:, c * 1024 + oc * 512:c * 1024 + (oc + 1) * 512],
                                  start=(c == 0), stop=(c == 7))
                ts.tick(mm)
            y_t[j][oc] = ts.n

            @blk.vector
            def _(v, j=j, oc=oc):
                v.wait_ge(ts.h, y_t[j][oc])
                vs.tick(v.tensor_copy(
                    y_all[:, j * 1024 + oc * 512:j * 1024 + (oc + 1) * 512],
                    ps_y[oc][:]))
            yev_t[j][oc] = vs.n

    Z4 = Z[:].rearrange("p (t i j) -> p t i j", t=NT, i=4)
    M14 = M1[:].rearrange("p (t i j) -> p t i j", t=NT, i=4)
    M24 = M2[:].rearrange("p (t i j) -> p t i j", t=NT, i=4)
    M2p = M2[:].rearrange("p (t i j) -> p t j i", t=NT, i=4)
    mx3 = mx[:].rearrange("p (t i) -> p t i", t=NT)
    sg3 = sgum[:].rearrange("p (t i) -> p t i", t=NT)
    uu3 = uu[:].rearrange("p (t i) -> p t i", t=NT)
    vv3 = vv[:].rearrange("p (t i) -> p t i", t=NT)
    BCAST = [128, NT, 4, 4]

    sink_start = [(vs, max(z_t)), (vs, init_t)]

    for it in range(50):
        @blk.vector
        def _(v, it=it):
            if it == 0:
                waits(v, sink_start)
            else:
                v.wait_ge(vs.h, vs.n)
            vs.tick(v.tensor_tensor(M14, Z4,
                                    vv3.unsqueeze(2).broadcast_to(BCAST),
                                    op=ALU.add))
            v.wait_ge(vs.h, vs.n)
            vs.tick(v.tensor_reduce(mx3, M14, axis=AX.X, op=ALU.max))
            v.wait_ge(vs.h, vs.n)
            vs.tick(v.tensor_tensor(M14, M14,
                                    mx3.unsqueeze(-1).broadcast_to(BCAST),
                                    op=ALU.subtract))
        m1_t = vs.n

        @blk.scalar
        def _(s, m1_t=m1_t):
            s.wait_ge(vs.h, m1_t)
            ss.tick(s.activation(out=M1[:], in_=M1[:], func=AF.Exp))
        e1_t = ss.n

        @blk.vector
        def _(v, e1_t=e1_t):
            v.wait_ge(ss.h, e1_t)
            vs.tick(v.tensor_reduce(sg3, M14, axis=AX.X, op=ALU.add))
        s1_t = vs.n

        @blk.scalar
        def _(s, s1_t=s1_t):
            s.wait_ge(vs.h, s1_t)
            ss.tick(s.activation(out=sgum[:], in_=sgum[:], func=AF.Ln))
        l1_t = ss.n

        @blk.vector
        def _(v, l1_t=l1_t):
            v.wait_ge(ss.h, l1_t)
            vs.tick(v.scalar_tensor_tensor(out=uu[:], in0=sgum[:], scalar=-1.0,
                                           in1=mx[:], op0=ALU.mult,
                                           op1=ALU.subtract))
            v.wait_ge(vs.h, vs.n)
            vs.tick(v.tensor_tensor(M24, Z4,
                                    uu3.unsqueeze(-1).broadcast_to(BCAST),
                                    op=ALU.add))
            v.wait_ge(vs.h, vs.n)
            vs.tick(v.tensor_reduce(mx3, M2p, axis=AX.X, op=ALU.max))
            v.wait_ge(vs.h, vs.n)
            vs.tick(v.tensor_tensor(M2p, M2p,
                                    mx3.unsqueeze(-1).broadcast_to(BCAST),
                                    op=ALU.subtract))
        m2_t = vs.n

        @blk.scalar
        def _(s, m2_t=m2_t):
            s.wait_ge(vs.h, m2_t)
            ss.tick(s.activation(out=M2[:], in_=M2[:], func=AF.Exp))
        e2_t = ss.n

        @blk.vector
        def _(v, e2_t=e2_t):
            v.wait_ge(ss.h, e2_t)
            vs.tick(v.tensor_reduce(sg3, M2p, axis=AX.X, op=ALU.add))
        s2_t = vs.n

        @blk.scalar
        def _(s, s2_t=s2_t):
            s.wait_ge(vs.h, s2_t)
            ss.tick(s.activation(out=sgum[:], in_=sgum[:], func=AF.Ln))
        l2_t = ss.n

        @blk.vector
        def _(v, l2_t=l2_t):
            v.wait_ge(ss.h, l2_t)
            vs.tick(v.scalar_tensor_tensor(out=vv[:], in0=sgum[:], scalar=-1.0,
                                           in1=mx[:], op0=ALU.mult,
                                           op1=ALU.subtract))
    @blk.vector
    def _(v):
        v.wait_ge(vs.h, vs.n)
        vs.tick(v.tensor_tensor(M14, Z4, uu3.unsqueeze(-1).broadcast_to(BCAST),
                                op=ALU.add))
        v.wait_ge(vs.h, vs.n)
        vs.tick(v.tensor_tensor(M14, M14, vv3.unsqueeze(2).broadcast_to(BCAST),
                                op=ALU.add))
    hsum_t = vs.n

    @blk.scalar
    def _(s):
        s.wait_ge(vs.h, hsum_t)
        ss.tick(s.activation(out=M1[:], in_=M1[:], func=AF.Exp))
    he_t = ss.n

    @blk.vector
    def _(v):
        v.wait_ge(ss.h, he_t)
        v.wait_ge(vs.h, g_rep)
        vs.tick(v.tensor_scalar(out=M1[:], in0=M1[:], scalar1=grep[:, 0:1],
                                scalar2=None, op0=ALU.mult))
        v.wait_ge(vs.h, vs.n)
        M116 = M1[:].rearrange("p (t d) -> p t d", t=NT)
        last = None
        for k in (0, 5, 10, 15):
            last = v.tensor_scalar(out=M116[:, :, k:k + 1],
                                   in0=M116[:, :, k:k + 1],
                                   scalar1=grep[:, 1:2], scalar2=None,
                                   op0=ALU.add)
        vs.tick(last)
    H_t = vs.n

    x3_ld = [None] * NT
    mix_t = [0] * NT

    @blk.gpsimd
    def _(g):
        x3_ld[0] = xsl3.issue(g, 0, xr[0][:], x_h[0:128, :],
                              pre_deps=[(vs, lnf_t[NT - 2])])
        x3_ld[1] = xsl3.issue(g, 1, xr[1][:], x_h[128:256, :],
                              pre_deps=[(vs, lnf_t[NT - 1])])

    st_t = [None] * NT
    ln3_t = [0] * NT
    for j in range(NT):
        v_deps = [x3_ld[j], (vs, H_t)]
        s_deps = [x3_ld[j], (vs, H_t)]
        lnf_deps = []
        if j >= 2:
            lnf_deps.append((vs, mix_t[j - 2]))
        ln3_t[j] = emit_norm_tile(p, xr[j % 2][:], lnf[j % 2][:], bufs,
                                  v_deps, s_deps, lnf_deps)

        @blk.vector
        def _(v, j=j):
            v.wait_ge(vs.h, ln3_t[j])
            v.wait_ge(vs.h, max(yev_t[j]))
            for i in range(4):
                v.wait_ge(vs.h, vs.n)
                vs.tick(v.tensor_scalar(out=acc[:],
                                        in0=lnf[j % 2][:, 0:1024],
                                        scalar1=M1[:, j * 16 + i * 4:j * 16 + i * 4 + 1],
                                        scalar2=None, op0=ALU.mult))
                for n in range(1, 4):
                    v.wait_ge(vs.h, vs.n)
                    vs.tick(v.scalar_tensor_tensor(
                        out=acc[:], in0=lnf[j % 2][:, n * 1024:(n + 1) * 1024],
                        scalar=M1[:, j * 16 + i * 4 + n:j * 16 + i * 4 + n + 1],
                        in1=acc[:], op0=ALU.mult, op1=ALU.add))
                v.wait_ge(vs.h, vs.n)
                vs.tick(v.scalar_tensor_tensor(
                    out=acc[:], in0=y_all[:, j * 1024:(j + 1) * 1024],
                    scalar=Hpost[:, j * 4 + i:j * 4 + i + 1],
                    in1=acc[:], op0=ALU.mult, op1=ALU.add))
                v.wait_ge(vs.h, vs.n)
                xa_i = xr[j % 2][:, i * 1024:(i + 1) * 1024]
                vs.tick(v.tensor_tensor(xa_i, xa_i, acc[:], op=ALU.add))
        mix_t[j] = vs.n

        @blk.scalar
        def _(s_, j=j):
            st_t[j] = ysl.issue(s_, j, y_h[j * 128:(j + 1) * 128, :],
                                xr[j % 2][:], pre_deps=[(vs, mix_t[j])])

        if j + 2 < NT:
            @blk.gpsimd
            def _(g, j=j):
                x3_ld[j + 2] = xsl3.issue(g, j + 2, xr[j % 2][:],
                                          x_h[(j + 2) * 128:(j + 3) * 128, :],
                                          pre_deps=[st_t[j]])

    if ctx is None:
        @blk.scalar
        def _(g):
            for s_ in ysl.sems:
                if s_.n:
                    g.wait_ge(s_.h, s_.n)
        p.phase_close()
        return p.finish()
    ctx["stores"] = [(s_, s_.n) for s_ in ysl.sems if s_.n]
    return



QKV_SS = 256
NSTRIP = 12288 // QKV_SS


def build_qkv0(ctx=None):
    if ctx is None:
        p = Prog()
        nc = p.nc
        x_h = nc.dram_tensor("x", [TOK, D], F32, kind="ExternalInput").ap()
        wqkv_h = nc.dram_tensor("wqkv", [128, 96, DC + 1, 128], F32,
                                kind="ExternalInput").ap()
        gq_h = nc.dram_tensor("gq", [128, 1], F32, kind="ExternalInput").ap()
        gk_h = nc.dram_tensor("gk", [128, 1], F32, kind="ExternalInput").ap()
        qT_o = nc.dram_tensor("qT", [32, 128, TOK], BF16,
                              kind="ExternalOutput").ap()
        kT_o = nc.dram_tensor("kT", [32, 128, TOK], BF16,
                              kind="ExternalOutput").ap()
        v_o = nc.dram_tensor("v", [TOK, D], BF16, kind="ExternalOutput").ap()
        ngroups = 1
        p.start_block()
        emit_act_table_pin(p)
        p.phase_open()
    else:
        p = ctx["p"]
        x_h, wqkv_h = ctx["x"], ctx["wqkv"]
        gq_h, gk_h = ctx["gq"], ctx["gk"]
        qT_o, kT_o, v_o = ctx["qT"], ctx["kT"], ctx["v"]
        ngroups = ctx.get("ngroups", 1)

    ntile = 8

    xr = [p.sbuf(f"xr{i}", [128, D], F32) for i in range(2)]
    lnx = [p.sbuf(f"lnx{i}", [128, D], BF16) for i in range(2)]
    lnxT = p.sbuf("lnxT_full", [128, DC, 1024], BF16)
    sq = p.sbuf("sq", [128, D], BF16)
    stats = p.sbuf("stats", [128, 4], F32)
    gktmp = p.sbuf("gktmp", [128, 1], F32)
    eps_ln = p.sbuf("eps_ln", [128, 1], F32)
    eps_qk = p.sbuf("eps_qk", [128, 1], F32)
    ones_bf = p.sbuf("ones_bf", [1, 512], BF16)
    ones_col = p.sbuf("ones_col", [128, 1], BF16)
    ones128 = p.sbuf("ones128", [128, 128], BF16)
    ones_f32 = p.sbuf("ones_f32", [1, 128], F32)
    gqks = p.sbuf("gqks", [128, 1], F32)
    one_col_f = p.sbuf("one_col_f", [128, 1], F32)
    wcol = [p.sbuf(f"wcol{i}", [128, DC + 1, 128], BF16) for i in range(2)]
    q2f = [p.sbuf(f"q2f{i}", [128, 1024], BF16) for i in range(2)]
    rstd = [p.sbuf(f"rstd{i}", [128, 1024], F32) for i in range(2)]
    ostage = [p.sbuf(f"ostage{i}", [128, 1024], BF16) for i in range(2)]
    vostage = [p.sbuf(f"vostage{i}", [128, 1024], BF16) for i in range(2)]
    vstage = [p.sbuf(f"vstage{i}", [128, ntile, 128], BF16) for i in range(2)]

    ps_fm = [p.psum(f"ps_fm{i}", [128, 1024]) for i in range(2)]
    ps_n = [p.psum(f"ps_n{i}", [128, 1024]) for i in range(2)]

    blk = p.block
    ldw = p.sem("ldw")
    tr = p.sem("tr")
    ts = p.sem("ts")
    vs, ss = p.sem("vs"), p.sem("ss")
    xsl = DmaSlots(p, "xld", 2)
    wsl = DmaSlots(p, "wld", 2)
    osl = DmaSlots(p, "osl", 4)

    @blk.gpsimd
    def _(g):
        ldw.tick(g.dma_start(out=gqks[:], in_=gq_h), 16)
        ldw.tick(g.dma_start(out=gktmp[:], in_=gk_h), 16)
    w_ld = ldw.n

    @blk.vector
    def _(v):
        v.memset(ones_bf[:], 1.0)
        v.memset(ones_col[:], 1.0)
        v.memset(ones128[:], 1.0)
        v.memset(ones_f32[:], 1.0)
        v.memset(one_col_f[:], 1.0)
        v.memset(eps_ln[:], LN_EPS)
        vs.tick(v.memset(eps_qk[:], QK_EPS))
        v.wait_ge(ldw.h, w_ld)
        vs.tick(v.tensor_tensor(gqks[:], gqks[:], gktmp[:], op=ALU.mult))
        v.wait_ge(vs.h, vs.n)
        vs.tick(v.tensor_scalar(out=gqks[:], in0=gqks[:], scalar1=INV_SQRT_HD,
                                scalar2=None, op0=ALU.mult))
    init_t = vs.n

    bufs = dict(stats=stats, sq=sq, eps_ln=eps_ln)

    apply_prev = [0]
    w_ld_t = [None, None]
    trL_hist = []
    gi = [0]
    mm_hist = {}
    ev_hist = {}
    nmm_hist = {}
    rstd_hist = {}
    rep_hist = {}
    repv_hist = {}
    sq_prev = [0]
    last_ost = [None, None]
    last_vxb = [None, None]
    last_vst = [None, None]
    x_ld_prev = [None, None]
    last_group_mm = [0]

    for grp in range(ngroups):
        trL = [0] * ntile
        apply_t = [0] * ntile
        x_ld = [None] * ntile

        @blk.gpsimd
        def _(g, grp=grp):
            for jj in range(2):
                jg = grp * ntile + jj
                pre = []
                if x_ld_prev[jj] is not None:
                    pre = [(vs, apply_prev[0])]
                x_ld[jj] = xsl.issue(g, jj, xr[jj][:],
                                     x_h[jg * 128:(jg + 1) * 128, :],
                                     pre_deps=pre)
                x_ld_prev[jj] = x_ld[jj]

        for j in range(ntile):
            jg = grp * ntile + j
            v_deps = [x_ld[j], (vs, init_t)]
            s_deps = [x_ld[j]]
            if j >= 1 or grp >= 1:
                v_deps.append((vs, apply_prev[0]))
                s_deps.append((vs, apply_prev[0]))
            if grp >= 1 and j == 0:
                s_deps.append((vs, sq_prev[0]))
                s_deps.append((ss, rstd_hist.get(gi[0] - 1, 0)))
            apply_deps = []
            if len(trL_hist) >= 2:
                apply_deps.append((tr, trL_hist[-2]))
            apply_t[j] = emit_norm_tile(p, xr[j % 2][:], lnx[j % 2][:], bufs,
                                        v_deps, s_deps, apply_deps)
            apply_prev[0] = apply_t[j]
            pre = [(vs, apply_t[j])]
            if grp >= 1 and j == 0:
                pre.append((ts, last_group_mm[0]))
            trL[j] = emit_xbar(p, lnxT[:, :, j * 128:(j + 1) * 128],
                               lnx[j % 2][:], pre)
            trL_hist.append(trL[j])
            if j + 2 < ntile:
                @blk.gpsimd
                def _(g, j=j, grp=grp):
                    jg2 = grp * ntile + j + 2
                    x_ld[j + 2] = xsl.issue(g, j + 2, xr[j % 2][:],
                                            x_h[jg2 * 128:(jg2 + 1) * 128, :],
                                            pre_deps=[(vs, apply_t[j])])

        heavies = (list(range(0, 32)) + list(range(32, 64))) if grp == 0 \
            else list(range(32, 64))
        lights = list(range(64, 96))
        oc_list = []
        hi = vi = 0
        while hi < len(heavies) or vi < len(lights):
            if hi < len(heavies):
                oc_list.append(heavies[hi]); hi += 1
            if vi < len(lights):
                oc_list.append(lights[vi]); vi += 1
            if hi < len(heavies):
                oc_list.append(heavies[hi]); hi += 1
        for oc in oc_list:
            kind = "q" if oc < 32 else ("k" if oc < 64 else "v")
            g_i = gi[0]
            gi[0] += 1

            @blk.gpsimd
            def _(g, oc=oc, g_i=g_i):
                w_ld_t[g_i % 2] = wsl.issue(
                    g, g_i, wcol[g_i % 2][:], wqkv_h[:, oc],
                    pre_deps=[(ts, mm_hist.get(g_i - 2, 0))])

            @blk.tensor
            def _(t, oc=oc, g_i=g_i, grp=grp):
                waits(t, [w_ld_t[g_i % 2], (tr, trL[ntile - 1])])
                if g_i >= 2:
                    t.wait_ge(vs.h, ev_hist.get(g_i - 2, 0))
                pb = ps_fm[g_i % 2]
                for c in range(DC):
                    t.matmul(pb[:, 0:512], wcol[g_i % 2][:, c, :],
                             lnxT[:, c, 0:512], start=(c == 0), stop=False)
                    t.matmul(pb[:, 512:1024], wcol[g_i % 2][:, c, :],
                             lnxT[:, c, 512:1024], start=(c == 0), stop=False)
                t.matmul(pb[:, 0:512], wcol[g_i % 2][0:1, DC, :],
                         ones_bf[:, 0:512], start=False, stop=True)
                ts.tick(t.matmul(pb[:, 512:1024], wcol[g_i % 2][0:1, DC, :],
                                 ones_bf[:, 0:512], start=False, stop=True))
            mm_hist[g_i] = ts.n
            last_group_mm[0] = ts.n

            if kind in ("q", "k"):
                @blk.scalar
                def _(s, g_i=g_i):
                    s.wait_ge(ts.h, mm_hist[g_i])
                    if g_i >= 2:
                        s.wait_ge(ts.h, nmm_hist.get(g_i - 2, 0))
                    ss.tick(s.activation(out=q2f[g_i % 2][:],
                                         in_=ps_fm[g_i % 2][:],
                                         func=AF.Square))
                sq_hist = ss.n
                sq_prev[0] = ss.n

                @blk.tensor
                def _(t, g_i=g_i, sq_hist=sq_hist):
                    t.wait_ge(ss.h, sq_hist)
                    if g_i >= 2:
                        t.wait_ge(ss.h, rstd_hist.get(g_i - 2, 0))
                    pn = ps_n[g_i % 2]
                    t.matmul(pn[:, 0:512], ones128[:],
                             q2f[g_i % 2][:, 0:512], start=True, stop=True)
                    ts.tick(t.matmul(pn[:, 512:1024], ones128[:],
                                     q2f[g_i % 2][:, 512:1024],
                                     start=True, stop=True))
                nmm_hist[g_i] = ts.n

                @blk.scalar
                def _(s, g_i=g_i):
                    s.wait_ge(ts.h, nmm_hist[g_i])
                    if g_i >= 2:
                        s.wait_ge(vs.h, ev_hist.get(g_i - 2, 0))
                    ss.tick(s.activation(out=rstd[g_i % 2][:],
                                          in_=ps_n[g_i % 2][:],
                                          func=AF.Ln, bias=eps_qk[:],
                                          scale=1.0 / 128))
                    s.wait_ge(ss.h, ss.n)
                    ss.tick(s.activation(out=rstd[g_i % 2][:],
                                         in_=rstd[g_i % 2][:],
                                         func=AF.Exp, scale=-0.5))
                rstd_hist[g_i] = ss.n

                @blk.vector
                def _(v, g_i=g_i, kind=kind):
                    v.wait_ge(ss.h, rstd_hist[g_i])
                    if last_ost[g_i % 2] is not None:
                        waits(v, [last_ost[g_i % 2]])
                    vs.tick(v.scalar_tensor_tensor(
                        out=ostage[g_i % 2][:], in0=ps_fm[g_i % 2][:],
                        scalar=(gqks[:] if kind == "k" else one_col_f[:]),
                        in1=rstd[g_i % 2][:], op0=ALU.mult, op1=ALU.mult))
                ev_hist[g_i] = vs.n

                @blk.scalar
                def _(s_, oc=oc, g_i=g_i, kind=kind, grp=grp):
                    dst = qT_o if kind == "q" else kT_o
                    hh = oc if kind == "q" else oc - 32
                    last_ost[g_i % 2] = osl.issue(
                        s_, g_i, dst[hh][:, grp * 1024:(grp + 1) * 1024],
                        ostage[g_i % 2][:],
                        pre_deps=[(vs, ev_hist[g_i])])
            else:
                @blk.vector
                def _(v, g_i=g_i):
                    v.wait_ge(ts.h, mm_hist[g_i])
                    if last_vxb[g_i % 2] is not None:
                        waits(v, [last_vxb[g_i % 2]])
                    vs.tick(v.tensor_copy(vostage[g_i % 2][:],
                                          ps_fm[g_i % 2][:]))
                ev_hist[g_i] = vs.n

                pre = [(vs, ev_hist[g_i])]
                if last_vst[g_i % 2] is not None:
                    pre.append(last_vst[g_i % 2])
                tr_n = emit_xbar(p, vstage[g_i % 2][:], vostage[g_i % 2][:], pre)
                last_vxb[g_i % 2] = (tr, tr_n)

                @blk.scalar
                def _(s_, oc=oc, g_i=g_i, tr_n=tr_n, grp=grp):
                    s_.wait_ge(tr.h, tr_n)
                    vd = (oc - 64) * 128
                    r0 = grp * 1024
                    last_vst[g_i % 2] = osl.issue(
                        s_, g_i, v_o[r0:r0 + 1024, vd:vd + 128].rearrange(
                            "(c p) d -> p c d", p=128),
                        vstage[g_i % 2][:])

    if ctx is None:
        @blk.gpsimd
        def _(g):
            for s_ in osl.sems:
                if s_.n:
                    g.wait_ge(s_.h, s_.n)
        p.phase_close()
        return p.finish()
    ctx["stores"] = [(s_, s_.n) for s_ in osl.sems if s_.n]
    return



def emit_cbound_kmax(p, kh_ap, k2, cb, ones_col, ps_misc, Tlen, deps, war_deps):
    blk = p.block
    vs, ss, ts = p.sem("vs"), p.sem("ss"), p.sem("ts")
    nch = Tlen // 512

    @blk.scalar
    def _(s):
        waits(s, deps + war_deps)
        ss.tick(s.activation(out=k2[:, 0:Tlen], in_=kh_ap, func=AF.Square))
    sq_t = ss.n

    red_ts = []
    for ch in range(nch):
        @blk.tensor
        def _(t, ch=ch):
            t.wait_ge(ss.h, sq_t)
            if ch == 0:
                waits(t, war_deps)
            else:
                t.wait_ge(vs.h, red_ts[-1])
            ts.tick(t.matmul(ps_misc[0:1, 0:512], ones_col[:],
                             k2[:, ch * 512:(ch + 1) * 512],
                             start=True, stop=True))
        mmn = ts.n

        @blk.vector
        def _(v, ch=ch, mmn=mmn):
            v.wait_ge(ts.h, mmn)
            vs.tick(v.tensor_reduce(cb[0:1, ch:ch + 1], ps_misc[0:1, 0:512],
                                    axis=AX.X, op=ALU.max))
        red_ts.append(vs.n)

    @blk.vector
    def _(v):
        v.wait_ge(vs.h, red_ts[-1])
        vs.tick(v.tensor_reduce(cb[0:1, 13:14], cb[0:1, 0:nch],
                                axis=AX.X, op=ALU.max))
    return vs.n


def emit_cbound_q(p, qh_ap, q2, cb, ones_col, ps_misc, lnsc, ones_f32,
                  negC_rep, deps, war_deps, kmax_t):
    blk = p.block
    vs, ss, ts = p.sem("vs"), p.sem("ss"), p.sem("ts")

    @blk.scalar
    def _(s):
        waits(s, deps + war_deps)
        ss.tick(s.activation(out=q2[:], in_=qh_ap, func=AF.Square))
    sq_t = ss.n

    red_ts = []
    for ch in range(2):
        @blk.tensor
        def _(t, ch=ch):
            t.wait_ge(ss.h, sq_t)
            if ch == 0:
                waits(t, war_deps)
                t.wait_ge(vs.h, kmax_t)
            else:
                t.wait_ge(vs.h, red_ts[-1])
            ts.tick(t.matmul(ps_misc[0:1, 0:512], ones_col[:],
                             q2[:, ch * 512:(ch + 1) * 512],
                             start=True, stop=True))
        mmn = ts.n

        @blk.vector
        def _(v, ch=ch, mmn=mmn):
            v.wait_ge(ts.h, mmn)
            vs.tick(v.tensor_reduce(cb[0:1, 10 + ch:11 + ch],
                                    ps_misc[0:1, 0:512], axis=AX.X, op=ALU.max))
        red_ts.append(vs.n)

    @blk.vector
    def _(v):
        v.wait_ge(vs.h, red_ts[-1])
        vs.tick(v.tensor_reduce(cb[0:1, 12:13], cb[0:1, 10:12],
                                axis=AX.X, op=ALU.max))
        v.wait_ge(vs.h, vs.n)
        v.wait_ge(vs.h, kmax_t)
        vs.tick(v.tensor_tensor(cb[0:1, 14:15], cb[0:1, 12:13], cb[0:1, 13:14],
                                op=ALU.mult))
    m2_t = vs.n

    @blk.scalar
    def _(s):
        s.wait_ge(vs.h, m2_t)
        ss.tick(s.activation(out=cb[0:1, 15:16], in_=cb[0:1, 14:15], func=AF.Ln))
        s.wait_ge(ss.h, ss.n)
        ss.tick(s.activation(out=cb[0:1, 15:16], in_=cb[0:1, 15:16],
                             func=AF.Exp, scale=0.5, bias=lnsc[:]))
    c_t = ss.n

    @blk.vector
    def _(v):
        v.wait_ge(ss.h, c_t)
        vs.tick(v.tensor_scalar(out=cb[0:1, 15:16], in0=cb[0:1, 15:16],
                                scalar1=-C_MARGIN, scalar2=0.0,
                                op0=ALU.add, op1=ALU.max))
        v.wait_ge(vs.h, vs.n)
        vs.tick(v.tensor_scalar(out=cb[0:1, 15:16], in0=cb[0:1, 15:16],
                                scalar1=-1.0, scalar2=None, op0=ALU.mult))
    neg_t = vs.n

    @blk.tensor
    def _(t):
        t.wait_ge(vs.h, neg_t)
        ts.tick(t.matmul(ps_misc[:, 0:1], ones_f32[:], cb[0:1, 15:16],
                         start=True, stop=True))
    rep_mm = ts.n

    @blk.vector
    def _(v):
        v.wait_ge(ts.h, rep_mm)
        waits(v, war_deps)
        vs.tick(v.tensor_copy(negC_rep[:], ps_misc[:, 0:1]))
    return vs.n, rep_mm


def emit_head(p, b, qh_full, kh_sb, vh_sb, nTT, exp_scale, negC_t, ctx_dsts,
              q_deps, k_deps, v_deps, ctx_war=()):
    blk = p.block
    vs, ss, ts = p.sem("vs"), p.sem("ss"), p.sem("ts")
    ps_s, ps_ctx, ps_l = b["ps_s2"], b["ps_ctx2"], b["ps_l2"]
    pT = b["pT2"]
    st = b["state"]

    s_t = [0] * nTT
    exp_t = [0] * nTT
    av_t = [0] * nTT
    pair_t = [0] * nTT
    lmm_t = [0] * nTT
    pl_pair = b["pl_pair"]

    def emit_S(tt):
        @blk.tensor
        def _(t, tt=tt):
            if tt == 0:
                waits(t, q_deps + k_deps)
            if tt >= 2:
                t.wait_ge(ss.h, exp_t[tt - 2])
            elif st["last_exp"]:
                t.wait_ge(ss.h, st["last_exp"])
            kT_tile = kh_sb[:, tt * 128:(tt + 1) * 128]
            t.matmul(ps_s[tt % 2][:, 0:512], kT_tile, qh_full[:, 0:512],
                     start=True, stop=True)
            ts.tick(t.matmul(ps_s[tt % 2][:, 512:1024], kT_tile,
                             qh_full[:, 512:1024], start=True, stop=True))
        s_t[tt] = ts.n

        @blk.scalar
        def _(s, tt=tt):
            s.wait_ge(ts.h, s_t[tt])
            s.wait_ge(vs.h, negC_t)
            if tt >= 2:
                s.wait_ge(ts.h, av_t[tt - 2])
                s.wait_ge(vs.h, pair_t[tt - 1])
            elif st["last_av"]:
                s.wait_ge(ts.h, st["last_av"])
                if st["last_pair"]:
                    s.wait_ge(vs.h, st["last_pair"])
            ss.tick(s.activation(out=pT[tt % 2][:], in_=ps_s[tt % 2][:],
                                 func=AF.Exp, bias=b["negC_rep"][:],
                                 scale=exp_scale))
        exp_t[tt] = ss.n

        if tt % 2 == 1:
            @blk.vector
            def _(v, tt=tt):
                v.wait_ge(ss.h, exp_t[tt])
                if tt >= 3:
                    v.wait_ge(ts.h, lmm_t[tt - 2])
                elif st["last_lmm"]:
                    v.wait_ge(ts.h, st["last_lmm"])
                vs.tick(v.tensor_tensor(pl_pair[(tt // 2) % 2][:],
                                        pT[0][:], pT[1][:], op=ALU.add))
            pair_t[tt] = vs.n

    def emit_AV(tt):
        @blk.tensor
        def _(t, tt=tt):
            t.wait_ge(ss.h, exp_t[tt])
            if tt == 0:
                waits(t, v_deps)
                if st["last_ctxev"]:
                    t.wait_ge(vs.h, st["last_ctxev"])
                if st["last_lev"]:
                    t.wait_ge(vs.h, st["last_lev"])
            v_tile = vh_sb[:, tt * 128:(tt + 1) * 128]
            first, last = tt == 0, tt == nTT - 1
            t.matmul(ps_ctx[0][:, 0:512], v_tile, pT[tt % 2][:, 0:512],
                     start=first, stop=last)
            mm = t.matmul(ps_ctx[1][:, 0:512], v_tile, pT[tt % 2][:, 512:1024],
                          start=first, stop=last)
            if tt % 2 == 0 and tt >= 2:
                t.wait_ge(vs.h, pair_t[tt - 1])
                sl = ((tt - 1) // 2) % 2
                t.matmul(ps_l[0][0:1, 0:512], b["ones_col"][:],
                         pl_pair[sl][:, 0:512], start=(tt == 2), stop=False)
                mm = t.matmul(ps_l[1][0:1, 0:512], b["ones_col"][:],
                              pl_pair[sl][:, 512:1024],
                              start=(tt == 2), stop=False)
            ts.tick(mm)
        av_t[tt] = ts.n
        if tt % 2 == 0 and tt >= 2:
            lmm_t[tt - 1] = ts.n

    for tt in range(nTT + 1):
        if tt < nTT:
            emit_S(tt)
        if tt >= 1:
            emit_AV(tt - 1)

    assert nTT >= 4 and nTT % 2 == 0

    @blk.tensor
    def _(t):
        t.wait_ge(vs.h, pair_t[nTT - 1])
        sl = ((nTT - 1) // 2) % 2
        t.matmul(ps_l[0][0:1, 0:512], b["ones_col"][:], pl_pair[sl][:, 0:512],
                 start=False, stop=True)
        ts.tick(t.matmul(ps_l[1][0:1, 0:512], b["ones_col"][:],
                         pl_pair[sl][:, 512:1024], start=False, stop=True))
    lmm_t[nTT - 1] = ts.n

    st["last_exp"] = exp_t[-1]
    st["last_av"] = av_t[-1]
    st["last_pair"] = pair_t[-1]
    st["last_lmm"] = lmm_t[-1]

    ctxev = []
    for qc in range(2):
        @blk.vector
        def _(v, qc=qc):
            v.wait_ge(ts.h, lmm_t[nTT - 1])
            vs.tick(v.tensor_scalar(out=b["lv"][0:1, :], in0=ps_l[qc][0:1, 0:512],
                                    scalar1=1e-30, scalar2=None, op0=ALU.max))
            v.wait_ge(vs.h, vs.n)
            vs.tick(v.reciprocal(b["linv"][0:1, :], b["lv"][0:1, :]))
        lev = vs.n
        st["last_lev"] = lev

        @blk.tensor
        def _(t, qc=qc, lev=lev):
            t.wait_ge(vs.h, lev)
            if st["last_rrep"]:
                t.wait_ge(vs.h, st["last_rrep"])
            ts.tick(t.matmul(ps_s[0][:, qc * 512:(qc + 1) * 512],
                             b["ones_f32"][:], b["linv"][0:1, :],
                             start=True, stop=True))
        rep_mm = ts.n

        @blk.vector
        def _(v, qc=qc, rep_mm=rep_mm):
            v.wait_ge(ts.h, rep_mm)
            vs.tick(v.tensor_copy(b["r_rep"][:], ps_s[0][:, qc * 512:(qc + 1) * 512]))
            v.wait_ge(vs.h, vs.n)
            waits(v, ctx_war)
            vs.tick(v.tensor_tensor(ctx_dsts[qc], ps_ctx[qc][:, 0:512],
                                    b["r_rep"][:], op=ALU.mult))
        st["last_rrep"] = vs.n - 1
        st["last_ctxev"] = vs.n
        ctxev.append(vs.n)
    return dict(ctxev=ctxev, last_ts=ts.n)


def emit_outproj(p, b, w_h, bias_sb, ctx_chunk_fn, x_h, y_h, has_bias,
                 phase_deps):
    blk = p.block
    nc = p.nc
    vs, ss, ts = p.sem("vs"), p.sem("ss"), p.sem("ts")
    stg = p.sem("stg")
    x_grp = b["x_grp"]
    wo_sb = b["wo_sb"]
    ps_o = b["ps_o"]
    ones_bf = b["ones_bf"]
    xgsl = DmaSlots(p, "xgsl", 2)
    wosl = DmaSlots(p, "wosl", 2)
    ysl = DmaSlots(p, "ysl", 4)

    strip_mm_done = {}
    ev_done = {}
    st_done = {}
    for g in range(2):
        x_ld = [None] * 4

        @blk.scalar
        def _(gq, g=g):
            for tl in range(4):
                j = g * 4 + tl
                pre = list(phase_deps) if (g == 0 and tl < 2) else []
                if g == 1:
                    pre.append(st_done[(0, tl)])
                x_ld[tl] = xgsl.issue(gq, tl % 2, x_grp[:, tl * D:(tl + 1) * D],
                                      x_h[j * 128:(j + 1) * 128, :],
                                      pre_deps=pre)
        get_lhsT, ctx_deps = ctx_chunk_fn(g, phase_deps)

        for oc in range(8):
            for cb4 in range(4):
                sid = (g * 8 + oc) * 4 + cb4

                @blk.gpsimd
                def _(gq, oc=oc, cb4=cb4, sid=sid):
                    pre = list(phase_deps) if sid < 2 else \
                        [(ts, strip_mm_done.get(sid - 2, 0))]
                    b["wo_ld"][sid] = wosl.issue(
                        gq, sid, wo_sb[sid % 2][:], w_h[:, cb4, oc],
                        pre_deps=pre)

                for tl in range(4):
                    @blk.tensor
                    def _(t, g=g, oc=oc, cb4=cb4, tl=tl, sid=sid):
                        waits(t, [b["wo_ld"][sid]] + ctx_deps)
                        if cb4 == 0:
                            prev = ev_done.get((g, tl, oc - 1)) if oc >= 1 else \
                                (ev_done.get((g - 1, tl, 7)) if g >= 1 else None)
                            if prev:
                                t.wait_ge(vs.h, prev)
                        mm = None
                        for c8 in range(8):
                            c = cb4 * 8 + c8
                            last = (cb4 == 3 and c8 == 7 and not has_bias)
                            mm = t.matmul(ps_o[tl][:, 0:512], get_lhsT(c, tl),
                                          wo_sb[sid % 2][:, c8, :],
                                          start=(cb4 == 0 and c8 == 0),
                                          stop=last)
                        if cb4 == 3 and has_bias:
                            mm = t.matmul(ps_o[tl][:, 0:512], ones_bf[:, 0:128],
                                          bias_sb[0:1, oc * 512:(oc + 1) * 512],
                                          start=False, stop=True)
                        ts.tick(mm)
                    strip_mm_done[sid] = ts.n
                    if cb4 == 3:
                        mmn = ts.n

                        @blk.vector
                        def _(v, g=g, oc=oc, tl=tl, mmn=mmn):
                            v.wait_ge(ts.h, mmn)
                            waits(v, [x_ld[tl]])
                            xa = x_grp[:, tl * D + oc * 512:tl * D + (oc + 1) * 512]
                            vs.tick(v.tensor_tensor(xa, xa, ps_o[tl][:, 0:512],
                                                    op=ALU.add))
                        ev_done[(g, tl, oc)] = vs.n

        for tl in range(4):
            @blk.scalar
            def _(gq, g=g, tl=tl):
                j = g * 4 + tl
                st_done[(g, tl)] = ysl.issue(
                    gq, j % 4, y_h[j * 128:(j + 1) * 128, :],
                    x_grp[:, tl * D:(tl + 1) * D],
                    pre_deps=[(vs, ev_done[(g, tl, 7)])])

    @blk.scalar
    def _(gq):
        for s_ in ysl.sems:
            if s_.n:
                gq.wait_ge(s_.h, s_.n)
    return [(s_, s_.n) for s_ in ysl.sems if s_.n]


def build_attn0(ctx=None):
    if ctx is None:
        p = Prog()
        nc = p.nc
        x_h = nc.dram_tensor("x", [TOK, D], F32, kind="ExternalInput").ap()
        qT_h = nc.dram_tensor("qT", [32, 128, TOK], BF16,
                              kind="ExternalInput").ap()
        kT_h = nc.dram_tensor("kT", [32, 128, 4096], BF16,
                              kind="ExternalInput").ap()
        vf_h = nc.dram_tensor("vf", [4096, D], BF16, kind="ExternalInput").ap()
        wo_h = nc.dram_tensor("wo", [128, 4, 8, 8, 512], F32,
                              kind="ExternalInput").ap()
        bo_h = nc.dram_tensor("bo", [1, D], F32, kind="ExternalInput").ap()
        y_h = nc.dram_tensor("y", [TOK, D], F32, kind="ExternalOutput").ap()
        kT_ag = None
        p.start_block()
        emit_act_table_pin(p)
        p.phase_open()
    else:
        p = ctx["p"]
        x_h, qT_h, vf_h = ctx["x"], ctx["qT"], ctx["vf"]
        kT_h = ctx.get("kT3")
        kT_ag = ctx.get("kT_ag")
        wo_h, bo_h, y_h = ctx["wo"], ctx["bo"], ctx["y"]

    ctxT = p.sbuf("ctxT", [128, 32 * TOK], BF16)
    ones_bf = p.sbuf("ones_bf", [1, 128], BF16)
    ones_col = p.sbuf("ones_col", [128, 1], BF16)
    ones_f32 = p.sbuf("ones_f32", [1, 128], F32)
    lnsc = p.sbuf("lnsc", [1, 1], F32)
    cb = p.sbuf("cb", [1, 16], F32)
    negC_rep = p.sbuf("negC_rep", [128, 1], F32)
    lv = p.sbuf("lv", [1, 512], F32)
    linv = p.sbuf("linv", [1, 512], F32)
    r_rep = p.sbuf("r_rep", [128, 512], F32)
    bo_sb = p.sbuf("bo_sb", [1, D], BF16)

    blk = p.block
    ldw = p.sem("ldw")
    vs, ss, ts = p.sem("vs"), p.sem("ss"), p.sem("ts")
    qsl = DmaSlots(p, "qsl", 2)
    ksl = DmaSlots(p, "ksl", 2)
    vsl = DmaSlots(p, "vsl", 2)

    @blk.gpsimd
    def _(g):
        ldw.tick(g.dma_start(out=bo_sb[:], in_=bo_h), 16)
    w_ld = ldw.n

    @blk.vector
    def _(v):
        v.memset(ones_bf[:], 1.0)
        v.memset(ones_col[:], 1.0)
        v.memset(ones_f32[:], 1.0)
        vs.tick(v.memset(lnsc[:], 0.0))
    init_t = vs.n

    p.phase_open()
    ps_s2 = [p.psum(f"ps_s{i}", [128, 1024]) for i in range(2)]
    ps_ctx2 = [p.psum(f"ps_ctx{q}", [128, 512]) for q in range(2)]
    ps_l2 = [p.psum(f"ps_l{q}", [128, 512]) for q in range(2)]
    qh_sb = [p.sbuf(f"qh{i}", [128, TOK], BF16) for i in range(2)]
    kh_sb = [p.sbuf(f"kh{i}", [128, 4096], BF16) for i in range(2)]
    vh_sb = [p.sbuf(f"vh{i}", [128, 32, 128], BF16) for i in range(2)]
    q2 = p.sbuf("q2", [128, TOK], BF16)
    k2 = p.sbuf("k2", [128, 4096], BF16)
    pT2 = [p.sbuf(f"pT{i}", [128, 1024], BF16) for i in range(2)]
    pl_pair = [p.sbuf(f"plp{i}", [128, 1024], BF16) for i in range(2)]
    ps_misc = ps_s2[0]

    b = dict(ps_s2=ps_s2, ps_ctx2=ps_ctx2, ps_l2=ps_l2, pT2=pT2,
             pl_pair=pl_pair,
             ones_col=ones_col, ones_f32=ones_f32, negC_rep=negC_rep,
             lv=lv, linv=linv, r_rep=r_rep, ones_bf=ones_bf,
             state=dict(last_exp=0, last_av=0, last_ctxev=0, last_lev=0,
                        last_rrep=0, last_pair=0, last_lmm=0))

    head_last_ts = [0] * 34
    head_last_ss = [0] * 34
    head_last_vs = [0] * 34
    ld_t = {}

    @blk.gpsimd
    def _(g):
        ld_t[("q", 0)] = qsl.issue(g, 0, qh_sb[0][:],
                                   qT_h[0])
        ld_t[("k", 0)] = ksl.issue(g, 0, kh_sb[0][:], kT_h[0])
        ld_t[("v", 0)] = vsl.issue(g, 0, vh_sb[0][:],
                                   vf_h[:, 0:128].rearrange(
                                       "(tt pp) d -> pp tt d", pp=128))

    for h in range(32):
        if h + 1 < 32:
            @blk.gpsimd
            def _(g, h=h):
                hp = h + 1
                pre = [(ts, head_last_ts[hp - 2])] if hp >= 2 else []
                ld_t[("q", hp)] = qsl.issue(g, hp, qh_sb[hp % 2][:], qT_h[hp],
                                            pre_deps=pre)
                ld_t[("k", hp)] = ksl.issue(g, hp, kh_sb[hp % 2][:], kT_h[hp],
                                            pre_deps=pre)
                ld_t[("v", hp)] = vsl.issue(
                    g, hp, vh_sb[hp % 2][:],
                    vf_h[:, hp * 128:(hp + 1) * 128].rearrange(
                        "(tt pp) d -> pp tt d", pp=128), pre_deps=pre)

        war = ([(ts, head_last_ts[h - 1]), (ss, head_last_ss[h - 1]),
                (vs, head_last_vs[h - 1])]
               if h >= 1 else [(vs, init_t)])
        kmax_t = emit_cbound_kmax(p, kh_sb[h % 2][:], k2, cb, ones_col,
                                  ps_misc, 4096, [ld_t[("k", h)]], war)
        negC_t, _mm = emit_cbound_q(p, qh_sb[h % 2][:], q2, cb, ones_col,
                                    ps_misc, lnsc, ones_f32, negC_rep,
                                    [ld_t[("q", h)]], war, kmax_t)
        emit_head(p, b, qh_sb[h % 2][:], kh_sb[h % 2][:],
                  vh_sb[h % 2][:].rearrange("p tt d -> p (tt d)"),
                  32, 1.0, negC_t,
                  [ctxT[:, h * TOK + qc * 512: h * TOK + (qc + 1) * 512]
                   for qc in range(2)],
                  [ld_t[("q", h)], (vs, negC_t)], [ld_t[("k", h)]],
                  [ld_t[("v", h)]])
        head_last_ts[h] = ts.n
        head_last_ss[h] = ss.n
        head_last_vs[h] = vs.n
    p.phase_close()

    p.phase_open()
    ps_o = [p.psum(f"ps_o{i}", [128, 512]) for i in range(4)]
    x_grp = p.sbuf("x_grp", [128, 4 * D], F32)
    wo_sb = [p.sbuf(f"wo_sb{i}", [128, 8, 512], BF16) for i in range(2)]
    b2 = dict(x_grp=x_grp, wo_sb=wo_sb, ps_o=ps_o, ones_bf=ones_bf,
              wo_ld=[None] * 64)
    phase_deps = [(ts, ts.n), (vs, vs.n), (ss, ss.n), (ldw, w_ld)]

    def ctx_chunk_fn(g, deps):
        def get_lhsT(c, tl):
            j = g * 4 + tl
            return ctxT[:, c * TOK + j * 128: c * TOK + (j + 1) * 128]
        return get_lhsT, [(vs, b["state"]["last_ctxev"])]

    ysl_st = emit_outproj(p, b2, wo_h, bo_sb, ctx_chunk_fn, x_h, y_h, True,
                          phase_deps)
    p.phase_close()
    if ctx is None:
        p.phase_close()
        return p.finish()
    ctx["stores"] = ysl_st
    return


WQ_SS = 256
NSTRIP_A = D // WQ_SS


def build_attn1(ctx=None):
    if ctx is None:
        p = Prog()
        nc = p.nc
        x_h = nc.dram_tensor("x", [TOK, D], F32, kind="ExternalInput").ap()
        wq_h = nc.dram_tensor("wq", [128, 32, DC, 128], F32,
                               kind="ExternalInput").ap()
        wo_h = nc.dram_tensor("wo", [128, 4, 8, 8, 512], F32,
                               kind="ExternalInput").ap()
        ktg_h = nc.dram_tensor("ktg", [8, 128, 8192], F32,
                               kind="ExternalInput").ap()
        vtg_h = nc.dram_tensor("vtg", [8, 8192, 128], F32,
                               kind="ExternalInput").ap()
        cos_h = nc.dram_tensor("cos", [64, TOK], F32, kind="ExternalInput").ap()
        sin_h = nc.dram_tensor("sin", [64, TOK], F32, kind="ExternalInput").ap()
        y_h = nc.dram_tensor("y", [TOK, D], F32, kind="ExternalOutput").ap()
        qT_int = nc.dram_tensor("qT_int", [32, 128, TOK], BF16).ap()
        ctx_int = nc.dram_tensor("ctx_int", [32, 128, TOK], BF16).ap()
        standalone = True
    else:
        p = ctx["p"]
        nc = p.nc
        x_h, y_h = ctx["x"], ctx["y"]
        wq_h, wo_h = ctx["wq"], ctx["wo"]
        ktg_h, vtg_h = ctx["ktg"], ctx["vtg"]
        cos_h, sin_h = ctx["cos"], ctx["sin"]
        qT_int, ctx_int = ctx["qT_int"], ctx["ctx_int"]
        standalone = False

    ones_bf = p.sbuf("ones_bf", [1, 128], BF16)
    ones_col = p.sbuf("ones_col", [128, 1], BF16)
    ones_f32 = p.sbuf("ones_f32", [1, 128], F32)
    lnsc = p.sbuf("lnsc", [1, 1], F32)
    cb = p.sbuf("cb", [1, 16], F32)
    negC_rep = p.sbuf("negC_rep", [128, 1], F32)
    lv = p.sbuf("lv", [1, 512], F32)
    linv = p.sbuf("linv", [1, 512], F32)
    r_rep = p.sbuf("r_rep", [128, 512], F32)
    stats = p.sbuf("stats", [128, 4], F32)
    eps_ln = p.sbuf("eps_ln", [128, 1], F32)
    cos_sb = p.sbuf("cos_sb", [64, TOK], F32)
    sin_sb = p.sbuf("sin_sb", [64, TOK], F32)

    blk = p.block if not standalone else p.start_block()
    if standalone:
        emit_act_table_pin(p)
        p.phase_open()
    ldw = p.sem("ldw")
    vs, ss, ts = p.sem("vs"), p.sem("ss"), p.sem("ts")
    tr = p.sem("tr")

    @blk.gpsimd
    def _(g):
        ldw.tick(g.dma_start(out=cos_sb[:], in_=cos_h), 16)
        ldw.tick(g.dma_start(out=sin_sb[:], in_=sin_h), 16)
    w_ld = ldw.n

    @blk.vector
    def _(v):
        v.memset(ones_bf[:], 1.0)
        v.memset(ones_col[:], 1.0)
        v.memset(ones_f32[:], 1.0)
        v.memset(eps_ln[:], LN_EPS)
        vs.tick(v.memset(lnsc[:], float(np.log(INV_SQRT_HD))))
    init_t = vs.n

    p.phase_open()
    xr = [p.sbuf(f"xr{i}", [128, D], F32) for i in range(2)]
    lnx = [p.sbuf(f"lnx{i}", [128, D], BF16) for i in range(2)]
    lnxT = p.sbuf("lnxT_full", [128, DC, TOK], BF16)
    sq = p.sbuf("sq", [128, D], BF16)
    wcol = [p.sbuf(f"wcol{i}", [128, DC, 128], BF16) for i in range(2)]
    tmp1 = p.sbuf("tmp1", [64, TOK], F32)
    tmp2 = p.sbuf("tmp2", [64, TOK], F32)
    qstage = [p.sbuf(f"qstage{i}", [128, TOK], BF16) for i in range(2)]
    ps_fm = [p.psum(f"ps_fm{i}", [128, 1024]) for i in range(2)]

    bufs = dict(stats=stats, sq=sq, eps_ln=eps_ln)
    xsl = DmaSlots(p, "xld", 2)
    wsl = DmaSlots(p, "wld", 2)
    osl = DmaSlots(p, "osl", 2)

    x_ld = [None] * NT
    apply_t = [0] * NT
    trL = [0] * NT

    @blk.gpsimd
    def _(g):
        x_ld[0] = xsl.issue(g, 0, xr[0][:], x_h[0:128, :])
        x_ld[1] = xsl.issue(g, 1, xr[1][:], x_h[128:256, :])

    for j in range(NT):
        v_deps = [x_ld[j], (vs, init_t)]
        s_deps = [x_ld[j]]
        if j >= 1:
            v_deps.append((vs, apply_t[j - 1]))
            s_deps.append((vs, apply_t[j - 1]))
        apply_deps = []
        if j >= 2:
            apply_deps.append((tr, trL[j - 2]))
        apply_t[j] = emit_norm_tile(p, xr[j % 2][:], lnx[j % 2][:], bufs,
                                    v_deps, s_deps, apply_deps)
        trL[j] = emit_xbar(p, lnxT[:, :, j * 128:(j + 1) * 128],
                           lnx[j % 2][:], [(vs, apply_t[j])])
        if j + 2 < NT:
            @blk.gpsimd
            def _(g, j=j):
                x_ld[j + 2] = xsl.issue(g, j + 2, xr[j % 2][:],
                                        x_h[(j + 2) * 128:(j + 3) * 128, :],
                                        pre_deps=[(vs, apply_t[j])])

    w_ld_t = [None] * 32
    mm_t = [0] * 32
    rope_t = [0] * 32
    qst = [None, None]

    for oc in range(32):
        @blk.gpsimd
        def _(g, oc=oc):
            w_ld_t[oc] = wsl.issue(
                g, oc, wcol[oc % 2][:], wq_h[:, oc],
                pre_deps=[(ts, mm_t[oc - 2] if oc >= 2 else 0)])

        @blk.tensor
        def _(t, oc=oc):
            waits(t, [w_ld_t[oc], (tr, trL[NT - 1]), (ldw, w_ld)])
            if oc >= 2:
                t.wait_ge(vs.h, rope_t[oc - 2])
            pb = ps_fm[oc % 2]
            mm = None
            for c in range(DC):
                t.matmul(pb[:, 0:512], wcol[oc % 2][:, c, :],
                         lnxT[:, c, 0:512], start=(c == 0), stop=(c == DC - 1))
                mm = t.matmul(pb[:, 512:1024], wcol[oc % 2][:, c, :],
                              lnxT[:, c, 512:1024], start=(c == 0),
                              stop=(c == DC - 1))
            ts.tick(mm)
        mm_t[oc] = ts.n

        @blk.vector
        def _(v, oc=oc):
            v.wait_ge(ts.h, mm_t[oc])
            if oc >= 1:
                v.wait_ge(vs.h, rope_t[oc - 1])
            if qst[oc % 2] is not None:
                waits(v, [qst[oc % 2]])
            pb = ps_fm[oc % 2]
            vs.tick(v.tensor_tensor(tmp1[:], pb[0:64, :], cos_sb[:], op=ALU.mult))
            vs.tick(v.tensor_tensor(tmp2[:], pb[64:128, :], sin_sb[:], op=ALU.mult))
            v.wait_ge(vs.h, vs.n)
            vs.tick(v.tensor_tensor(qstage[oc % 2][0:64, :], tmp1[:], tmp2[:],
                                    op=ALU.subtract))
            v.wait_ge(vs.h, vs.n)
            vs.tick(v.tensor_tensor(tmp1[:], pb[64:128, :], cos_sb[:], op=ALU.mult))
            vs.tick(v.tensor_tensor(tmp2[:], pb[0:64, :], sin_sb[:], op=ALU.mult))
            v.wait_ge(vs.h, vs.n)
            vs.tick(v.tensor_tensor(qstage[oc % 2][64:128, :], tmp1[:], tmp2[:],
                                    op=ALU.add))
        rope_t[oc] = vs.n

        @blk.scalar
        def _(s_, oc=oc):
            qst[oc % 2] = osl.issue(s_, oc, qT_int[oc], qstage[oc % 2][:],
                                    pre_deps=[(vs, rope_t[oc])])

    phaseA_end = [(ts, ts.n), (vs, vs.n), (ss, ss.n), (tr, tr.n)] + \
        [(s_, s_.n) for s_ in osl.sems if s_.n]
    p.phase_close()

    p.phase_open()
    ps_s2 = [p.psum(f"ps_s{i}", [128, 1024]) for i in range(2)]
    ps_ctx2 = [p.psum(f"ps_ctx{q}", [128, 512]) for q in range(2)]
    ps_l2 = [p.psum(f"ps_l{q}", [128, 512]) for q in range(2)]
    qh_sb = [p.sbuf(f"qh{i}", [128, TOK], BF16) for i in range(2)]
    kh_sb = [p.sbuf(f"kh{i}", [128, 8192], BF16) for i in range(2)]
    vh_sb = [p.sbuf(f"vh{i}", [128, 64, 128], BF16) for i in range(2)]
    q2 = p.sbuf("q2", [128, TOK], BF16)
    k2 = p.sbuf("k2", [128, 8192], BF16)
    pT2 = [p.sbuf(f"pT{i}", [128, 1024], BF16) for i in range(2)]
    pl_pair = [p.sbuf(f"plp{i}", [128, 1024], BF16) for i in range(2)]
    ctx_ev = [p.sbuf(f"ctx_ev{i}", [128, 512], BF16) for i in range(2)]
    ps_misc = ps_s2[0]

    b = dict(ps_s2=ps_s2, ps_ctx2=ps_ctx2, ps_l2=ps_l2, pT2=pT2,
             pl_pair=pl_pair,
             ones_col=ones_col, ones_f32=ones_f32, negC_rep=negC_rep,
             lv=lv, linv=linv, r_rep=r_rep, ones_bf=ones_bf,
             state=dict(last_exp=0, last_av=0, last_ctxev=0, last_lev=0,
                        last_rrep=0, last_pair=0, last_lmm=0))

    qsl = DmaSlots(p, "qsl", 2)
    ksl = DmaSlots(p, "ksl", 2)
    vsl = DmaSlots(p, "vsl", 2)
    csl = DmaSlots(p, "cst", 2)
    kv_last_ts = [0] * 10
    qh_last = {}
    ld_t = {}
    ctx_st = {}

    @blk.gpsimd
    def _(g):
        ld_t[("k", 0)] = ksl.issue(g, 0, kh_sb[0][:], ktg_h[0],
                                   pre_deps=phaseA_end)
        ld_t[("v", 0)] = vsl.issue(g, 0, vh_sb[0][:],
                                   vtg_h[0].rearrange("(tt pp) d -> pp tt d",
                                                      pp=128),
                                   pre_deps=phaseA_end)
        ld_t[("q", 0)] = qsl.issue(g, 0, qh_sb[0][:], qT_int[0],
                                   pre_deps=phaseA_end)

    hh_ss = [0] * 34
    hh_ts = [0] * 34
    hh_vs = [0] * 34
    for kv in range(8):
        if kv + 1 < 8:
            @blk.gpsimd
            def _(g, kv=kv):
                kp = kv + 1
                pre = [(ts, kv_last_ts[kp - 2])] if kp >= 2 else []
                ld_t[("k", kp)] = ksl.issue(g, kp, kh_sb[kp % 2][:], ktg_h[kp],
                                            pre_deps=pre)
                ld_t[("v", kp)] = vsl.issue(
                    g, kp, vh_sb[kp % 2][:],
                    vtg_h[kp].rearrange("(tt pp) d -> pp tt d", pp=128),
                    pre_deps=pre)

        war0 = ([(ts, hh_ts[kv * 4 - 1]), (ss, hh_ss[kv * 4 - 1]),
                 (vs, hh_vs[kv * 4 - 1])]
                if kv >= 1 else [(vs, init_t)])
        kmax_t = emit_cbound_kmax(p, kh_sb[kv % 2][:], k2, cb, ones_col,
                                  ps_misc, 8192, [ld_t[("k", kv)]], war0)
        for qi in range(4):
            h = kv * 4 + qi
            if h + 1 < 32:
                @blk.gpsimd
                def _(g, h=h):
                    hp = h + 1
                    pre = [(ts, hh_ts[hp - 2])] if hp >= 2 else []
                    ld_t[("q", hp)] = qsl.issue(g, hp, qh_sb[hp % 2][:],
                                                qT_int[hp], pre_deps=pre)

            war = ([(ts, hh_ts[h - 1]), (ss, hh_ss[h - 1]),
                    (vs, hh_vs[h - 1])]
                   if h >= 1 else [(vs, init_t)])
            negC_t, _mm = emit_cbound_q(p, qh_sb[h % 2][:], q2, cb, ones_col,
                                        ps_misc, lnsc, ones_f32, negC_rep,
                                        [ld_t[("q", h)]], war, kmax_t)
            prev_sts = [ctx_st.get((h - 1, qc)) for qc in range(2)]
            r = emit_head(p, b, qh_sb[h % 2][:], kh_sb[kv % 2][:],
                          vh_sb[kv % 2][:].rearrange("p tt d -> p (tt d)"),
                          64, INV_SQRT_HD, negC_t,
                          [ctx_ev[0][:], ctx_ev[1][:]],
                          [ld_t[("q", h)], (vs, negC_t)], [ld_t[("k", kv)]],
                          [ld_t[("v", kv)]],
                          ctx_war=[d for d in prev_sts if d])

            @blk.scalar
            def _(s_, h=h, r=r):
                for qc in range(2):
                    ctx_st[(h, qc)] = csl.issue(
                        s_, qc, ctx_int[h, :, qc * 512:(qc + 1) * 512],
                        ctx_ev[qc][:],
                        pre_deps=[(vs, r["ctxev"][qc])])
            hh_ts[h] = ts.n
            hh_ss[h] = ss.n
            hh_vs[h] = vs.n
        kv_last_ts[kv] = ts.n
    ctx_stores = [(s_, s_.n) for s_ in csl.sems if s_.n]
    phaseB_end = [(ts, ts.n), (vs, vs.n), (ss, ss.n)] + ctx_stores
    p.phase_close()

    p.phase_open()
    ps_o = [p.psum(f"ps_o{i}", [128, 512]) for i in range(4)]
    x_grp = p.sbuf("x_grp", [128, 4 * D], F32)
    wo_sb = [p.sbuf(f"wo_sb{i}", [128, 8, 512], BF16) for i in range(2)]
    ctx_grp = [p.sbuf(f"ctx_grp{i}", [128, 32, 512], BF16) for i in range(2)]
    b2 = dict(x_grp=x_grp, wo_sb=wo_sb, ps_o=ps_o, ones_bf=ones_bf,
              wo_ld=[None] * 64)
    cgsl = DmaSlots(p, "cgsl", 2)
    cg_ld = {}

    def ctx_chunk_fn(g, deps):
        @blk.gpsimd
        def _(gq, g=g):
            cg_ld[g] = cgsl.issue(
                gq, g, ctx_grp[g][:],
                ctx_int[:, :, g * 512:(g + 1) * 512].rearrange("h p r -> p h r"),
                pre_deps=list(phaseB_end))

        def get_lhsT(c, tl):
            return ctx_grp[g][:, c, tl * 128:(tl + 1) * 128]
        return get_lhsT, [cg_ld[g]]

    ysl_st = emit_outproj(p, b2, wo_h, None, ctx_chunk_fn, x_h, y_h, False,
                          phaseB_end)
    p.phase_close()
    if standalone:
        p.phase_close()
        return p.finish()
    ctx["stores"] = ysl_st
    return


ACT_COMBINED_SET = 6


def emit_act_table_pin(p):
    blk = p.block

    @blk.scalar
    def _(s):
        s.add_instruction(mybir.InstLoadActFuncSet(
            name=p.nc.get_next_instruction_name(),
            act_func_set_id=ACT_COMBINED_SET, ins=[], outs=[]))


def emit_stage_barrier(p):
    blk = p.block
    deps = p.stage_pre()

    for section in (blk.tensor, blk.vector, blk.scalar, blk.sync, blk.gpsimd):
        @section
        def _(eng):
            waits(eng, deps)
    return deps


def build_fused():
    p = Prog()
    nc = p.nc
    xb_h = nc.dram_tensor("xb", [4096, D], F32, kind="ExternalInput").ap()
    atwr_h = nc.dram_tensor("atwr", [D, 136], F32, kind="ExternalInput").ap()
    brow_h = nc.dram_tensor("brow", [1, 136], F32, kind="ExternalInput").ap()
    bflat_h = nc.dram_tensor("bflat", [128, D], F32, kind="ExternalInput").ap()
    wqkv_h = nc.dram_tensor("wqkv", [128, 96, DC + 1, 128], F32,
                            kind="ExternalInput").ap()
    gq_h = nc.dram_tensor("gq", [128, 1], F32, kind="ExternalInput").ap()
    gk_h = nc.dram_tensor("gk", [128, 1], F32, kind="ExternalInput").ap()
    wo0_h = nc.dram_tensor("wo0", [128, 4, 8, 8, 512], F32,
                           kind="ExternalInput").ap()
    bo0_h = nc.dram_tensor("bo0", [1, D], F32, kind="ExternalInput").ap()
    wallT_h = nc.dram_tensor("wallT", [D, 24], F32, kind="ExternalInput").ap()
    bases_h = nc.dram_tensor("bases", [1, 24], F32, kind="ExternalInput").ap()
    wb_h = nc.dram_tensor("wb", [1024, 1024], F32, kind="ExternalInput").ap()
    gate_h = nc.dram_tensor("gate", [1, 1], F32, kind="ExternalInput").ap()
    wq1_h = nc.dram_tensor("wq1", [128, 32, DC, 128], F32,
                           kind="ExternalInput").ap()
    wo1_h = nc.dram_tensor("wo1", [128, 4, 8, 8, 512], F32,
                           kind="ExternalInput").ap()
    ktg_h = nc.dram_tensor("ktg", [8, 128, 8192], F32, kind="ExternalInput").ap()
    vtg_h = nc.dram_tensor("vtg", [8, 8192, 128], F32, kind="ExternalInput").ap()
    cos_h = nc.dram_tensor("cos", [64, TOK], F32, kind="ExternalInput").ap()
    sin_h = nc.dram_tensor("sin", [64, TOK], F32, kind="ExternalInput").ap()
    y_h = nc.dram_tensor("y", [TOK, D], F32, kind="ExternalOutput").ap()

    x1b = nc.dram_tensor("x1b", [4096, D], F32).ap()
    xB = nc.dram_tensor("xB", [TOK, D], F32).ap()
    xC = nc.dram_tensor("xC", [TOK, D], F32).ap()
    qT0_int = nc.dram_tensor("qT0_int", [32, 128, TOK], BF16).ap()
    kTf_int = nc.dram_tensor("kTf_int", [32, 128, 4096], BF16).ap()
    vf_int = nc.dram_tensor("vf_int", [4096, D], BF16).ap()
    qT1_int = nc.dram_tensor("qT1_int", [32, 128, TOK], BF16).ap()
    ctx1_int = nc.dram_tensor("ctx1_int", [32, 128, TOK], BF16).ap()

    blk = p.start_block()
    emit_act_table_pin(p)

    def stage(fn, **aps):
        emit_stage_barrier(p)
        p.phase_open()
        c = dict(p=p, **aps)
        fn(c)
        p.phase_close()
        return c

    stage(build_ffn0, x=xb_h, y=x1b, atwr=atwr_h, brow=brow_h, bflat=bflat_h,
          ntile=32)
    stage(build_qkv0, x=x1b, wqkv=wqkv_h, gq=gq_h, gk=gk_h,
          qT=qT0_int, kT=kTf_int, v=vf_int, ngroups=4)
    stage(build_attn0, x=x1b[0:TOK, :], qT=qT0_int, kT3=kTf_int, vf=vf_int,
          wo=wo0_h, bo=bo0_h, y=xB)
    stage(build_ffn0, x=xB, y=xC, atwr=atwr_h, brow=brow_h, bflat=bflat_h)

    stage(build_ffn1, x=xC, y=xB, wallT=wallT_h, bases=bases_h, wb=wb_h,
          gate=gate_h)
    stage(build_attn1, x=xB, y=xC, wq=wq1_h, wo=wo1_h, ktg=ktg_h,
          vtg=vtg_h, cos=cos_h, sin=sin_h, qT_int=qT1_int, ctx_int=ctx1_int)
    c6 = stage(build_ffn1, x=xC, y=y_h, wallT=wallT_h, bases=bases_h,
               wb=wb_h, gate=gate_h)

    @blk.gpsimd
    def _(g):
        waits(g, c6["stores"])

    return p.finish()


def kernel_fused(inputs):
    nc = _get("fused", build_fused)
    x = np.ascontiguousarray(
        inputs["L0_attn_hidden_states"].reshape(2 * 4096, D).astype(np.float32))
    xbat = []
    for c in range(NCORES):
        bb = c // 4
        rows = np.concatenate(
            [x[c * TOK:(c + 1) * TOK]] +
            [x[cc * TOK:(cc + 1) * TOK] for cc in range(bb * 4, bb * 4 + 4)
             if cc != c], axis=0)
        xbat.append(np.ascontiguousarray(rows))
    A, B = inputs["L0_ffn_A"], inputs["L0_ffn_B"]
    AT = np.ascontiguousarray(A.reshape(128, D).T)
    WrT = np.ascontiguousarray(inputs["L0_ffn_Wr"].T)
    atwr = np.ascontiguousarray(np.concatenate([AT, WrT], axis=1))
    brow = np.ascontiguousarray(np.concatenate(
        [np.zeros(128, np.float32), inputs["L0_ffn_br"]]).reshape(1, 136))
    bflat = np.ascontiguousarray(B.transpose(0, 2, 1).reshape(128, D))
    wqkv = np.concatenate([inputs["L0_attn_Wq"], inputs["L0_attn_Wk"],
                           inputs["L0_attn_Wv"]], axis=1)
    bpad = np.zeros((128, 12288), np.float32)
    bpad[0] = np.concatenate([inputs["L0_attn_bq"], inputs["L0_attn_bk"],
                              inputs["L0_attn_bv"]])
    wqkv = _wcols(np.concatenate([wqkv, bpad], axis=0), 128)
    wall = np.concatenate([inputs["L1_ffn_W_res"], inputs["L1_ffn_W_pre"],
                           inputs["L1_ffn_W_post"]], axis=0)
    wallT = np.ascontiguousarray(wall.T)
    bases = np.ascontiguousarray(np.concatenate(
        [inputs["L1_ffn_H_res_base"].reshape(16), inputs["L1_ffn_H_pre_base"],
         inputs["L1_ffn_H_post_base"]]).reshape(1, 24))
    blocks = inputs["L1_attn_kv_cache"][inputs["L1_attn_block_table"]]
    Kg = blocks[:, 0].transpose(1, 0, 2, 3).reshape(8, 8192, 128)
    Vg = blocks[:, 1].transpose(1, 0, 2, 3).reshape(8, 8192, 128)
    ktg = np.ascontiguousarray(Kg.transpose(0, 2, 1))
    vtg = np.ascontiguousarray(Vg)
    inv_freq = (1.0 / 10000.0 ** (np.arange(0, 128, 2, dtype=np.float32) / 128.0)
                ).astype(np.float32)
    ang = (np.arange(4096, dtype=np.float32)[:, None] * inv_freq[None, :])
    cos_full = np.cos(ang).astype(np.float32)
    sin_full = np.sin(ang).astype(np.float32)
    common = {
        "atwr": atwr, "brow": brow, "bflat": bflat, "wqkv": wqkv,
        "gq": np.ascontiguousarray(inputs["L0_attn_gq"].reshape(128, 1)),
        "gk": np.ascontiguousarray(inputs["L0_attn_gk"].reshape(128, 1)),
        "wo0": np.ascontiguousarray(inputs["L0_attn_Wo"].reshape(
            4, 8, 128, 8, 512).transpose(2, 0, 3, 1, 4)),
        "bo0": np.ascontiguousarray(inputs["L0_attn_bo"].reshape(1, D)),
        "wallT": wallT, "bases": bases,
        "wb": np.ascontiguousarray(inputs["L1_ffn_W_branch"]),
        "gate": np.ascontiguousarray(inputs["L1_ffn_gate"].reshape(1, 1)),
        "wq1": _wcols(inputs["L1_attn_Wq"], 128),
        "wo1": np.ascontiguousarray(inputs["L1_attn_Wo"].reshape(
            4, 8, 128, 8, 512).transpose(2, 0, 3, 1, 4)),
        "ktg": ktg, "vtg": vtg,
    }
    in_maps = []
    for c in range(NCORES):
        n0 = (c % 4) * TOK
        m = dict(common)
        m["xb"] = xbat[c]
        m["cos"] = np.ascontiguousarray(cos_full[n0:n0 + TOK].T)
        m["sin"] = np.ascontiguousarray(sin_full[n0:n0 + TOK].T)
        in_maps.append(m)
    res = _spmd(nc, in_maps)
    y = np.concatenate([r["y"] for r in res.results], axis=0)
    return y.reshape(2, 4096, D)



_NC_CACHE = {}


def _get(name, builder):
    if name not in _NC_CACHE:
        _NC_CACHE[name] = builder()
    return _NC_CACHE[name]


def _wcols(W, blk):
    R, C = W.shape
    return np.ascontiguousarray(
        W.reshape(R // 128, 128, C // blk, blk).transpose(1, 2, 0, 3))


def _spmd(nc, in_maps):
    return run_bass_kernel_spmd(nc, in_maps, core_ids=list(range(NCORES)))


def run_ffn0(x, A, B, Wr, br):
    nc = _get("ffn0", build_ffn0)
    AT = np.ascontiguousarray(A.reshape(128, D).T)
    WrT = np.ascontiguousarray(Wr.T)
    atwr = np.ascontiguousarray(np.concatenate([AT, WrT], axis=1))
    brow = np.ascontiguousarray(
        np.concatenate([np.zeros(128, np.float32), br]).reshape(1, 136))
    bflat = np.ascontiguousarray(B.transpose(0, 2, 1).reshape(128, D))
    in_maps = [{"x": np.ascontiguousarray(x[c * TOK:(c + 1) * TOK]),
                "atwr": atwr, "brow": brow, "bflat": bflat}
               for c in range(NCORES)]
    res = _spmd(nc, in_maps)
    return np.concatenate([r["y"] for r in res.results], axis=0)


def run_ffn1(x, W_res, W_pre, W_post, Hrb, Hpb, Hqb, gate, Wb):
    nc = _get("ffn1", build_ffn1)
    wall = np.concatenate([W_res, W_pre, W_post], axis=0)
    wallT = np.ascontiguousarray(wall.T)
    bases = np.ascontiguousarray(
        np.concatenate([Hrb.reshape(16), Hpb, Hqb]).reshape(1, 24))
    gate2 = np.ascontiguousarray(gate.reshape(1, 1))
    wb = np.ascontiguousarray(Wb)
    in_maps = [{"x": np.ascontiguousarray(x[c * TOK:(c + 1) * TOK]),
                "wallT": wallT, "bases": bases, "wb": wb, "gate": gate2}
               for c in range(NCORES)]
    res = _spmd(nc, in_maps)
    return np.concatenate([r["y"] for r in res.results], axis=0)


def run_qkv0(x, Wq, bq, Wk, bk, Wv, bv, gq, gk):
    nc = _get("qkv0", build_qkv0)
    wqkv = np.concatenate([Wq, Wk, Wv], axis=1)
    bpad = np.zeros((128, 12288), np.float32)
    bpad[0] = np.concatenate([bq, bk, bv])
    wqkv = _wcols(np.concatenate([wqkv, bpad], axis=0), 128)
    gq2 = np.ascontiguousarray(gq.reshape(128, 1))
    gk2 = np.ascontiguousarray(gk.reshape(128, 1))
    in_maps = [{"x": np.ascontiguousarray(x[c * TOK:(c + 1) * TOK]),
                "wqkv": wqkv, "gq": gq2, "gk": gk2}
               for c in range(NCORES)]
    res = _spmd(nc, in_maps)
    qT = [r["qT"] for r in res.results]
    kT = [r["kT"] for r in res.results]
    v = [r["v"] for r in res.results]
    return qT, kT, v


def run_attn0(x, qT, kT, v, Wo, bo):
    nc = _get("attn0", build_attn0)
    kT_full = [np.ascontiguousarray(np.concatenate(kT[b * 4:(b + 1) * 4], axis=2))
               for b in range(2)]
    v_full = [np.ascontiguousarray(np.concatenate(v[b * 4:(b + 1) * 4], axis=0))
              for b in range(2)]
    wo = np.ascontiguousarray(Wo.reshape(4, 8, 128, 8, 512).transpose(2, 0, 3, 1, 4))
    bo2 = np.ascontiguousarray(bo.reshape(1, D))
    in_maps = [{"x": np.ascontiguousarray(x[c * TOK:(c + 1) * TOK]),
                "qT": qT[c], "kT": kT_full[c // 4], "vf": v_full[c // 4],
                "wo": wo, "bo": bo2}
               for c in range(NCORES)]
    res = _spmd(nc, in_maps)
    return np.concatenate([r["y"] for r in res.results], axis=0)


def run_attn1(x, Wq, Wo, kv_cache, block_table):
    nc = _get("attn1", build_attn1)
    blocks = kv_cache[block_table]
    K = blocks[:, 0].transpose(1, 0, 2, 3).reshape(8, 8192, 128)
    V = blocks[:, 1].transpose(1, 0, 2, 3).reshape(8, 8192, 128)
    ktg = np.ascontiguousarray(K.transpose(0, 2, 1))
    vtg = np.ascontiguousarray(V)
    wq = _wcols(Wq, 128)
    wo = np.ascontiguousarray(Wo.reshape(4, 8, 128, 8, 512).transpose(2, 0, 3, 1, 4))
    inv_freq = (1.0 / 10000.0 ** (np.arange(0, 128, 2, dtype=np.float32) / 128.0)
                ).astype(np.float32)
    ang = (np.arange(4096, dtype=np.float32)[:, None] * inv_freq[None, :])
    cos_full = np.cos(ang).astype(np.float32)
    sin_full = np.sin(ang).astype(np.float32)
    in_maps = []
    for c in range(NCORES):
        n0 = (c % 4) * TOK
        in_maps.append({"x": np.ascontiguousarray(x[c * TOK:(c + 1) * TOK]),
                        "wq": wq, "wo": wo, "ktg": ktg, "vtg": vtg,
                        "cos": np.ascontiguousarray(cos_full[n0:n0 + TOK].T),
                        "sin": np.ascontiguousarray(sin_full[n0:n0 + TOK].T)})
    res = _spmd(nc, in_maps)
    return np.concatenate([r["y"] for r in res.results], axis=0)


def kernel_unfused(**inputs):
    x = np.ascontiguousarray(
        inputs["L0_attn_hidden_states"].reshape(2 * 4096, D).astype(np.float32))
    x = run_ffn0(x, inputs["L0_ffn_A"], inputs["L0_ffn_B"],
                 inputs["L0_ffn_Wr"], inputs["L0_ffn_br"])
    qT, kT, v = run_qkv0(x, inputs["L0_attn_Wq"], inputs["L0_attn_bq"],
                         inputs["L0_attn_Wk"], inputs["L0_attn_bk"],
                         inputs["L0_attn_Wv"], inputs["L0_attn_bv"],
                         inputs["L0_attn_gq"], inputs["L0_attn_gk"])
    x = run_attn0(x, qT, kT, v, inputs["L0_attn_Wo"], inputs["L0_attn_bo"])
    x = run_ffn0(x, inputs["L0_ffn_A"], inputs["L0_ffn_B"],
                 inputs["L0_ffn_Wr"], inputs["L0_ffn_br"])
    x = run_ffn1(x, inputs["L1_ffn_W_res"], inputs["L1_ffn_W_pre"],
                 inputs["L1_ffn_W_post"], inputs["L1_ffn_H_res_base"],
                 inputs["L1_ffn_H_pre_base"], inputs["L1_ffn_H_post_base"],
                 inputs["L1_ffn_gate"], inputs["L1_ffn_W_branch"])
    x = run_attn1(x, inputs["L1_attn_Wq"], inputs["L1_attn_Wo"],
                  inputs["L1_attn_kv_cache"], inputs["L1_attn_block_table"])
    x = run_ffn1(x, inputs["L1_ffn_W_res"], inputs["L1_ffn_W_pre"],
                 inputs["L1_ffn_W_post"], inputs["L1_ffn_H_res_base"],
                 inputs["L1_ffn_H_pre_base"], inputs["L1_ffn_H_post_base"],
                 inputs["L1_ffn_gate"], inputs["L1_ffn_W_branch"])
    return x.reshape(2, 4096, D)


def kernel(**inputs):
    return kernel_fused(inputs)
